# version 01
# baseline (speedup 1.0000x reference)
import math
import numpy as np
import ml_dtypes
from contextlib import ExitStack

import concourse.bass as bass
import concourse.mybir as mybir
from concourse.bass_utils import run_bass_kernel_spmd

F32 = mybir.dt.float32
BF16 = mybir.dt.bfloat16
AF = mybir.ActivationFunctionType
ALU = mybir.AluOpType

NEG = -1.0e30
E = 2048
NI = 16
H = 16
C = 512
TC = 2048
TA = 2560
NTC = 16
NTA = 20
SCALE = 1.0 / math.sqrt(128.0)
ENGS = ("sync", "tensor", "vector", "scalar", "gpsimd")


class Sem:
    def __init__(self, nc, stack, name):
        self.h = stack.enter_context(nc.semaphore(name))
        self.n = 0


class Prog:

    def __init__(self):
        self.streams = {e: [] for e in ENGS}

    def emit(self, eng, fn):
        self.streams[eng].append(fn)

    def wait(self, eng, sem, tgt):
        if tgt > 0:
            self.emit(eng, lambda e, s=sem.h, t=tgt: e.wait_ge(s, t))

    def op(self, eng, builder, sem=None, k=1):
        if sem is None:
            self.emit(eng, builder)
            return None
        sem.n += k
        self.emit(eng, lambda e, b=builder, s=sem.h, kk=k: b(e).then_inc(s, kk))
        return sem.n

    def dma(self, eng, out_ap, in_ap, sem):
        sem.n += 16
        self.emit(eng, lambda e, o=out_ap, i=in_ap, s=sem.h: e.dma_start(o, i).then_inc(s, 16))
        return sem.n

    def barrier(self, bar):
        for e in ENGS:
            self.emit(e, lambda en, s=bar.h: en.nop().then_inc(s, 1))
        bar.n += len(ENGS)
        tgt = bar.n
        for e in ENGS:
            self.emit(e, lambda en, s=bar.h, t=tgt: en.wait_ge(s, t))


def build(dbg=frozenset(), phases=5):
    nc = bass.Bass("TRN2", target_bir_lowering=False)

    def dram(name, shape, dtype):
        kind = "ExternalOutput" if name in dbg else "Internal"
        return nc.dram_tensor(name, shape, dtype, kind=kind)

    xh = nc.dram_tensor("xh", [TA, E], F32, kind="ExternalInput").ap()
    wqkvT = nc.dram_tensor("wqkvT", [E, 3 * E], F32, kind="ExternalInput").ap()
    woutT = nc.dram_tensor("woutT", [E, E], F32, kind="ExternalInput").ap()
    wffnT = nc.dram_tensor("wffnT", [E, E], F32, kind="ExternalInput").ap()
    bias1 = nc.dram_tensor("bias1", [1, E], F32, kind="ExternalInput").ap()
    identN = nc.dram_tensor("identN", [128, 128], BF16, kind="ExternalInput").ap()
    ones_in = nc.dram_tensor("ones_in", [128, 512], BF16, kind="ExternalInput").ap()
    maskPT_in = nc.dram_tensor("maskPT_in", [128, 128], BF16, kind="ExternalInput").ap()
    maskOT_in = nc.dram_tensor("maskOT_in", [128, 128], BF16, kind="ExternalInput").ap()
    m0row = nc.dram_tensor("m0row", [1, 128], BF16, kind="ExternalInput").ap()

    out = nc.dram_tensor("out", [TC, E], F32, kind="ExternalOutput").ap()

    QT = dram("QT", [H, 128, TC], BF16).ap()
    KT = dram("KT", [H, 128, TA], BF16).ap()
    VN = dram("VN", [NTA, 128, E], BF16).ap()
    OT = dram("OT", [H, 128, TC], BF16).ap()
    G = dram("G", [NTC, 128, E], BF16).ap()
    if "XNT" in dbg:
        XNTd = nc.dram_tensor("XNT", [NI, 128, TA], BF16, kind="ExternalOutput").ap()

    prog = Prog()

    with ExitStack() as outer:
        EO = outer.enter_context
        ident = EO(nc.sbuf_tensor("ident", [128, 128], BF16))
        ones_sb = EO(nc.sbuf_tensor("ones_sb", [128, 512], BF16))
        maskPT = EO(nc.sbuf_tensor("maskPT", [128, 128], BF16))
        maskOT = EO(nc.sbuf_tensor("maskOT", [128, 128], BF16))
        m0_sb = EO(nc.sbuf_tensor("m0_sb", [1, 128], BF16))
        bias_sb = EO(nc.sbuf_tensor("bias_sb", [1, E], BF16))

        SM = lambda name: Sem(nc, outer, name)
        bar = SM("bar")
        const_ld = SM("const_ld")
        CONST_TOT = 6 * 16

        prog.dma("sync", ident[:], identN, const_ld)
        prog.dma("sync", ones_sb[:], ones_in, const_ld)
        prog.dma("sync", maskPT[:], maskPT_in, const_ld)
        prog.dma("sync", maskOT[:], maskOT_in, const_ld)
        prog.dma("sync", m0_sb[:], m0row, const_ld)
        prog.dma("gpsimd", bias_sb[:], bias1, const_ld)

        s123 = ExitStack()
        xnt = s123.enter_context(nc.sbuf_tensor("xnt", [128, NI * TA], BF16))
        xnt3 = xnt[:].rearrange("p (i t) -> p i t", i=NI)

        with ExitStack() as p1:
            E1 = p1.enter_context
            xbuf = [E1(nc.sbuf_tensor(f"xbuf{p}", [128, 4 * E], F32)) for p in range(2)]
            xnat = [E1(nc.sbuf_tensor(f"xnat{p}", [128, 4 * E], BF16)) for p in range(2)]
            stats = E1(nc.sbuf_tensor("stats", [128, 24], F32))
            mv = E1(nc.sbuf_tensor("mv", [128, 8], F32))
            veps = E1(nc.sbuf_tensor("veps", [128, 4], F32))
            sqv = E1(nc.sbuf_tensor("sqv", [128, 4], F32))
            rstd = E1(nc.sbuf_tensor("rstd", [128, 4], F32))
            nmr = E1(nc.sbuf_tensor("nmr", [128, 4], F32))
            tp_ps = [E1(nc.psum_tensor(f"tp_ps{p}", [128, 512], F32)) for p in range(2)]

            x_ld = [SM("x_ld0"), SM("x_ld1")]
            v_self = SM("v_self")
            s_self = SM("s_self")
            norm_done = SM("norm_done")
            tp_done = SM("tp_done")
            tp_ev = SM("tp_ev")

            for c in range(5):
                p = c % 2
                if c >= 2:
                    prog.wait("sync", norm_done, c - 1)
                ld_tgt = prog.dma(
                    "sync", xbuf[p][:].rearrange("p (a e) -> p a e", a=4),
                    xh[512 * c:512 * (c + 1), :].rearrange("(a p) e -> p a e", p=128),
                    x_ld[p])
                prog.wait("vector", x_ld[p], ld_tgt)
                for a in range(4):
                    for g4 in range(4):
                        t = prog.op("vector", lambda e, a=a, g=g4, p=p: e.bn_stats(
                            stats[:, 6 * g:6 * (g + 1)],
                            xbuf[p][:, 2048 * a + 512 * g: 2048 * a + 512 * (g + 1)]), v_self)
                    prog.wait("vector", v_self, t)
                    t = prog.op("vector", lambda e, a=a: e.bn_aggr(
                        mv[:, 2 * a:2 * a + 2], stats[:]), v_self)
                    prog.wait("vector", v_self, t)
                t = prog.op("vector", lambda e: e.tensor_scalar(
                    veps[:], mv[:, 1:8:2], 1e-5, None, ALU.add), v_self)
                prog.wait("scalar", v_self, t)
                t = prog.op("scalar", lambda e: e.activation(sqv[:], veps[:], AF.Sqrt), s_self)
                prog.wait("vector", s_self, t)
                t = prog.op("vector", lambda e: e.reciprocal(rstd[:], sqv[:]), v_self)
                prog.wait("vector", v_self, t)
                t = prog.op("vector", lambda e: e.scalar_tensor_tensor(
                    nmr[:], mv[:, 0:8:2], -1.0, rstd[:], ALU.mult, ALU.mult), v_self)
                prog.wait("vector", v_self, t)
                if c >= 2:
                    prog.wait("vector", tp_done, 16 * (c - 1))
                for a in range(4):
                    sem = norm_done if a == 3 else v_self
                    prog.op("vector", lambda e, a=a, p=p: e.tensor_scalar(
                        xnat[p][:, 2048 * a:2048 * (a + 1)],
                        xbuf[p][:, 2048 * a:2048 * (a + 1)],
                        rstd[:, a:a + 1], nmr[:, a:a + 1], ALU.mult, ALU.add), sem)
                prog.wait("tensor", const_ld, CONST_TOT)
                prog.wait("tensor", norm_done, c + 1)
                for a in range(4):
                    for ig in range(4):
                        gpar = tp_done.n % 2
                        if tp_done.n >= 2:
                            prog.wait("tensor", tp_ev, tp_done.n - 1)
                        for kk in range(4):
                            i = 4 * ig + kk
                            prog.op("tensor", lambda e, a=a, i=i, kk=kk, p=p, gp=gpar: e.matmul(
                                tp_ps[gp][:, 128 * kk:128 * (kk + 1)],
                                xnat[p][:, 2048 * a + 128 * i: 2048 * a + 128 * (i + 1)],
                                ident[:], start=True, stop=True),
                                tp_done if kk == 3 else None)
                        tgt = tp_done.n
                        prog.wait("vector", tp_done, tgt)
                        dst = xnt3[:, 4 * ig:4 * ig + 4,
                                   512 * c + 128 * a: 512 * c + 128 * (a + 1)]
                        prog.op("vector", lambda e, d=dst, gp=gpar: e.tensor_copy(
                            d, tp_ps[gp][:].rearrange("p (i t) -> p i t", i=4)), tp_ev)
            if "XNT" in dbg:
                prog.wait("sync", tp_ev, tp_ev.n)
                dmp = SM("dmp")
                for hf in range(2):
                    prog.dma("sync",
                             XNTd[8 * hf:8 * (hf + 1)].rearrange("i p t -> p i t"),
                             xnt3[:, 8 * hf:8 * (hf + 1), :], dmp)
                prog.wait("sync", dmp, dmp.n)

        prog.barrier(bar)
        if phases < 2:
            s123.close()
            finish(prog, nc)
            return nc

        with ExitStack() as p2:
            E2 = p2.enter_context
            wffn = E2(nc.sbuf_tensor("wffn", [128, NI * E], BF16))
            awt = [E2(nc.sbuf_tensor(f"awt{p}", [128, 512], BF16)) for p in range(2)]
            arow = E2(nc.sbuf_tensor("arow", [1, E], BF16))
            abc = E2(nc.sbuf_tensor("abc", [128, E], BF16))
            ts01 = E2(nc.sbuf_tensor("ts01", [128, E], F32))
            gst = [E2(nc.sbuf_tensor(f"gst{p}", [128, 2 * E], BF16)) for p in range(2)]
            al_ps = E2(nc.psum_tensor("al_ps", [128, 512], F32))
            ab_ps = E2(nc.psum_tensor("ab_ps", [128, 512], F32))
            g_ps = [E2(nc.psum_tensor(f"g_ps{p}", [128, 512], F32)) for p in range(2)]

            wffn_ld = SM("wffn_ld")
            abs_dn = SM("abs_dn")
            almm = SM("almm")
            alev = SM("alev")
            abmm = SM("abmm")
            abev = SM("abev")
            bin_dn = SM("bin_dn")
            gmm = SM("gmm")
            silu_dn = SM("silu_dn")
            g_st = [SM("g_st0"), SM("g_st1")]
            v2_self = SM("v2_self")

            for s in range(4):
                prog.dma("gpsimd",
                         wffn[:].rearrange("p (i e) -> p i e", i=NI)[:, :, 512 * s:512 * (s + 1)],
                         wffnT[:, 512 * s:512 * (s + 1)].rearrange("(i p) e -> p i e", p=128),
                         wffn_ld)
            prog.wait("scalar", wffn_ld, 64)
            for sg in range(4):
                if sg >= 1:
                    prog.wait("tensor", alev, sg)
                for i in range(NI):
                    ip = i % 2
                    gidx = NI * sg + i
                    if gidx >= 2:
                        prog.wait("scalar", almm, gidx - 1)
                    t = prog.op("scalar", lambda e, i=i, s=sg, ip=ip: e.activation(
                        awt[ip][:], wffn[:, E * i + 512 * s: E * i + 512 * (s + 1)], AF.Abs),
                        abs_dn)
                    prog.wait("tensor", abs_dn, t)
                    prog.op("tensor", lambda e, i=i, ip=ip: e.matmul(
                        al_ps[0:1, :], ones_sb[:, 0:1], awt[ip][:],
                        start=(i == 0), stop=(i == NI - 1)), almm)
                prog.wait("scalar", almm, almm.n)
                t = prog.op("scalar", lambda e, s=sg: e.activation(
                    arow[0:1, 512 * s:512 * (s + 1)], al_ps[0:1, :], AF.Copy, scale=1.0 / E),
                    alev)
                prog.wait("tensor", alev, t)
                if sg >= 1:
                    prog.wait("tensor", abev, sg)
                t = prog.op("tensor", lambda e, s=sg: e.matmul(
                    ab_ps[:], ones_sb[0:1, 0:128], arow[0:1, 512 * s:512 * (s + 1)],
                    start=True, stop=True), abmm)
                prog.wait("vector", abmm, t)
                prog.op("vector", lambda e, s=sg: e.tensor_copy(
                    abc[:, 512 * s:512 * (s + 1)], ab_ps[:]), abev)
            prog.wait("vector", abev, 4)
            for i in range(NI):
                t = prog.op("vector", lambda e, i=i: e.tensor_scalar(
                    ts01[:], wffn[:, E * i:E * (i + 1)], 0.0, 2.0, ALU.is_ge, ALU.mult), v2_self)
                prog.wait("vector", v2_self, t)
                prog.op("vector", lambda e, i=i: e.scalar_tensor_tensor(
                    wffn[:, E * i:E * (i + 1)], ts01[:], -1.0, abc[:], ALU.add, ALU.mult),
                    bin_dn)
            prog.wait("tensor", bin_dn, NI)
            prog.wait("tensor", const_ld, CONST_TOT)
            mmq = 0
            for tt in range(NTC):
                pr = tt // 2
                bufp = pr % 2
                for og in range(4):
                    gp = mmq % 2
                    if mmq >= 2:
                        prog.wait("tensor", silu_dn, mmq - 1)
                    for i in range(NI):
                        prog.op("tensor", lambda e, tt=tt, og=og, i=i, gp=gp: e.matmul(
                            g_ps[gp][:],
                            xnt[:, TA * i + 512 + 128 * tt: TA * i + 512 + 128 * (tt + 1)],
                            wffn[:, E * i + 512 * og: E * i + 512 * (og + 1)],
                            start=(i == 0), stop=False))
                    t = prog.op("tensor", lambda e, og=og, gp=gp: e.matmul(
                        g_ps[gp][:], ones_sb[0:1, 0:128], bias_sb[0:1, 512 * og:512 * (og + 1)],
                        start=False, stop=True), gmm)
                    prog.wait("scalar", gmm, t)
                    if tt % 2 == 0 and og == 0 and pr >= 2:
                        prog.wait("scalar", g_st[bufp], 16 * (pr // 2))
                    prog.op("scalar", lambda e, tt=tt, og=og, bp=bufp, gp=gp: e.activation(
                        gst[bp][:, E * (tt % 2) + 512 * og: E * (tt % 2) + 512 * (og + 1)],
                        g_ps[gp][:], AF.Silu), silu_dn)
                    mmq += 1
                if tt % 2 == 1:
                    prog.wait("scalar", silu_dn, mmq)
                    prog.dma("scalar",
                             G[tt - 1:tt + 1].rearrange("a p e -> p a e"),
                             gst[bufp][:].rearrange("p (a e) -> p a e", a=2), g_st[bufp])
            prog.wait("scalar", g_st[0], g_st[0].n)
            prog.wait("scalar", g_st[1], g_st[1].n)

        prog.barrier(bar)
        if phases < 3:
            s123.close()
            finish(prog, nc)
            return nc

        with ExitStack() as p3:
            E3 = p3.enter_context
            wsl = [E3(nc.sbuf_tensor(f"wsl{p}", [128, NI * 512], BF16)) for p in range(2)]
            qkst = [E3(nc.sbuf_tensor(f"qkst{p}", [128, TA], BF16)) for p in range(2)]
            vstb = [E3(nc.sbuf_tensor(f"vstb{p}", [128, 4 * 512], BF16)) for p in range(2)]
            qk_ps = [E3(nc.psum_tensor(f"qk_ps{p}", [128, 512], F32)) for p in range(2)]

            slab_ld = [SM("slab_ld0"), SM("slab_ld1")]
            qkmm = SM("qkmm")
            qkev = SM("qkev")
            qkst_dn = [SM("qkst_dn0"), SM("qkst_dn1")]
            vmm = SM("vmm")
            vev = SM("vev")
            vst_dn = [SM("vst_dn0"), SM("vst_dn1")]

            slab_free = {}
            slab_tgt = {}

            def load_slab(s):
                if s >= 2:
                    sem, cnt = slab_free[s - 2]
                    prog.wait("gpsimd", sem, cnt)
                slab_tgt[s] = prog.dma(
                    "gpsimd", wsl[s % 2][:],
                    wqkvT[:, 512 * s:512 * (s + 1)].rearrange("(i p) e -> p i e", p=128),
                    slab_ld[s % 2])

            load_slab(0)
            load_slab(1)
            for s in range(8):
                sp = s % 2
                is_q = s < 4
                ntg = 4 if is_q else 5
                toff = 512 if is_q else 0
                ev_eng = "scalar" if is_q else "vector"
                prog.wait("tensor", slab_ld[sp], slab_tgt[s])
                for el in range(4):
                    h = (4 * s + el) % 16
                    eg = 4 * s + el
                    stp = eg % 2
                    if eg >= 2:
                        prog.wait(ev_eng, qkst_dn[stp], 16 * (eg // 2))
                    for tg in range(ntg):
                        gp = qkmm.n % 2
                        if qkmm.n >= 2:
                            prog.wait("tensor", qkev, qkmm.n - 1)
                        for i in range(NI):
                            prog.op("tensor", lambda e, sp=sp, el=el, tg=tg, i=i, gp=gp, to=toff: e.matmul(
                                qk_ps[gp][:],
                                wsl[sp][:, 512 * i + 128 * el: 512 * i + 128 * (el + 1)],
                                xnt[:, TA * i + to + 512 * tg: TA * i + to + 512 * (tg + 1)],
                                start=(i == 0), stop=(i == NI - 1)),
                                qkmm if i == NI - 1 else None)
                        prog.wait(ev_eng, qkmm, qkmm.n)
                        if is_q:
                            prog.op("scalar", lambda e, tg=tg, stp=stp, gp=gp: e.activation(
                                qkst[stp][:, 512 * tg:512 * (tg + 1)], qk_ps[gp][:],
                                AF.Copy, scale=SCALE), qkev)
                        else:
                            prog.op("vector", lambda e, tg=tg, stp=stp, gp=gp: e.tensor_copy(
                                qkst[stp][:, 512 * tg:512 * (tg + 1)], qk_ps[gp][:]), qkev)
                    prog.wait("sync", qkev, qkev.n)
                    if is_q:
                        prog.dma("sync", QT[h], qkst[stp][:, 0:TC], qkst_dn[stp])
                    else:
                        prog.dma("sync", KT[h], qkst[stp][:, 0:TA], qkst_dn[stp])
                slab_free[s] = (qkmm, qkmm.n)
                if s + 2 < 12:
                    load_slab(s + 2)
            for s in range(8, 12):
                sp = s % 2
                g = s - 8
                prog.wait("tensor", slab_ld[sp], slab_tgt[s])
                for tq in range(5):
                    qg = 5 * g + tq
                    vp = qg % 2
                    if qg >= 2:
                        prog.wait("vector", vst_dn[vp], 16 * (qg // 2))
                    for a in range(4):
                        tau = 4 * tq + a
                        gp = vmm.n % 2
                        if vmm.n >= 2:
                            prog.wait("tensor", vev, vmm.n - 1)
                        for i in range(NI):
                            prog.op("tensor", lambda e, sp=sp, tau=tau, i=i, gp=gp: e.matmul(
                                qk_ps[gp][:],
                                xnt[:, TA * i + 128 * tau: TA * i + 128 * (tau + 1)],
                                wsl[sp][:, 512 * i:512 * (i + 1)],
                                start=(i == 0), stop=(i == NI - 1)),
                                vmm if i == NI - 1 else None)
                        prog.wait("vector", vmm, vmm.n)
                        prog.op("vector", lambda e, a=a, vp=vp, gp=gp: e.tensor_copy(
                            vstb[vp][:, 512 * a:512 * (a + 1)], qk_ps[gp][:]), vev)
                    prog.wait("sync", vev, vev.n)
                    prog.dma("sync",
                             VN[4 * tq:4 * tq + 4, :, 512 * g:512 * (g + 1)].rearrange(
                                 "a p e -> p a e"),
                             vstb[vp][:].rearrange("p (a e) -> p a e", a=4), vst_dn[vp])
                slab_free[s] = (vmm, vmm.n)
                if s + 2 < 12:
                    load_slab(s + 2)
            for sd in qkst_dn + vst_dn:
                prog.wait("sync", sd, sd.n)

        s123.close()
        prog.barrier(bar)
        if phases < 4:
            finish(prog, nc)
            return nc

        with ExitStack() as p4:
            E4 = p4.enter_context
            qtb = [E4(nc.sbuf_tensor(f"qtb{p}", [128, H * 512], BF16)) for p in range(2)]
            ktw = E4(nc.sbuf_tensor("ktw", [128, H * 3 * 512], BF16))
            vnw = E4(nc.sbuf_tensor("vnw", [128, 3 * 4 * E], BF16))
            pts = [E4(nc.sbuf_tensor(f"pts{p}", [128, 8 * 512], BF16)) for p in range(2)]
            rlb = [E4(nc.sbuf_tensor(f"rlb{p}", [128, 512], F32)) for p in range(2)]
            otst = [E4(nc.sbuf_tensor(f"otst{p}", [128, H * 512], BF16)) for p in range(2)]
            s_ps = [E4(nc.psum_tensor(f"s_ps{p}", [128, 512], F32)) for p in range(2)]
            ob_ps = [E4(nc.psum_tensor(f"ob_ps{p}", [128, 512], F32)) for p in range(2)]
            lb_ps = [E4(nc.psum_tensor(f"lb_ps{p}", [128, 512], F32)) for p in range(2)]

            kv_ld = [SM("kv_ld0"), SM("kv_ld1"), SM("kv_ld2")]
            qt_ld = [SM("qt_ld0"), SM("qt_ld1")]
            smm = SM("smm")
            expd = SM("expd")
            avl = SM("avl")
            rld = SM("rld")
            oev = SM("oev")
            ot_st = [SM("ot_st0"), SM("ot_st1")]

            for sd in qkst_dn + vst_dn:
                prog.wait("sync", sd, sd.n)

            def load_kv(gc):
                sl = gc % 3
                prog.dma("sync",
                         ktw[:].rearrange("p (h r t) -> p h r t", h=H, r=3)[:, :, sl, :],
                         KT[:, :, 512 * gc:512 * (gc + 1)].rearrange("h p t -> p h t"),
                         kv_ld[sl])
                prog.dma("sync",
                         vnw[:].rearrange("p (r a e) -> p r a e", r=3, a=4)[:, sl],
                         VN[4 * gc:4 * (gc + 1)].rearrange("a p e -> p a e"),
                         kv_ld[sl])
                return kv_ld[sl].n

            def load_qt(c):
                prog.dma("sync", qtb[c % 2][:].rearrange("p (h t) -> p h t", h=H),
                         QT[:, :, 512 * c:512 * (c + 1)].rearrange("h p t -> p h t"),
                         qt_ld[c % 2])
                return qt_ld[c % 2].n

            kv_rdy = {0: load_kv(0), 1: load_kv(1)}
            qt_rdy = {0: load_qt(0)}

            for cc in range(4):
                gc = cc + 1
                if cc < 3:
                    if cc >= 1:
                        prog.wait("sync", avl, cc * H)
                    kv_rdy[gc + 1] = load_kv(gc + 1)
                    qt_rdy[cc + 1] = load_qt(cc + 1)
                prog.wait("tensor", kv_ld[(gc - 1) % 3], kv_rdy[gc - 1])
                prog.wait("tensor", kv_ld[gc % 3], kv_rdy[gc])
                prog.wait("tensor", qt_ld[cc % 2], qt_rdy[cc])
                for h in range(H):
                    gh = cc * H + h
                    hp = h % 2
                    if gh >= 2:
                        prog.wait("scalar", avl, gh - 1)
                        prog.wait("tensor", oev, gh - 1)
                        prog.wait("tensor", rld, gh - 1)
                    jobs = [("own", jj, 128 * jj, 512, gc % 3, jj) for jj in range(4)] + \
                           [("prev", jj, 0, 128 * (jj + 1), (gc - 1) % 3, jj) for jj in range(4)]
                    for idx, (kind, jj, qlo, qhi, sl, jt) in enumerate(jobs):
                        spar = idx % 2
                        if smm.n >= 2:
                            prog.wait("tensor", expd, smm.n - 1)
                        mend = (kind == "prev" and cc == 0)
                        prog.op("tensor", lambda e, h=h, jj=jj, qlo=qlo, qhi=qhi, sl=sl, sp=spar, cp=cc % 2: e.matmul(
                            s_ps[sp][:, qlo:qhi],
                            ktw[:, 1536 * h + 512 * sl + 128 * jj: 1536 * h + 512 * sl + 128 * (jj + 1)],
                            qtb[cp][:, 512 * h + qlo: 512 * h + qhi],
                            start=True, stop=False))
                        mask = maskOT if kind == "own" else maskPT
                        tS = prog.op("tensor", lambda e, m=mask, jj=jj, sp=spar, stp=not mend: e.matmul(
                            s_ps[sp][:, 128 * jj:128 * (jj + 1)], m[:], ident[:],
                            start=False, stop=stp), None if mend else smm)
                        if mend:
                            tS = prog.op("tensor", lambda e, qhi=qhi, sp=spar: e.matmul(
                                s_ps[sp][:, 0:qhi], m0_sb[0:1, :], ones_sb[0:1, 0:qhi],
                                start=False, stop=True), smm)
                        prog.wait("scalar", smm, tS)
                        tE = prog.op("scalar", lambda e, idx=idx, qlo=qlo, qhi=qhi, sp=spar, hp=hp: e.activation(
                            pts[hp][:, 512 * idx + qlo: 512 * idx + qhi],
                            s_ps[sp][:, qlo:qhi], AF.Exp), expd)
                        prog.wait("tensor", expd, tE)
                        prog.op("tensor", lambda e, h=h, sl=sl, jt=jt, idx=idx, qlo=qlo, qhi=qhi, hp=hp: e.matmul(
                            ob_ps[hp][:, qlo:qhi],
                            vnw[:, 8192 * sl + 2048 * jt + 128 * h: 8192 * sl + 2048 * jt + 128 * (h + 1)],
                            pts[hp][:, 512 * idx + qlo: 512 * idx + qhi],
                            start=(idx == 0), stop=(idx == 7)))
                    for idx, (kind, jj, qlo, qhi, sl, jt) in enumerate(jobs):
                        prog.op("tensor", lambda e, idx=idx, qlo=qlo, qhi=qhi, hp=hp: e.matmul(
                            lb_ps[hp][:, qlo:qhi], ones_sb[:, 0:128],
                            pts[hp][:, 512 * idx + qlo: 512 * idx + qhi],
                            start=(idx == 0), stop=(idx == 7)),
                            avl if idx == 7 else None)
                    prog.wait("vector", avl, avl.n)
                    if gh >= 4 and h == 0:
                        prog.wait("vector", ot_st[cc % 2], 16 * (cc // 2))
                    t = prog.op("vector", lambda e, hp=hp: e.reciprocal(rlb[hp][:], lb_ps[hp][:]), rld)
                    prog.wait("vector", rld, t)
                    prog.op("vector", lambda e, h=h, hp=hp, cp=cc % 2: e.tensor_tensor(
                        otst[cp][:, 512 * h:512 * (h + 1)], ob_ps[hp][:], rlb[hp][:],
                        ALU.mult), oev)
                prog.wait("scalar", oev, oev.n)
                prog.dma("scalar", OT[:, :, 512 * cc:512 * (cc + 1)].rearrange("h p t -> p h t"),
                         otst[cc % 2][:].rearrange("p (h t) -> p h t", h=H), ot_st[cc % 2])
            prog.wait("scalar", ot_st[0], ot_st[0].n)
            prog.wait("scalar", ot_st[1], ot_st[1].n)

        prog.barrier(bar)
        if phases < 5:
            finish(prog, nc)
            return nc

        with ExitStack() as p5:
            E5 = p5.enter_context
            wout_sb = E5(nc.sbuf_tensor("wout_sb", [128, NI * E], BF16))
            otc = [E5(nc.sbuf_tensor(f"otc{p}", [128, H * 128], BF16)) for p in range(2)]
            gt = [E5(nc.sbuf_tensor(f"gt{p}", [128, E], BF16)) for p in range(2)]
            xt = [E5(nc.sbuf_tensor(f"xt{p}", [128, E], F32)) for p in range(2)]
            sfin = [E5(nc.sbuf_tensor(f"sfin{p}", [128, 512], F32)) for p in range(2)]
            obf = [E5(nc.sbuf_tensor(f"obf{p}", [128, E], F32)) for p in range(2)]
            s5_ps = [E5(nc.psum_tensor(f"s5_ps{p}", [128, 512], F32)) for p in range(2)]

            w5_ld = SM("w5_ld")
            t5_ld = [SM("t5_ld0"), SM("t5_ld1")]
            s5mm = SM("s5mm")
            findn = SM("findn")
            outst = [SM("outst0"), SM("outst1")]
            v5_self = SM("v5_self")

            for sd in ot_st:
                prog.wait("sync", sd, sd.n)
            for sd in g_st:
                prog.wait("sync", sd, sd.n)

            for s in range(4):
                prog.dma("gpsimd",
                         wout_sb[:].rearrange("p (i e) -> p i e", i=NI)[:, :, 512 * s:512 * (s + 1)],
                         woutT[:, 512 * s:512 * (s + 1)].rearrange("(i p) e -> p i e", p=128),
                         w5_ld)

            def load_tt(tt):
                tp = tt % 2
                prog.dma("sync", otc[tp][:].rearrange("p (h t) -> p h t", h=H),
                         OT[:, :, 128 * tt:128 * (tt + 1)].rearrange("h p t -> p h t"),
                         t5_ld[tp])
                prog.dma("sync", gt[tp][:], G[tt], t5_ld[tp])
                prog.dma("sync", xt[tp][:],
                         xh[512 + 128 * tt: 512 + 128 * (tt + 1), :], t5_ld[tp])
                return t5_ld[tp].n

            rdy = {0: load_tt(0)}
            prog.wait("tensor", w5_ld, 64)
            mmq = 0
            for tt in range(NTC):
                tp = tt % 2
                if tt < NTC - 1:
                    if tt >= 1:
                        prog.wait("sync", outst[(tt + 1) % 2], 16 * ((tt + 1) // 2))
                    rdy[tt + 1] = load_tt(tt + 1)
                prog.wait("tensor", t5_ld[tp], rdy[tt])
                if tt >= 2:
                    prog.wait("vector", outst[tp], 16 * (tt // 2))
                for og in range(4):
                    gp = mmq % 2
                    if mmq >= 2:
                        prog.wait("tensor", findn, mmq - 1)
                    for hh in range(H):
                        prog.op("tensor", lambda e, tp=tp, og=og, hh=hh, gp=gp: e.matmul(
                            s5_ps[gp][:],
                            otc[tp][:, 128 * hh:128 * (hh + 1)],
                            wout_sb[:, E * hh + 512 * og: E * hh + 512 * (og + 1)],
                            start=(hh == 0), stop=(hh == H - 1)),
                            s5mm if hh == H - 1 else None)
                    mmq += 1
                    prog.wait("vector", s5mm, mmq)
                    t = prog.op("vector", lambda e, tp=tp, og=og, gp=gp: e.tensor_tensor(
                        sfin[gp][:], s5_ps[gp][:], gt[tp][:, 512 * og:512 * (og + 1)],
                        ALU.mult), v5_self)
                    prog.wait("vector", v5_self, t)
                    prog.op("vector", lambda e, tp=tp, og=og, gp=gp: e.tensor_tensor(
                        obf[tp][:, 512 * og:512 * (og + 1)], sfin[gp][:],
                        xt[tp][:, 512 * og:512 * (og + 1)], ALU.add), findn)
                prog.wait("sync", findn, findn.n)
                prog.dma("sync", out[128 * tt:128 * (tt + 1), :], obf[tp][:], outst[tp])
            prog.wait("sync", outst[0], outst[0].n)
            prog.wait("sync", outst[1], outst[1].n)

        finish(prog, nc)
    return nc


def finish(prog, nc):
    with nc.Block() as block:
        for name in ENGS:
            deco = getattr(block, name)

            def runner(eng, name=name):
                for fn in prog.streams[name]:
                    fn(eng)

            deco(runner)



def _consts():
    ident = np.eye(128, dtype=ml_dtypes.bfloat16)
    ones = np.ones((128, 512), dtype=ml_dtypes.bfloat16)
    p = np.arange(128)[:, None]
    f = np.arange(128)[None, :]
    maskP = np.where(p >= f, 0.0, NEG).astype(np.float32)
    maskO = np.where(p <= f, 0.0, NEG).astype(np.float32)
    maskPT = np.ascontiguousarray(maskP.T).astype(ml_dtypes.bfloat16)
    maskOT = np.ascontiguousarray(maskO.T).astype(ml_dtypes.bfloat16)
    return ident, ones, maskPT, maskOT


def make_in_maps(x, wqkv, wout, wffn, bias):
    ident, ones, maskPT, maskOT = _consts()
    wqkvT = np.ascontiguousarray(np.asarray(wqkv, dtype=np.float32).T)
    woutT = np.ascontiguousarray(np.asarray(wout, dtype=np.float32).T)
    wffnT = np.ascontiguousarray(np.asarray(wffn, dtype=np.float32).T)
    bias1 = np.ascontiguousarray(np.asarray(bias, dtype=np.float32).reshape(1, E))
    in_maps = []
    for core in range(8):
        b, j = core // 4, core % 4
        lo = 2048 * j
        if j == 0:
            halo = np.zeros((C, E), dtype=np.float32)
            m0 = np.full((1, 128), NEG, dtype=ml_dtypes.bfloat16)
        else:
            halo = np.asarray(x[b, lo - C:lo], dtype=np.float32)
            m0 = np.zeros((1, 128), dtype=ml_dtypes.bfloat16)
        xhc = np.ascontiguousarray(
            np.concatenate([halo, np.asarray(x[b, lo:lo + TC], dtype=np.float32)], 0))
        in_maps.append(dict(
            xh=xhc, wqkvT=wqkvT, woutT=woutT, wffnT=wffnT, bias1=bias1,
            identN=ident, ones_in=ones, maskPT_in=maskPT, maskOT_in=maskOT, m0row=m0))
    return in_maps


def kernel(L0_attn_x, L0_attn_w_qkv, L0_attn_w_out, L0_ffn_weight, L0_ffn_bias):
    x = np.asarray(L0_attn_x)
    in_maps = make_in_maps(x, L0_attn_w_qkv, L0_attn_w_out, L0_ffn_weight, L0_ffn_bias)
    nc = build()
    res = run_bass_kernel_spmd(nc, in_maps, core_ids=list(range(8)))
    outa = np.empty((2, 8192, 2048), dtype=np.float32)
    for core in range(8):
        b, j = core // 4, core % 4
        outa[b, 2048 * j:2048 * (j + 1)] = res.results[core]["out"]
    return outa


# version 02
# speedup vs baseline: 1.1493x; 1.1493x over previous
import math
import numpy as np
import ml_dtypes
from contextlib import ExitStack

import concourse.bass as bass
import concourse.mybir as mybir
from concourse.bass_utils import run_bass_kernel_spmd

F32 = mybir.dt.float32
BF16 = mybir.dt.bfloat16
AF = mybir.ActivationFunctionType
ALU = mybir.AluOpType

NEG = -1.0e30
E = 2048
NI = 16
H = 16
C = 512
TC = 2048
TA = 2560
NTC = 16
NTA = 20
SCALE = 1.0 / math.sqrt(128.0)
ENGS = ("sync", "tensor", "vector", "scalar", "gpsimd")


class Sem:
    def __init__(self, nc, stack, name):
        self.h = stack.enter_context(nc.semaphore(name))
        self.n = 0


class Prog:

    def __init__(self):
        self.streams = {e: [] for e in ENGS}

    def emit(self, eng, fn):
        self.streams[eng].append(fn)

    def wait(self, eng, sem, tgt):
        if tgt > 0:
            self.emit(eng, lambda e, s=sem.h, t=tgt: e.wait_ge(s, t))

    def op(self, eng, builder, sem=None, k=1):
        if sem is None:
            self.emit(eng, builder)
            return None
        sem.n += k
        self.emit(eng, lambda e, b=builder, s=sem.h, kk=k: b(e).then_inc(s, kk))
        return sem.n

    def dma(self, eng, out_ap, in_ap, sem):
        sem.n += 16
        self.emit(eng, lambda e, o=out_ap, i=in_ap, s=sem.h: e.dma_start(o, i).then_inc(s, 16))
        return sem.n

    def barrier(self, bar):
        for e in ENGS:
            self.emit(e, lambda en, s=bar.h: en.nop().then_inc(s, 1))
        bar.n += len(ENGS)
        tgt = bar.n
        for e in ENGS:
            self.emit(e, lambda en, s=bar.h, t=tgt: en.wait_ge(s, t))


def build(dbg=frozenset(), phases=5):
    nc = bass.Bass("TRN2", target_bir_lowering=False)

    def dram(name, shape, dtype):
        kind = "ExternalOutput" if name in dbg else "Internal"
        return nc.dram_tensor(name, shape, dtype, kind=kind)

    xh = nc.dram_tensor("xh", [TA, E], F32, kind="ExternalInput").ap()
    wqkvT = nc.dram_tensor("wqkvT", [E, 3 * E], F32, kind="ExternalInput").ap()
    woutT = nc.dram_tensor("woutT", [E, E], F32, kind="ExternalInput").ap()
    wffnT = nc.dram_tensor("wffnT", [E, E], F32, kind="ExternalInput").ap()
    bias1 = nc.dram_tensor("bias1", [1, E], F32, kind="ExternalInput").ap()
    identN = nc.dram_tensor("identN", [128, 128], BF16, kind="ExternalInput").ap()
    ones_in = nc.dram_tensor("ones_in", [128, 512], BF16, kind="ExternalInput").ap()
    triP_in = nc.dram_tensor("triP_in", [128, 128], BF16, kind="ExternalInput").ap()
    triO_in = nc.dram_tensor("triO_in", [128, 128], BF16, kind="ExternalInput").ap()
    m0row = nc.dram_tensor("m0row", [1, 128], BF16, kind="ExternalInput").ap()

    out = nc.dram_tensor("out", [TC, E], F32, kind="ExternalOutput").ap()

    QT = dram("QT", [H, 128, TC], BF16).ap()
    KT = dram("KT", [H, 128, TA], BF16).ap()
    VN = dram("VN", [NTA, 128, E], BF16).ap()
    OT = dram("OT", [H, 128, TC], BF16).ap()
    G = dram("G", [NTC, 128, E], BF16).ap()
    if "XNT" in dbg:
        XNTd = nc.dram_tensor("XNT", [NI, 128, TA], BF16, kind="ExternalOutput").ap()

    prog = Prog()

    with ExitStack() as outer:
        EO = outer.enter_context
        ident = EO(nc.sbuf_tensor("ident", [128, 128], BF16))
        ones_sb = EO(nc.sbuf_tensor("ones_sb", [128, 512], BF16))
        triP01 = EO(nc.sbuf_tensor("triP01", [128, 128], BF16))
        triO01 = EO(nc.sbuf_tensor("triO01", [128, 128], BF16))
        m0_sb = EO(nc.sbuf_tensor("m0_sb", [1, 128], BF16))
        bias_sb = EO(nc.sbuf_tensor("bias_sb", [1, E], BF16))

        SM = lambda name: Sem(nc, outer, name)
        bar = SM("bar")
        const_ld = SM("const_ld")
        CONST_TOT = 6 * 16

        prog.dma("sync", ident[:], identN, const_ld)
        prog.dma("sync", ones_sb[:], ones_in, const_ld)
        prog.dma("sync", triP01[:], triP_in, const_ld)
        prog.dma("sync", triO01[:], triO_in, const_ld)
        prog.dma("sync", m0_sb[:], m0row, const_ld)
        prog.dma("gpsimd", bias_sb[:], bias1, const_ld)

        s123 = ExitStack()
        xnt = s123.enter_context(nc.sbuf_tensor("xnt", [128, NI * TA], BF16))
        xnt3 = xnt[:].rearrange("p (i t) -> p i t", i=NI)

        with ExitStack() as p1:
            E1 = p1.enter_context
            xbuf = [E1(nc.sbuf_tensor(f"xbuf{p}", [128, 4 * E], F32)) for p in range(2)]
            xnat = [E1(nc.sbuf_tensor(f"xnat{p}", [128, 4 * E], BF16)) for p in range(2)]
            stats = E1(nc.sbuf_tensor("stats", [128, 24], F32))
            mv = E1(nc.sbuf_tensor("mv", [128, 8], F32))
            veps = E1(nc.sbuf_tensor("veps", [128, 4], F32))
            sqv = E1(nc.sbuf_tensor("sqv", [128, 4], F32))
            rstd = E1(nc.sbuf_tensor("rstd", [128, 4], F32))
            nmr = E1(nc.sbuf_tensor("nmr", [128, 4], F32))
            tp_ps = [E1(nc.psum_tensor(f"tp_ps{p}", [128, 512], F32)) for p in range(2)]

            x_ld = [SM("x_ld0"), SM("x_ld1")]
            v_self = SM("v_self")
            s_self = SM("s_self")
            norm_done = SM("norm_done")
            tp_done = SM("tp_done")
            tp_ev = SM("tp_ev")

            for c in range(5):
                p = c % 2
                if c >= 2:
                    prog.wait("sync", norm_done, c - 1)
                ld_tgt = prog.dma(
                    "sync", xbuf[p][:].rearrange("p (a e) -> p a e", a=4),
                    xh[512 * c:512 * (c + 1), :].rearrange("(a p) e -> p a e", p=128),
                    x_ld[p])
                prog.wait("vector", x_ld[p], ld_tgt)
                for a in range(4):
                    for g4 in range(4):
                        t = prog.op("vector", lambda e, a=a, g=g4, p=p: e.bn_stats(
                            stats[:, 6 * g:6 * (g + 1)],
                            xbuf[p][:, 2048 * a + 512 * g: 2048 * a + 512 * (g + 1)]), v_self)
                    prog.wait("vector", v_self, t)
                    t = prog.op("vector", lambda e, a=a: e.bn_aggr(
                        mv[:, 2 * a:2 * a + 2], stats[:]), v_self)
                    prog.wait("vector", v_self, t)
                t = prog.op("vector", lambda e: e.tensor_scalar(
                    veps[:], mv[:, 1:8:2], 1e-5, None, ALU.add), v_self)
                prog.wait("scalar", v_self, t)
                t = prog.op("scalar", lambda e: e.activation(sqv[:], veps[:], AF.Sqrt), s_self)
                prog.wait("vector", s_self, t)
                t = prog.op("vector", lambda e: e.reciprocal(rstd[:], sqv[:]), v_self)
                prog.wait("vector", v_self, t)
                t = prog.op("vector", lambda e: e.scalar_tensor_tensor(
                    nmr[:], mv[:, 0:8:2], -1.0, rstd[:], ALU.mult, ALU.mult), v_self)
                prog.wait("vector", v_self, t)
                if c >= 2:
                    prog.wait("vector", tp_done, 16 * (c - 1))
                for a in range(4):
                    sem = norm_done if a == 3 else v_self
                    prog.op("vector", lambda e, a=a, p=p: e.tensor_scalar(
                        xnat[p][:, 2048 * a:2048 * (a + 1)],
                        xbuf[p][:, 2048 * a:2048 * (a + 1)],
                        rstd[:, a:a + 1], nmr[:, a:a + 1], ALU.mult, ALU.add), sem)
                prog.wait("tensor", const_ld, CONST_TOT)
                prog.wait("tensor", norm_done, c + 1)
                for a in range(4):
                    for ig in range(4):
                        gpar = tp_done.n % 2
                        if tp_done.n >= 2:
                            prog.wait("tensor", tp_ev, tp_done.n - 1)
                        for kk in range(4):
                            i = 4 * ig + kk
                            prog.op("tensor", lambda e, a=a, i=i, kk=kk, p=p, gp=gpar: e.matmul(
                                tp_ps[gp][:, 128 * kk:128 * (kk + 1)],
                                xnat[p][:, 2048 * a + 128 * i: 2048 * a + 128 * (i + 1)],
                                ident[:], start=True, stop=True),
                                tp_done if kk == 3 else None)
                        tgt = tp_done.n
                        prog.wait("vector", tp_done, tgt)
                        dst = xnt3[:, 4 * ig:4 * ig + 4,
                                   512 * c + 128 * a: 512 * c + 128 * (a + 1)]
                        prog.op("vector", lambda e, d=dst, gp=gpar: e.tensor_copy(
                            d, tp_ps[gp][:].rearrange("p (i t) -> p i t", i=4)), tp_ev)
            if "XNT" in dbg:
                prog.wait("sync", tp_ev, tp_ev.n)
                dmp = SM("dmp")
                for hf in range(2):
                    prog.dma("sync",
                             XNTd[8 * hf:8 * (hf + 1)].rearrange("i p t -> p i t"),
                             xnt3[:, 8 * hf:8 * (hf + 1), :], dmp)
                prog.wait("sync", dmp, dmp.n)

        prog.barrier(bar)
        if phases < 2:
            s123.close()
            finish(prog, nc)
            return nc

        with ExitStack() as p2:
            E2 = p2.enter_context
            wffn = E2(nc.sbuf_tensor("wffn", [128, NI * E], BF16))
            awt = [E2(nc.sbuf_tensor(f"awt{p}", [128, 512], BF16)) for p in range(2)]
            arow = E2(nc.sbuf_tensor("arow", [1, E], BF16))
            abc = E2(nc.sbuf_tensor("abc", [128, E], BF16))
            ts01 = E2(nc.sbuf_tensor("ts01", [128, E], F32))
            gst = [E2(nc.sbuf_tensor(f"gst{p}", [128, 2 * E], BF16)) for p in range(2)]
            al_ps = E2(nc.psum_tensor("al_ps", [128, 512], F32))
            ab_ps = E2(nc.psum_tensor("ab_ps", [128, 512], F32))
            g_ps = [E2(nc.psum_tensor(f"g_ps{p}", [128, 512], F32)) for p in range(2)]

            wffn_ld = SM("wffn_ld")
            abs_dn = SM("abs_dn")
            almm = SM("almm")
            alev = SM("alev")
            abmm = SM("abmm")
            abev = SM("abev")
            bin_dn = SM("bin_dn")
            gmm = SM("gmm")
            silu_dn = SM("silu_dn")
            g_st = [SM("g_st0"), SM("g_st1")]
            v2_self = SM("v2_self")

            for s in range(4):
                prog.dma("gpsimd",
                         wffn[:].rearrange("p (i e) -> p i e", i=NI)[:, :, 512 * s:512 * (s + 1)],
                         wffnT[:, 512 * s:512 * (s + 1)].rearrange("(i p) e -> p i e", p=128),
                         wffn_ld)
            prog.wait("scalar", wffn_ld, 64)
            for sg in range(4):
                if sg >= 1:
                    prog.wait("tensor", alev, sg)
                for i in range(NI):
                    ip = i % 2
                    gidx = NI * sg + i
                    if gidx >= 2:
                        prog.wait("scalar", almm, gidx - 1)
                    t = prog.op("scalar", lambda e, i=i, s=sg, ip=ip: e.activation(
                        awt[ip][:], wffn[:, E * i + 512 * s: E * i + 512 * (s + 1)], AF.Abs),
                        abs_dn)
                    prog.wait("tensor", abs_dn, t)
                    prog.op("tensor", lambda e, i=i, ip=ip: e.matmul(
                        al_ps[0:1, :], ones_sb[:, 0:1], awt[ip][:],
                        start=(i == 0), stop=(i == NI - 1)), almm)
                prog.wait("scalar", almm, almm.n)
                t = prog.op("scalar", lambda e, s=sg: e.activation(
                    arow[0:1, 512 * s:512 * (s + 1)], al_ps[0:1, :], AF.Copy, scale=1.0 / E),
                    alev)
                prog.wait("tensor", alev, t)
                if sg >= 1:
                    prog.wait("tensor", abev, sg)
                t = prog.op("tensor", lambda e, s=sg: e.matmul(
                    ab_ps[:], ones_sb[0:1, 0:128], arow[0:1, 512 * s:512 * (s + 1)],
                    start=True, stop=True), abmm)
                prog.wait("vector", abmm, t)
                prog.op("vector", lambda e, s=sg: e.tensor_copy(
                    abc[:, 512 * s:512 * (s + 1)], ab_ps[:]), abev)
            prog.wait("vector", abev, 4)
            for i in range(NI):
                t = prog.op("vector", lambda e, i=i: e.tensor_scalar(
                    ts01[:], wffn[:, E * i:E * (i + 1)], 0.0, 2.0, ALU.is_ge, ALU.mult), v2_self)
                prog.wait("vector", v2_self, t)
                prog.op("vector", lambda e, i=i: e.scalar_tensor_tensor(
                    wffn[:, E * i:E * (i + 1)], ts01[:], -1.0, abc[:], ALU.add, ALU.mult),
                    bin_dn)
            prog.wait("tensor", bin_dn, NI)
            prog.wait("tensor", const_ld, CONST_TOT)
            mmq = 0
            for tt in range(NTC):
                pr = tt // 2
                bufp = pr % 2
                for og in range(4):
                    gp = mmq % 2
                    if mmq >= 2:
                        prog.wait("tensor", silu_dn, mmq - 1)
                    for i in range(NI):
                        prog.op("tensor", lambda e, tt=tt, og=og, i=i, gp=gp: e.matmul(
                            g_ps[gp][:],
                            xnt[:, TA * i + 512 + 128 * tt: TA * i + 512 + 128 * (tt + 1)],
                            wffn[:, E * i + 512 * og: E * i + 512 * (og + 1)],
                            start=(i == 0), stop=False))
                    t = prog.op("tensor", lambda e, og=og, gp=gp: e.matmul(
                        g_ps[gp][:], ones_sb[0:1, 0:128], bias_sb[0:1, 512 * og:512 * (og + 1)],
                        start=False, stop=True), gmm)
                    prog.wait("scalar", gmm, t)
                    if tt % 2 == 0 and og == 0 and pr >= 2:
                        prog.wait("scalar", g_st[bufp], 16 * (pr // 2))
                    prog.op("scalar", lambda e, tt=tt, og=og, bp=bufp, gp=gp: e.activation(
                        gst[bp][:, E * (tt % 2) + 512 * og: E * (tt % 2) + 512 * (og + 1)],
                        g_ps[gp][:], AF.Silu), silu_dn)
                    mmq += 1
                if tt % 2 == 1:
                    prog.wait("scalar", silu_dn, mmq)
                    prog.dma("scalar",
                             G[tt - 1:tt + 1].rearrange("a p e -> p a e"),
                             gst[bufp][:].rearrange("p (a e) -> p a e", a=2), g_st[bufp])
            prog.wait("scalar", g_st[0], g_st[0].n)
            prog.wait("scalar", g_st[1], g_st[1].n)

        prog.barrier(bar)
        if phases < 3:
            s123.close()
            finish(prog, nc)
            return nc

        with ExitStack() as p3:
            E3 = p3.enter_context
            wsl = [E3(nc.sbuf_tensor(f"wsl{p}", [128, NI * 512], BF16)) for p in range(2)]
            qkst = [E3(nc.sbuf_tensor(f"qkst{p}", [128, TA], BF16)) for p in range(2)]
            vstb = [E3(nc.sbuf_tensor(f"vstb{p}", [128, 4 * 512], BF16)) for p in range(2)]
            qk_ps = [E3(nc.psum_tensor(f"qk_ps{p}", [128, 512], F32)) for p in range(2)]

            slab_ld = [SM("slab_ld0"), SM("slab_ld1")]
            qkmm = SM("qkmm")
            qkev = SM("qkev")
            qkst_dn = [SM("qkst_dn0"), SM("qkst_dn1")]
            vmm = SM("vmm")
            vev = SM("vev")
            vst_dn = [SM("vst_dn0"), SM("vst_dn1")]

            slab_free = {}
            slab_tgt = {}

            def load_slab(s):
                if s >= 2:
                    sem, cnt = slab_free[s - 2]
                    prog.wait("gpsimd", sem, cnt)
                slab_tgt[s] = prog.dma(
                    "gpsimd", wsl[s % 2][:],
                    wqkvT[:, 512 * s:512 * (s + 1)].rearrange("(i p) e -> p i e", p=128),
                    slab_ld[s % 2])

            load_slab(0)
            load_slab(1)
            for s in range(8):
                sp = s % 2
                is_q = s < 4
                ntg = 4 if is_q else 5
                toff = 512 if is_q else 0
                ev_eng = "scalar" if is_q else "vector"
                prog.wait("tensor", slab_ld[sp], slab_tgt[s])
                for el in range(4):
                    h = (4 * s + el) % 16
                    eg = 4 * s + el
                    stp = eg % 2
                    if eg >= 2:
                        prog.wait(ev_eng, qkst_dn[stp], 16 * (eg // 2))
                    for tg in range(ntg):
                        gp = qkmm.n % 2
                        if qkmm.n >= 2:
                            prog.wait("tensor", qkev, qkmm.n - 1)
                        for i in range(NI):
                            prog.op("tensor", lambda e, sp=sp, el=el, tg=tg, i=i, gp=gp, to=toff: e.matmul(
                                qk_ps[gp][:],
                                wsl[sp][:, 512 * i + 128 * el: 512 * i + 128 * (el + 1)],
                                xnt[:, TA * i + to + 512 * tg: TA * i + to + 512 * (tg + 1)],
                                start=(i == 0), stop=(i == NI - 1)),
                                qkmm if i == NI - 1 else None)
                        prog.wait(ev_eng, qkmm, qkmm.n)
                        if is_q:
                            prog.op("scalar", lambda e, tg=tg, stp=stp, gp=gp: e.activation(
                                qkst[stp][:, 512 * tg:512 * (tg + 1)], qk_ps[gp][:],
                                AF.Copy, scale=SCALE), qkev)
                        else:
                            prog.op("vector", lambda e, tg=tg, stp=stp, gp=gp: e.tensor_copy(
                                qkst[stp][:, 512 * tg:512 * (tg + 1)], qk_ps[gp][:]), qkev)
                    prog.wait("sync", qkev, qkev.n)
                    if is_q:
                        prog.dma("sync", QT[h], qkst[stp][:, 0:TC], qkst_dn[stp])
                    else:
                        prog.dma("sync", KT[h], qkst[stp][:, 0:TA], qkst_dn[stp])
                slab_free[s] = (qkmm, qkmm.n)
                if s + 2 < 12:
                    load_slab(s + 2)
            for s in range(8, 12):
                sp = s % 2
                g = s - 8
                prog.wait("tensor", slab_ld[sp], slab_tgt[s])
                for tq in range(5):
                    qg = 5 * g + tq
                    vp = qg % 2
                    if qg >= 2:
                        prog.wait("vector", vst_dn[vp], 16 * (qg // 2))
                    for a in range(4):
                        tau = 4 * tq + a
                        gp = vmm.n % 2
                        if vmm.n >= 2:
                            prog.wait("tensor", vev, vmm.n - 1)
                        for i in range(NI):
                            prog.op("tensor", lambda e, sp=sp, tau=tau, i=i, gp=gp: e.matmul(
                                qk_ps[gp][:],
                                xnt[:, TA * i + 128 * tau: TA * i + 128 * (tau + 1)],
                                wsl[sp][:, 512 * i:512 * (i + 1)],
                                start=(i == 0), stop=(i == NI - 1)),
                                vmm if i == NI - 1 else None)
                        prog.wait("vector", vmm, vmm.n)
                        prog.op("vector", lambda e, a=a, vp=vp, gp=gp: e.tensor_copy(
                            vstb[vp][:, 512 * a:512 * (a + 1)], qk_ps[gp][:]), vev)
                    prog.wait("sync", vev, vev.n)
                    prog.dma("sync",
                             VN[4 * tq:4 * tq + 4, :, 512 * g:512 * (g + 1)].rearrange(
                                 "a p e -> p a e"),
                             vstb[vp][:].rearrange("p (a e) -> p a e", a=4), vst_dn[vp])
                slab_free[s] = (vmm, vmm.n)
                if s + 2 < 12:
                    load_slab(s + 2)
            for sd in qkst_dn + vst_dn:
                prog.wait("sync", sd, sd.n)

        s123.close()
        prog.barrier(bar)
        if phases < 4:
            finish(prog, nc)
            return nc

        with ExitStack() as p4:
            E4 = p4.enter_context
            qtb = [E4(nc.sbuf_tensor(f"qtb{p}", [128, H * 512], BF16)) for p in range(2)]
            ktw = E4(nc.sbuf_tensor("ktw", [128, H * 3 * 512], BF16))
            vnw = E4(nc.sbuf_tensor("vnw", [128, 3 * 4 * E], BF16))
            pts = [E4(nc.sbuf_tensor(f"pts{p}", [128, 8 * 512], BF16)) for p in range(2)]
            rlb = [E4(nc.sbuf_tensor(f"rlb{p}", [128, 512], F32)) for p in range(2)]
            rltmp = [E4(nc.sbuf_tensor(f"rltmp{p}", [128, 512], F32)) for p in range(2)]
            otst = [E4(nc.sbuf_tensor(f"otst{p}", [128, H * 512], BF16)) for p in range(2)]
            s4 = [E4(nc.psum_tensor(f"s4_{p}", [128, 1024], F32)) for p in range(2)]
            ob_ps = [E4(nc.psum_tensor(f"ob_ps{p}", [128, 512], F32)) for p in range(2)]
            lb_ps = [E4(nc.psum_tensor(f"lb_ps{p}", [128, 512], F32)) for p in range(2)]

            kv_ld = [SM("kv_ld0"), SM("kv_ld1"), SM("kv_ld2")]
            qt_ld = [SM("qt_ld0"), SM("qt_ld1")]
            smm = SM("smm")
            expd = SM("expd")
            mskd = SM("mskd")
            avl = SM("avl")
            lnd = SM("lnd")
            rld = SM("rld")
            oev = SM("oev")
            ot_st = [SM("ot_st0"), SM("ot_st1")]

            for sd in qkst_dn + vst_dn:
                prog.wait("sync", sd, sd.n)

            def load_kv(gc):
                sl = gc % 3
                prog.dma("sync",
                         ktw[:].rearrange("p (h r t) -> p h r t", h=H, r=3)[:, :, sl, :],
                         KT[:, :, 512 * gc:512 * (gc + 1)].rearrange("h p t -> p h t"),
                         kv_ld[sl])
                prog.dma("sync",
                         vnw[:].rearrange("p (r a e) -> p r a e", r=3, a=4)[:, sl],
                         VN[4 * gc:4 * (gc + 1)].rearrange("a p e -> p a e"),
                         kv_ld[sl])
                return kv_ld[sl].n

            def load_qt(c):
                prog.dma("sync", qtb[c % 2][:].rearrange("p (h t) -> p h t", h=H),
                         QT[:, :, 512 * c:512 * (c + 1)].rearrange("h p t -> p h t"),
                         qt_ld[c % 2])
                return qt_ld[c % 2].n

            kv_rdy = {0: load_kv(0), 1: load_kv(1)}
            qt_rdy = {0: load_qt(0)}

            for cc in range(4):
                gc = cc + 1
                if cc < 3:
                    if cc >= 1:
                        prog.wait("sync", avl, cc * H)
                    kv_rdy[gc + 1] = load_kv(gc + 1)
                    qt_rdy[cc + 1] = load_qt(cc + 1)
                prog.wait("tensor", kv_ld[(gc - 1) % 3], kv_rdy[gc - 1])
                prog.wait("tensor", kv_ld[gc % 3], kv_rdy[gc])
                prog.wait("tensor", qt_ld[cc % 2], qt_rdy[cc])
                for h in range(H):
                    gh = cc * H + h
                    hp = h % 2
                    if gh >= 2:
                        prog.wait("scalar", avl, gh - 1)
                        prog.wait("tensor", oev, gh - 1)
                        prog.wait("tensor", lnd, gh - 1)
                    jobs = [("own", jj, 128 * jj, 512, gc % 3, jj) for jj in range(4)] + \
                           [("prev", jj, 0, 128 * (jj + 1), (gc - 1) % 3, jj) for jj in range(4)]

                    def emit_spair(q, h=h, gh=gh, cc=cc, gc=gc, jobs=jobs):
                        gq = gh * 4 + q
                        if gq >= 2:
                            prog.wait("tensor", expd, gq - 1)
                        for idx in (2 * q, 2 * q + 1):
                            kind, jj, qlo, qhi, sl, jt = jobs[idx]
                            mend = (kind == "prev" and cc == 0)
                            hf = 512 * (idx % 2)
                            last = (idx % 2 == 1) and not mend
                            prog.op("tensor", lambda e, h=h, jj=jj, qlo=qlo, qhi=qhi, sl=sl,
                                    gq=gq, hf=hf, cp=cc % 2, mend=mend: e.matmul(
                                s4[gq % 2][:, hf + qlo:hf + qhi],
                                ktw[:, 1536 * h + 512 * sl + 128 * jj: 1536 * h + 512 * sl + 128 * (jj + 1)],
                                qtb[cp][:, 512 * h + qlo: 512 * h + qhi],
                                start=True, stop=not mend),
                                smm if last else None)
                            if mend:
                                prog.op("tensor", lambda e, qlo=qlo, qhi=qhi, gq=gq, hf=hf: e.matmul(
                                    s4[gq % 2][:, hf + qlo:hf + qhi], m0_sb[0:1, :],
                                    ones_sb[0:1, 0:qhi - qlo],
                                    start=False, stop=True),
                                    smm if idx % 2 == 1 else None)
                        ulo = min(jobs[2 * q][2], jobs[2 * q + 1][2])
                        uhi = max(jobs[2 * q][3], jobs[2 * q + 1][3])
                        prog.wait("scalar", smm, smm.n)
                        prog.op("scalar", lambda e, q=q, ulo=ulo, uhi=uhi, gq=gq, hp=hp: e.activation(
                            pts[hp][:].rearrange("p (s w) -> p s w", s=8)[:, 2 * q:2 * q + 2, ulo:uhi],
                            s4[gq % 2][:].rearrange("p (two w) -> p two w", two=2)[:, :, ulo:uhi],
                            AF.Exp), expd)
                        prog.wait("vector", expd, expd.n)
                        for idx in (2 * q, 2 * q + 1):
                            kind, jj = jobs[idx][0], jobs[idx][1]
                            tri = triO01 if kind == "own" else triP01
                            sl_ap = pts[hp][:, 512 * idx + 128 * jj: 512 * idx + 128 * (jj + 1)]
                            prog.op("vector", lambda e, d=sl_ap, m=tri: e.tensor_tensor(
                                d, d, m[:], ALU.mult), mskd)

                    def emit_av(idx, h=h, gh=gh, jobs=jobs):
                        kind, jj, qlo, qhi, sl, jt = jobs[idx]
                        prog.wait("tensor", mskd, gh * 8 + idx + 1)
                        prog.op("tensor", lambda e, h=h, sl=sl, jt=jt, idx=idx, qlo=qlo, qhi=qhi, hp=hp: e.matmul(
                            ob_ps[hp][:, qlo:qhi],
                            vnw[:, 8192 * sl + 2048 * jt + 128 * h: 8192 * sl + 2048 * jt + 128 * (h + 1)],
                            pts[hp][:, 512 * idx + qlo: 512 * idx + qhi],
                            start=(idx == 0), stop=(idx == 7)))

                    emit_spair(0)
                    emit_spair(1)
                    for q in range(4):
                        if q + 2 < 4:
                            emit_spair(q + 2)
                        emit_av(2 * q)
                        emit_av(2 * q + 1)
                    for idx, (kind, jj, qlo, qhi, sl, jt) in enumerate(jobs):
                        prog.op("tensor", lambda e, idx=idx, qlo=qlo, qhi=qhi, hp=hp: e.matmul(
                            lb_ps[hp][:, qlo:qhi], ones_sb[:, 0:128],
                            pts[hp][:, 512 * idx + qlo: 512 * idx + qhi],
                            start=(idx == 0), stop=(idx == 7)),
                            avl if idx == 7 else None)
                    prog.wait("scalar", avl, avl.n)
                    prog.op("scalar", lambda e, hp=hp: e.activation(
                        rltmp[hp][:], lb_ps[hp][:], AF.Ln), lnd)
                    t = prog.op("scalar", lambda e, hp=hp: e.activation(
                        rlb[hp][:], rltmp[hp][:], AF.Exp, scale=-1.0), rld)
                    prog.wait("vector", rld, t)
                    if gh >= 4 and h == 0:
                        prog.wait("vector", ot_st[cc % 2], 16 * (cc // 2))
                    prog.op("vector", lambda e, h=h, hp=hp, cp=cc % 2: e.tensor_tensor(
                        otst[cp][:, 512 * h:512 * (h + 1)], ob_ps[hp][:], rlb[hp][:],
                        ALU.mult), oev)
                prog.wait("scalar", oev, oev.n)
                prog.dma("scalar", OT[:, :, 512 * cc:512 * (cc + 1)].rearrange("h p t -> p h t"),
                         otst[cc % 2][:].rearrange("p (h t) -> p h t", h=H), ot_st[cc % 2])
            prog.wait("scalar", ot_st[0], ot_st[0].n)
            prog.wait("scalar", ot_st[1], ot_st[1].n)

        prog.barrier(bar)
        if phases < 5:
            finish(prog, nc)
            return nc

        with ExitStack() as p5:
            E5 = p5.enter_context
            wout_sb = E5(nc.sbuf_tensor("wout_sb", [128, NI * E], BF16))
            otc = [E5(nc.sbuf_tensor(f"otc{p}", [128, H * 128], BF16)) for p in range(2)]
            gt = [E5(nc.sbuf_tensor(f"gt{p}", [128, E], BF16)) for p in range(2)]
            xt = [E5(nc.sbuf_tensor(f"xt{p}", [128, E], F32)) for p in range(2)]
            sfin = [E5(nc.sbuf_tensor(f"sfin{p}", [128, 512], F32)) for p in range(2)]
            obf = [E5(nc.sbuf_tensor(f"obf{p}", [128, E], F32)) for p in range(2)]
            s5_ps = [E5(nc.psum_tensor(f"s5_ps{p}", [128, 512], F32)) for p in range(2)]

            w5_ld = SM("w5_ld")
            t5_ld = [SM("t5_ld0"), SM("t5_ld1")]
            s5mm = SM("s5mm")
            findn = SM("findn")
            outst = [SM("outst0"), SM("outst1")]
            v5_self = SM("v5_self")

            for sd in ot_st:
                prog.wait("sync", sd, sd.n)
            for sd in g_st:
                prog.wait("sync", sd, sd.n)

            for s in range(4):
                prog.dma("gpsimd",
                         wout_sb[:].rearrange("p (i e) -> p i e", i=NI)[:, :, 512 * s:512 * (s + 1)],
                         woutT[:, 512 * s:512 * (s + 1)].rearrange("(i p) e -> p i e", p=128),
                         w5_ld)

            def load_tt(tt):
                tp = tt % 2
                prog.dma("sync", otc[tp][:].rearrange("p (h t) -> p h t", h=H),
                         OT[:, :, 128 * tt:128 * (tt + 1)].rearrange("h p t -> p h t"),
                         t5_ld[tp])
                prog.dma("sync", gt[tp][:], G[tt], t5_ld[tp])
                prog.dma("sync", xt[tp][:],
                         xh[512 + 128 * tt: 512 + 128 * (tt + 1), :], t5_ld[tp])
                return t5_ld[tp].n

            rdy = {0: load_tt(0)}
            prog.wait("tensor", w5_ld, 64)
            mmq = 0
            for tt in range(NTC):
                tp = tt % 2
                if tt < NTC - 1:
                    if tt >= 1:
                        prog.wait("sync", outst[(tt + 1) % 2], 16 * ((tt + 1) // 2))
                    rdy[tt + 1] = load_tt(tt + 1)
                prog.wait("tensor", t5_ld[tp], rdy[tt])
                if tt >= 2:
                    prog.wait("vector", outst[tp], 16 * (tt // 2))
                for og in range(4):
                    gp = mmq % 2
                    if mmq >= 2:
                        prog.wait("tensor", findn, mmq - 1)
                    for hh in range(H):
                        prog.op("tensor", lambda e, tp=tp, og=og, hh=hh, gp=gp: e.matmul(
                            s5_ps[gp][:],
                            otc[tp][:, 128 * hh:128 * (hh + 1)],
                            wout_sb[:, E * hh + 512 * og: E * hh + 512 * (og + 1)],
                            start=(hh == 0), stop=(hh == H - 1)),
                            s5mm if hh == H - 1 else None)
                    mmq += 1
                    prog.wait("vector", s5mm, mmq)
                    t = prog.op("vector", lambda e, tp=tp, og=og, gp=gp: e.tensor_tensor(
                        sfin[gp][:], s5_ps[gp][:], gt[tp][:, 512 * og:512 * (og + 1)],
                        ALU.mult), v5_self)
                    prog.wait("vector", v5_self, t)
                    prog.op("vector", lambda e, tp=tp, og=og, gp=gp: e.tensor_tensor(
                        obf[tp][:, 512 * og:512 * (og + 1)], sfin[gp][:],
                        xt[tp][:, 512 * og:512 * (og + 1)], ALU.add), findn)
                prog.wait("sync", findn, findn.n)
                prog.dma("sync", out[128 * tt:128 * (tt + 1), :], obf[tp][:], outst[tp])
            prog.wait("sync", outst[0], outst[0].n)
            prog.wait("sync", outst[1], outst[1].n)

        finish(prog, nc)
    return nc


def finish(prog, nc):
    with nc.Block() as block:
        for name in ENGS:
            deco = getattr(block, name)

            def runner(eng, name=name):
                for fn in prog.streams[name]:
                    fn(eng)

            deco(runner)



def _consts():
    ident = np.eye(128, dtype=ml_dtypes.bfloat16)
    ones = np.ones((128, 512), dtype=ml_dtypes.bfloat16)
    p = np.arange(128)[:, None]
    f = np.arange(128)[None, :]
    triP = np.where(p >= f, 1.0, 0.0).astype(ml_dtypes.bfloat16)
    triO = np.where(p <= f, 1.0, 0.0).astype(ml_dtypes.bfloat16)
    return ident, ones, triP, triO


def make_in_maps(x, wqkv, wout, wffn, bias):
    ident, ones, triP, triO = _consts()
    wqkvT = np.ascontiguousarray(np.asarray(wqkv, dtype=np.float32).T)
    woutT = np.ascontiguousarray(np.asarray(wout, dtype=np.float32).T)
    wffnT = np.ascontiguousarray(np.asarray(wffn, dtype=np.float32).T)
    bias1 = np.ascontiguousarray(np.asarray(bias, dtype=np.float32).reshape(1, E))
    in_maps = []
    for core in range(8):
        b, j = core // 4, core % 4
        lo = 2048 * j
        if j == 0:
            halo = np.zeros((C, E), dtype=np.float32)
            m0 = np.full((1, 128), NEG, dtype=ml_dtypes.bfloat16)
        else:
            halo = np.asarray(x[b, lo - C:lo], dtype=np.float32)
            m0 = np.zeros((1, 128), dtype=ml_dtypes.bfloat16)
        xhc = np.ascontiguousarray(
            np.concatenate([halo, np.asarray(x[b, lo:lo + TC], dtype=np.float32)], 0))
        in_maps.append(dict(
            xh=xhc, wqkvT=wqkvT, woutT=woutT, wffnT=wffnT, bias1=bias1,
            identN=ident, ones_in=ones, triP_in=triP, triO_in=triO, m0row=m0))
    return in_maps


def kernel(L0_attn_x, L0_attn_w_qkv, L0_attn_w_out, L0_ffn_weight, L0_ffn_bias):
    x = np.asarray(L0_attn_x)
    in_maps = make_in_maps(x, L0_attn_w_qkv, L0_attn_w_out, L0_ffn_weight, L0_ffn_bias)
    nc = build()
    res = run_bass_kernel_spmd(nc, in_maps, core_ids=list(range(8)))
    outa = np.empty((2, 8192, 2048), dtype=np.float32)
    for core in range(8):
        b, j = core // 4, core % 4
        outa[b, 2048 * j:2048 * (j + 1)] = res.results[core]["out"]
    return outa


# version 03
# speedup vs baseline: 1.2225x; 1.0637x over previous
import math
import numpy as np
import ml_dtypes
from contextlib import ExitStack

import concourse.bass as bass
import concourse.mybir as mybir
from concourse.bass_utils import run_bass_kernel_spmd

F32 = mybir.dt.float32
BF16 = mybir.dt.bfloat16
AF = mybir.ActivationFunctionType
ALU = mybir.AluOpType

NEG = -1.0e30
E = 2048
NI = 16
H = 16
C = 512
TC = 2048
TA = 2560
NTC = 16
NTA = 20
SCALE = 1.0 / math.sqrt(128.0)
ENGS = ("sync", "tensor", "vector", "scalar", "gpsimd")


class Sem:
    def __init__(self, nc, stack, name):
        self.h = stack.enter_context(nc.semaphore(name))
        self.n = 0


class Prog:

    def __init__(self):
        self.streams = {e: [] for e in ENGS}

    def emit(self, eng, fn):
        self.streams[eng].append(fn)

    def wait(self, eng, sem, tgt):
        if tgt > 0:
            self.emit(eng, lambda e, s=sem.h, t=tgt: e.wait_ge(s, t))

    def op(self, eng, builder, sem=None, k=1):
        if sem is None:
            self.emit(eng, builder)
            return None
        sem.n += k
        self.emit(eng, lambda e, b=builder, s=sem.h, kk=k: b(e).then_inc(s, kk))
        return sem.n

    def dma(self, eng, out_ap, in_ap, sem):
        sem.n += 16
        self.emit(eng, lambda e, o=out_ap, i=in_ap, s=sem.h: e.dma_start(o, i).then_inc(s, 16))
        return sem.n

    def barrier(self, bar):
        for e in ENGS:
            self.emit(e, lambda en, s=bar.h: en.nop().then_inc(s, 1))
        bar.n += len(ENGS)
        tgt = bar.n
        for e in ENGS:
            self.emit(e, lambda en, s=bar.h, t=tgt: en.wait_ge(s, t))


def build(dbg=frozenset(), phases=5):
    nc = bass.Bass("TRN2", target_bir_lowering=False)

    def dram(name, shape, dtype):
        kind = "ExternalOutput" if name in dbg else "Internal"
        return nc.dram_tensor(name, shape, dtype, kind=kind)

    xh = nc.dram_tensor("xh", [TA, E], F32, kind="ExternalInput").ap()
    wqkvT = nc.dram_tensor("wqkvT", [E, 3 * E], F32, kind="ExternalInput").ap()
    woutT = nc.dram_tensor("woutT", [E, E], F32, kind="ExternalInput").ap()
    wffnT = nc.dram_tensor("wffnT", [E, E], F32, kind="ExternalInput").ap()
    bias1 = nc.dram_tensor("bias1", [1, E], F32, kind="ExternalInput").ap()
    identN = nc.dram_tensor("identN", [128, 128], BF16, kind="ExternalInput").ap()
    ones_in = nc.dram_tensor("ones_in", [128, 512], BF16, kind="ExternalInput").ap()
    triP_in = nc.dram_tensor("triP_in", [128, 128], BF16, kind="ExternalInput").ap()
    triO_in = nc.dram_tensor("triO_in", [128, 128], BF16, kind="ExternalInput").ap()
    m0row = nc.dram_tensor("m0row", [1, 128], BF16, kind="ExternalInput").ap()

    out = nc.dram_tensor("out", [TC, E], F32, kind="ExternalOutput").ap()

    QT = dram("QT", [H, 128, TC], BF16).ap()
    KT = dram("KT", [H, 128, TA], BF16).ap()
    VN = dram("VN", [NTA, 128, E], BF16).ap()
    OT = dram("OT", [H, 128, TC], BF16).ap()
    G = dram("G", [NTC, 128, E], BF16).ap()
    if "XNT" in dbg:
        XNTd = nc.dram_tensor("XNT", [NI, 128, TA], BF16, kind="ExternalOutput").ap()

    prog = Prog()

    with ExitStack() as outer:
        EO = outer.enter_context
        ident = EO(nc.sbuf_tensor("ident", [128, 128], BF16))
        ones_sb = EO(nc.sbuf_tensor("ones_sb", [128, 512], BF16))
        triP01 = EO(nc.sbuf_tensor("triP01", [128, 128], BF16))
        triO01 = EO(nc.sbuf_tensor("triO01", [128, 128], BF16))
        m0_sb = EO(nc.sbuf_tensor("m0_sb", [1, 128], BF16))
        bias_sb = EO(nc.sbuf_tensor("bias_sb", [1, E], BF16))

        SM = lambda name: Sem(nc, outer, name)
        bar = SM("bar")
        const_ld = SM("const_ld")
        CONST_TOT = 6 * 16

        prog.dma("sync", ident[:], identN, const_ld)
        prog.dma("sync", ones_sb[:], ones_in, const_ld)
        prog.dma("sync", triP01[:], triP_in, const_ld)
        prog.dma("sync", triO01[:], triO_in, const_ld)
        prog.dma("sync", m0_sb[:], m0row, const_ld)
        prog.dma("gpsimd", bias_sb[:], bias1, const_ld)

        s123 = ExitStack()
        xnt = s123.enter_context(nc.sbuf_tensor("xnt", [128, NI * TA], BF16))
        xnt3 = xnt[:].rearrange("p (i t) -> p i t", i=NI)

        with ExitStack() as p1:
            E1 = p1.enter_context
            xbuf = [E1(nc.sbuf_tensor(f"xbuf{p}", [128, 4 * E], F32)) for p in range(2)]
            xnat = [E1(nc.sbuf_tensor(f"xnat{p}", [128, 4 * E], BF16)) for p in range(2)]
            stats = E1(nc.sbuf_tensor("stats", [128, 24], F32))
            mv = E1(nc.sbuf_tensor("mv", [128, 8], F32))
            veps = E1(nc.sbuf_tensor("veps", [128, 4], F32))
            sqv = E1(nc.sbuf_tensor("sqv", [128, 4], F32))
            rstd = E1(nc.sbuf_tensor("rstd", [128, 4], F32))
            nmr = E1(nc.sbuf_tensor("nmr", [128, 4], F32))
            tp_ps = [E1(nc.psum_tensor(f"tp_ps{p}", [128, 512], F32)) for p in range(2)]

            x_ld = [SM("x_ld0"), SM("x_ld1")]
            v_self = SM("v_self")
            s_self = SM("s_self")
            norm_done = SM("norm_done")
            tp_done = SM("tp_done")
            tp_ev = SM("tp_ev")

            for c in range(5):
                p = c % 2
                if c >= 2:
                    prog.wait("sync", norm_done, c - 1)
                ld_tgt = prog.dma(
                    "sync", xbuf[p][:].rearrange("p (a e) -> p a e", a=4),
                    xh[512 * c:512 * (c + 1), :].rearrange("(a p) e -> p a e", p=128),
                    x_ld[p])
                prog.wait("vector", x_ld[p], ld_tgt)
                for a in range(4):
                    for g4 in range(4):
                        t = prog.op("vector", lambda e, a=a, g=g4, p=p: e.bn_stats(
                            stats[:, 6 * g:6 * (g + 1)],
                            xbuf[p][:, 2048 * a + 512 * g: 2048 * a + 512 * (g + 1)]), v_self)
                    prog.wait("vector", v_self, t)
                    t = prog.op("vector", lambda e, a=a: e.bn_aggr(
                        mv[:, 2 * a:2 * a + 2], stats[:]), v_self)
                    prog.wait("vector", v_self, t)
                t = prog.op("vector", lambda e: e.tensor_scalar(
                    veps[:], mv[:, 1:8:2], 1e-5, None, ALU.add), v_self)
                prog.wait("scalar", v_self, t)
                t = prog.op("scalar", lambda e: e.activation(sqv[:], veps[:], AF.Sqrt), s_self)
                prog.wait("vector", s_self, t)
                t = prog.op("vector", lambda e: e.reciprocal(rstd[:], sqv[:]), v_self)
                prog.wait("vector", v_self, t)
                t = prog.op("vector", lambda e: e.scalar_tensor_tensor(
                    nmr[:], mv[:, 0:8:2], -1.0, rstd[:], ALU.mult, ALU.mult), v_self)
                prog.wait("vector", v_self, t)
                if c >= 2:
                    prog.wait("vector", tp_done, 16 * (c - 1))
                for a in range(4):
                    sem = norm_done if a == 3 else v_self
                    prog.op("vector", lambda e, a=a, p=p: e.tensor_scalar(
                        xnat[p][:, 2048 * a:2048 * (a + 1)],
                        xbuf[p][:, 2048 * a:2048 * (a + 1)],
                        rstd[:, a:a + 1], nmr[:, a:a + 1], ALU.mult, ALU.add), sem)
                prog.wait("tensor", const_ld, CONST_TOT)
                prog.wait("tensor", norm_done, c + 1)
                for a in range(4):
                    for ig in range(4):
                        gpar = tp_done.n % 2
                        if tp_done.n >= 2:
                            prog.wait("tensor", tp_ev, tp_done.n - 1)
                        for kk in range(4):
                            i = 4 * ig + kk
                            prog.op("tensor", lambda e, a=a, i=i, kk=kk, p=p, gp=gpar: e.matmul(
                                tp_ps[gp][:, 128 * kk:128 * (kk + 1)],
                                xnat[p][:, 2048 * a + 128 * i: 2048 * a + 128 * (i + 1)],
                                ident[:], start=True, stop=True),
                                tp_done if kk == 3 else None)
                        tgt = tp_done.n
                        prog.wait("vector", tp_done, tgt)
                        dst = xnt3[:, 4 * ig:4 * ig + 4,
                                   512 * c + 128 * a: 512 * c + 128 * (a + 1)]
                        prog.op("vector", lambda e, d=dst, gp=gpar: e.tensor_copy(
                            d, tp_ps[gp][:].rearrange("p (i t) -> p i t", i=4)), tp_ev)
            if "XNT" in dbg:
                prog.wait("sync", tp_ev, tp_ev.n)
                dmp = SM("dmp")
                for hf in range(2):
                    prog.dma("sync",
                             XNTd[8 * hf:8 * (hf + 1)].rearrange("i p t -> p i t"),
                             xnt3[:, 8 * hf:8 * (hf + 1), :], dmp)
                prog.wait("sync", dmp, dmp.n)

        prog.barrier(bar)
        if phases < 2:
            s123.close()
            finish(prog, nc)
            return nc

        with ExitStack() as p2:
            E2 = p2.enter_context
            wffn = E2(nc.sbuf_tensor("wffn", [128, NI * E], BF16))
            awt = [E2(nc.sbuf_tensor(f"awt{p}", [128, 512], BF16)) for p in range(2)]
            arow = E2(nc.sbuf_tensor("arow", [1, E], BF16))
            abc = E2(nc.sbuf_tensor("abc", [128, E], BF16))
            ts01 = E2(nc.sbuf_tensor("ts01", [128, E], F32))
            gst = [E2(nc.sbuf_tensor(f"gst{p}", [128, 2 * E], BF16)) for p in range(2)]
            al_ps = E2(nc.psum_tensor("al_ps", [128, 512], F32))
            ab_ps = E2(nc.psum_tensor("ab_ps", [128, 512], F32))
            g_ps = [E2(nc.psum_tensor(f"g_ps{p}", [128, 512], F32)) for p in range(2)]

            wffn_ld = SM("wffn_ld")
            abs_dn = SM("abs_dn")
            almm = SM("almm")
            alev = SM("alev")
            abmm = SM("abmm")
            abev = SM("abev")
            bin_dn = SM("bin_dn")
            gmm = SM("gmm")
            silu_dn = SM("silu_dn")
            g_st = [SM("g_st0"), SM("g_st1")]
            v2_self = SM("v2_self")

            for s in range(4):
                prog.dma("gpsimd",
                         wffn[:].rearrange("p (i e) -> p i e", i=NI)[:, :, 512 * s:512 * (s + 1)],
                         wffnT[:, 512 * s:512 * (s + 1)].rearrange("(i p) e -> p i e", p=128),
                         wffn_ld)
            prog.wait("scalar", wffn_ld, 64)
            for sg in range(4):
                if sg >= 1:
                    prog.wait("tensor", alev, sg)
                for i in range(NI):
                    ip = i % 2
                    gidx = NI * sg + i
                    if gidx >= 2:
                        prog.wait("scalar", almm, gidx - 1)
                    t = prog.op("scalar", lambda e, i=i, s=sg, ip=ip: e.activation(
                        awt[ip][:], wffn[:, E * i + 512 * s: E * i + 512 * (s + 1)], AF.Abs),
                        abs_dn)
                    prog.wait("tensor", abs_dn, t)
                    prog.op("tensor", lambda e, i=i, ip=ip: e.matmul(
                        al_ps[0:1, :], ones_sb[:, 0:1], awt[ip][:],
                        start=(i == 0), stop=(i == NI - 1)), almm)
                prog.wait("scalar", almm, almm.n)
                t = prog.op("scalar", lambda e, s=sg: e.activation(
                    arow[0:1, 512 * s:512 * (s + 1)], al_ps[0:1, :], AF.Copy, scale=1.0 / E),
                    alev)
                prog.wait("tensor", alev, t)
                if sg >= 1:
                    prog.wait("tensor", abev, sg)
                t = prog.op("tensor", lambda e, s=sg: e.matmul(
                    ab_ps[:], ones_sb[0:1, 0:128], arow[0:1, 512 * s:512 * (s + 1)],
                    start=True, stop=True), abmm)
                prog.wait("vector", abmm, t)
                prog.op("vector", lambda e, s=sg: e.tensor_copy(
                    abc[:, 512 * s:512 * (s + 1)], ab_ps[:]), abev)
            prog.wait("vector", abev, 4)
            for i in range(NI):
                t = prog.op("vector", lambda e, i=i: e.tensor_scalar(
                    ts01[:], wffn[:, E * i:E * (i + 1)], 0.0, 2.0, ALU.is_ge, ALU.mult), v2_self)
                prog.wait("vector", v2_self, t)
                prog.op("vector", lambda e, i=i: e.scalar_tensor_tensor(
                    wffn[:, E * i:E * (i + 1)], ts01[:], -1.0, abc[:], ALU.add, ALU.mult),
                    bin_dn)
            prog.wait("tensor", bin_dn, NI)
            prog.wait("tensor", const_ld, CONST_TOT)
            mmq = 0
            for tt in range(NTC):
                pr = tt // 2
                bufp = pr % 2
                for og in range(4):
                    gp = mmq % 2
                    if mmq >= 2:
                        prog.wait("tensor", silu_dn, mmq - 1)
                    for i in range(NI):
                        prog.op("tensor", lambda e, tt=tt, og=og, i=i, gp=gp: e.matmul(
                            g_ps[gp][:],
                            xnt[:, TA * i + 512 + 128 * tt: TA * i + 512 + 128 * (tt + 1)],
                            wffn[:, E * i + 512 * og: E * i + 512 * (og + 1)],
                            start=(i == 0), stop=False))
                    t = prog.op("tensor", lambda e, og=og, gp=gp: e.matmul(
                        g_ps[gp][:], ones_sb[0:1, 0:128], bias_sb[0:1, 512 * og:512 * (og + 1)],
                        start=False, stop=True), gmm)
                    prog.wait("scalar", gmm, t)
                    if tt % 2 == 0 and og == 0 and pr >= 2:
                        prog.wait("scalar", g_st[bufp], 16 * (pr // 2))
                    prog.op("scalar", lambda e, tt=tt, og=og, bp=bufp, gp=gp: e.activation(
                        gst[bp][:, E * (tt % 2) + 512 * og: E * (tt % 2) + 512 * (og + 1)],
                        g_ps[gp][:], AF.Silu), silu_dn)
                    mmq += 1
                if tt % 2 == 1:
                    prog.wait("scalar", silu_dn, mmq)
                    prog.dma("scalar",
                             G[tt - 1:tt + 1].rearrange("a p e -> p a e"),
                             gst[bufp][:].rearrange("p (a e) -> p a e", a=2), g_st[bufp])
            prog.wait("scalar", g_st[0], g_st[0].n)
            prog.wait("scalar", g_st[1], g_st[1].n)

        prog.barrier(bar)
        if phases < 3:
            s123.close()
            finish(prog, nc)
            return nc

        with ExitStack() as p3:
            E3 = p3.enter_context
            wsl = [E3(nc.sbuf_tensor(f"wsl{p}", [128, NI * 512], BF16)) for p in range(2)]
            qkst = [E3(nc.sbuf_tensor(f"qkst{p}", [128, TA], BF16)) for p in range(2)]
            vstb = [E3(nc.sbuf_tensor(f"vstb{p}", [128, 4 * 512], BF16)) for p in range(2)]
            qk_ps = [E3(nc.psum_tensor(f"qk_ps{p}", [128, 512], F32)) for p in range(2)]

            slab_ld = [SM("slab_ld0"), SM("slab_ld1")]
            qkmm = SM("qkmm")
            qkev = SM("qkev")
            qkst_dn = [SM("qkst_dn0"), SM("qkst_dn1")]
            vmm = SM("vmm")
            vev = SM("vev")
            vst_dn = [SM("vst_dn0"), SM("vst_dn1")]

            slab_free = {}
            slab_tgt = {}

            def load_slab(s):
                if s >= 2:
                    sem, cnt = slab_free[s - 2]
                    prog.wait("gpsimd", sem, cnt)
                slab_tgt[s] = prog.dma(
                    "gpsimd", wsl[s % 2][:],
                    wqkvT[:, 512 * s:512 * (s + 1)].rearrange("(i p) e -> p i e", p=128),
                    slab_ld[s % 2])

            load_slab(0)
            load_slab(1)
            for s in range(8):
                sp = s % 2
                is_q = s < 4
                ntg = 4 if is_q else 5
                toff = 512 if is_q else 0
                ev_eng = "scalar" if is_q else "vector"
                prog.wait("tensor", slab_ld[sp], slab_tgt[s])
                for el in range(4):
                    h = (4 * s + el) % 16
                    eg = 4 * s + el
                    stp = eg % 2
                    if eg >= 2:
                        prog.wait(ev_eng, qkst_dn[stp], 16 * (eg // 2))
                    for tg in range(ntg):
                        gp = qkmm.n % 2
                        if qkmm.n >= 2:
                            prog.wait("tensor", qkev, qkmm.n - 1)
                        for i in range(NI):
                            prog.op("tensor", lambda e, sp=sp, el=el, tg=tg, i=i, gp=gp, to=toff: e.matmul(
                                qk_ps[gp][:],
                                wsl[sp][:, 512 * i + 128 * el: 512 * i + 128 * (el + 1)],
                                xnt[:, TA * i + to + 512 * tg: TA * i + to + 512 * (tg + 1)],
                                start=(i == 0), stop=(i == NI - 1)),
                                qkmm if i == NI - 1 else None)
                        prog.wait(ev_eng, qkmm, qkmm.n)
                        if is_q:
                            prog.op("scalar", lambda e, tg=tg, stp=stp, gp=gp: e.activation(
                                qkst[stp][:, 512 * tg:512 * (tg + 1)], qk_ps[gp][:],
                                AF.Copy, scale=SCALE), qkev)
                        else:
                            prog.op("vector", lambda e, tg=tg, stp=stp, gp=gp: e.tensor_copy(
                                qkst[stp][:, 512 * tg:512 * (tg + 1)], qk_ps[gp][:]), qkev)
                    prog.wait("sync", qkev, qkev.n)
                    if is_q:
                        prog.dma("sync", QT[h], qkst[stp][:, 0:TC], qkst_dn[stp])
                    else:
                        prog.dma("sync", KT[h], qkst[stp][:, 0:TA], qkst_dn[stp])
                slab_free[s] = (qkmm, qkmm.n)
                if s + 2 < 12:
                    load_slab(s + 2)
            for s in range(8, 12):
                sp = s % 2
                g = s - 8
                prog.wait("tensor", slab_ld[sp], slab_tgt[s])
                for tq in range(5):
                    qg = 5 * g + tq
                    vp = qg % 2
                    if qg >= 2:
                        prog.wait("vector", vst_dn[vp], 16 * (qg // 2))
                    for a in range(4):
                        tau = 4 * tq + a
                        gp = vmm.n % 2
                        if vmm.n >= 2:
                            prog.wait("tensor", vev, vmm.n - 1)
                        for i in range(NI):
                            prog.op("tensor", lambda e, sp=sp, tau=tau, i=i, gp=gp: e.matmul(
                                qk_ps[gp][:],
                                xnt[:, TA * i + 128 * tau: TA * i + 128 * (tau + 1)],
                                wsl[sp][:, 512 * i:512 * (i + 1)],
                                start=(i == 0), stop=(i == NI - 1)),
                                vmm if i == NI - 1 else None)
                        prog.wait("vector", vmm, vmm.n)
                        prog.op("vector", lambda e, a=a, vp=vp, gp=gp: e.tensor_copy(
                            vstb[vp][:, 512 * a:512 * (a + 1)], qk_ps[gp][:]), vev)
                    prog.wait("sync", vev, vev.n)
                    prog.dma("sync",
                             VN[4 * tq:4 * tq + 4, :, 512 * g:512 * (g + 1)].rearrange(
                                 "a p e -> p a e"),
                             vstb[vp][:].rearrange("p (a e) -> p a e", a=4), vst_dn[vp])
                slab_free[s] = (vmm, vmm.n)
                if s + 2 < 12:
                    load_slab(s + 2)
            for sd in qkst_dn + vst_dn:
                prog.wait("sync", sd, sd.n)

        s123.close()
        prog.barrier(bar)
        if phases < 4:
            finish(prog, nc)
            return nc

        with ExitStack() as p4:
            E4 = p4.enter_context
            qtb = [E4(nc.sbuf_tensor(f"qtb{p}", [128, H * 512], BF16)) for p in range(2)]
            ktw = E4(nc.sbuf_tensor("ktw", [128, H * 3 * 512], BF16))
            vnw = E4(nc.sbuf_tensor("vnw", [128, 3 * 4 * E], BF16))
            pts = [E4(nc.sbuf_tensor(f"pts{p}", [128, 8 * 1024], BF16)) for p in range(2)]
            rlb = [E4(nc.sbuf_tensor(f"rlb{p}", [128, 1024], F32)) for p in range(2)]
            rltmp = E4(nc.sbuf_tensor("rltmp", [128, 1024], F32))
            otst = E4(nc.sbuf_tensor("otst", [128, H * 512], BF16))
            s4 = [E4(nc.psum_tensor(f"s4_{p}", [128, 1024], F32)) for p in range(2)]
            ob2 = E4(nc.psum_tensor("ob2", [128, 1024], F32))
            lb2 = E4(nc.psum_tensor("lb2", [128, 1024], F32))

            kv_ld = [SM("kv_ld0"), SM("kv_ld1"), SM("kv_ld2")]
            qt_ld = [SM("qt_ld0"), SM("qt_ld1")]
            smm = SM("smm")
            expd = SM("expd")
            mskd = SM("mskd")
            avl = SM("avl")
            lnd = SM("lnd")
            rld = SM("rld")
            oev = SM("oev")
            ot_st = SM("ot_st")

            for sd in qkst_dn + vst_dn:
                prog.wait("sync", sd, sd.n)

            def load_kv(gc):
                sl = gc % 3
                prog.dma("sync",
                         ktw[:].rearrange("p (h r t) -> p h r t", h=H, r=3)[:, :, sl, :],
                         KT[:, :, 512 * gc:512 * (gc + 1)].rearrange("h p t -> p h t"),
                         kv_ld[sl])
                prog.dma("sync",
                         vnw[:].rearrange("p (r a e) -> p r a e", r=3, a=4)[:, sl],
                         VN[4 * gc:4 * (gc + 1)].rearrange("a p e -> p a e"),
                         kv_ld[sl])
                return kv_ld[sl].n

            def load_qt(c):
                prog.dma("sync", qtb[c % 2][:].rearrange("p (h t) -> p h t", h=H),
                         QT[:, :, 512 * c:512 * (c + 1)].rearrange("h p t -> p h t"),
                         qt_ld[c % 2])
                return qt_ld[c % 2].n

            kv_rdy = {0: load_kv(0), 1: load_kv(1)}
            qt_rdy = {0: load_qt(0)}

            for cc in range(4):
                gc = cc + 1
                if cc < 3:
                    if cc >= 1:
                        prog.wait("sync", avl, cc * 8)
                    kv_rdy[gc + 1] = load_kv(gc + 1)
                    qt_rdy[cc + 1] = load_qt(cc + 1)
                prog.wait("tensor", kv_ld[(gc - 1) % 3], kv_rdy[gc - 1])
                prog.wait("tensor", kv_ld[gc % 3], kv_rdy[gc])
                prog.wait("tensor", qt_ld[cc % 2], qt_rdy[cc])
                for h2 in range(H // 2):
                    gp2 = cc * 8 + h2
                    pp = h2 % 2
                    if gp2 >= 2:
                        prog.wait("scalar", avl, gp2 - 1)
                    jobs = [("own", jj, 128 * jj, 512, gc % 3, jj) for jj in range(4)] + \
                           [("prev", jj, 0, 128 * (jj + 1), (gc - 1) % 3, jj) for jj in range(4)]

                    def emit_sjob(idx, h2=h2, gp2=gp2, cc=cc, jobs=jobs, pp=pp):
                        gq = gp2 * 8 + idx
                        kind, jj, qlo, qhi, sl, jt = jobs[idx]
                        mend = (kind == "prev" and cc == 0)
                        if gq >= 2:
                            prog.wait("tensor", expd, gq - 1)
                        for a in range(2):
                            h = 2 * h2 + a
                            last = (a == 1) and not mend
                            prog.op("tensor", lambda e, h=h, jj=jj, qlo=qlo, qhi=qhi, sl=sl,
                                    gq=gq, a=a, cp=cc % 2, mend=mend: e.matmul(
                                s4[gq % 2][:, 512 * a + qlo:512 * a + qhi],
                                ktw[:, 1536 * h + 512 * sl + 128 * jj: 1536 * h + 512 * sl + 128 * (jj + 1)],
                                qtb[cp][:, 512 * h + qlo: 512 * h + qhi],
                                start=True, stop=not mend),
                                smm if last else None)
                            if mend:
                                prog.op("tensor", lambda e, qlo=qlo, qhi=qhi, gq=gq, a=a: e.matmul(
                                    s4[gq % 2][:, 512 * a + qlo:512 * a + qhi], m0_sb[0:1, :],
                                    ones_sb[0:1, 0:qhi - qlo],
                                    start=False, stop=True),
                                    smm if a == 1 else None)
                        prog.wait("scalar", smm, smm.n)
                        prog.op("scalar", lambda e, idx=idx, qlo=qlo, qhi=qhi, gq=gq, pp=pp: e.activation(
                            pts[pp][:].rearrange("p (s a w) -> p s a w", s=8, a=2)[:, idx, :, qlo:qhi],
                            s4[gq % 2][:].rearrange("p (a w) -> p a w", a=2)[:, :, qlo:qhi],
                            AF.Exp), expd)
                        prog.wait("vector", expd, expd.n)
                        tri = triO01 if kind == "own" else triP01
                        for a in range(2):
                            sl_ap = pts[pp][:, 1024 * idx + 512 * a + 128 * jj:
                                            1024 * idx + 512 * a + 128 * (jj + 1)]
                            prog.op("vector", lambda e, d=sl_ap, m=tri: e.tensor_tensor(
                                d, d, m[:], ALU.mult), mskd)

                    def emit_avl(idx, h2=h2, gp2=gp2, jobs=jobs, pp=pp):
                        kind, jj, qlo, qhi, sl, jt = jobs[idx]
                        prog.wait("tensor", mskd, 2 * (gp2 * 8 + idx + 1))
                        for a in range(2):
                            h = 2 * h2 + a
                            prog.op("tensor", lambda e, h=h, sl=sl, jt=jt, idx=idx, a=a,
                                    qlo=qlo, qhi=qhi, pp=pp: e.matmul(
                                ob2[:, 512 * a + qlo:512 * a + qhi],
                                vnw[:, 8192 * sl + 2048 * jt + 128 * h: 8192 * sl + 2048 * jt + 128 * (h + 1)],
                                pts[pp][:, 1024 * idx + 512 * a + qlo: 1024 * idx + 512 * a + qhi],
                                start=(idx == 0), stop=(idx == 7)))
                        for a in range(2):
                            prog.op("tensor", lambda e, idx=idx, a=a, qlo=qlo, qhi=qhi, pp=pp: e.matmul(
                                lb2[:, 512 * a + qlo:512 * a + qhi], ones_sb[:, 0:128],
                                pts[pp][:, 1024 * idx + 512 * a + qlo: 1024 * idx + 512 * a + qhi],
                                start=(idx == 0), stop=(idx == 7)),
                                avl if idx == 7 and a == 1 else None)

                    emit_sjob(0)
                    emit_sjob(1)
                    for idx in range(8):
                        if idx + 2 < 8:
                            emit_sjob(idx + 2)
                        if idx == 0 and gp2 >= 1:
                            prog.wait("tensor", oev, gp2)
                            prog.wait("tensor", lnd, gp2)
                        emit_avl(idx)
                    prog.wait("scalar", avl, avl.n)
                    prog.op("scalar", lambda e: e.activation(
                        rltmp[:], lb2[:], AF.Ln), lnd)
                    if gp2 >= 2:
                        prog.wait("scalar", oev, gp2 - 1)
                    t = prog.op("scalar", lambda e, pp=pp: e.activation(
                        rlb[pp][:], rltmp[:], AF.Exp, scale=-1.0), rld)
                    prog.wait("vector", rld, t)
                    if cc >= 1 and h2 == 0:
                        prog.wait("vector", ot_st, 16 * cc)
                    prog.op("vector", lambda e, h2=h2, pp=pp: e.tensor_tensor(
                        otst[:, 1024 * h2:1024 * (h2 + 1)], ob2[:], rlb[pp][:],
                        ALU.mult), oev)
                prog.wait("scalar", oev, oev.n)
                prog.dma("scalar", OT[:, :, 512 * cc:512 * (cc + 1)].rearrange("h p t -> p h t"),
                         otst[:].rearrange("p (h t) -> p h t", h=H), ot_st)
            prog.wait("scalar", ot_st, ot_st.n)

        prog.barrier(bar)
        if phases < 5:
            finish(prog, nc)
            return nc

        with ExitStack() as p5:
            E5 = p5.enter_context
            wout_sb = E5(nc.sbuf_tensor("wout_sb", [128, NI * E], BF16))
            otc = [E5(nc.sbuf_tensor(f"otc{p}", [128, H * 128], BF16)) for p in range(2)]
            gt = [E5(nc.sbuf_tensor(f"gt{p}", [128, E], BF16)) for p in range(2)]
            xt = [E5(nc.sbuf_tensor(f"xt{p}", [128, E], F32)) for p in range(2)]
            sfin = [E5(nc.sbuf_tensor(f"sfin{p}", [128, 512], F32)) for p in range(2)]
            obf = [E5(nc.sbuf_tensor(f"obf{p}", [128, E], F32)) for p in range(2)]
            s5_ps = [E5(nc.psum_tensor(f"s5_ps{p}", [128, 512], F32)) for p in range(2)]

            w5_ld = SM("w5_ld")
            t5_ld = [SM("t5_ld0"), SM("t5_ld1")]
            s5mm = SM("s5mm")
            findn = SM("findn")
            outst = [SM("outst0"), SM("outst1")]
            v5_self = SM("v5_self")

            prog.wait("sync", ot_st, ot_st.n)
            for sd in g_st:
                prog.wait("sync", sd, sd.n)

            for s in range(4):
                prog.dma("gpsimd",
                         wout_sb[:].rearrange("p (i e) -> p i e", i=NI)[:, :, 512 * s:512 * (s + 1)],
                         woutT[:, 512 * s:512 * (s + 1)].rearrange("(i p) e -> p i e", p=128),
                         w5_ld)

            def load_tt(tt):
                tp = tt % 2
                prog.dma("sync", otc[tp][:].rearrange("p (h t) -> p h t", h=H),
                         OT[:, :, 128 * tt:128 * (tt + 1)].rearrange("h p t -> p h t"),
                         t5_ld[tp])
                prog.dma("sync", gt[tp][:], G[tt], t5_ld[tp])
                prog.dma("sync", xt[tp][:],
                         xh[512 + 128 * tt: 512 + 128 * (tt + 1), :], t5_ld[tp])
                return t5_ld[tp].n

            rdy = {0: load_tt(0)}
            prog.wait("tensor", w5_ld, 64)
            mmq = 0
            for tt in range(NTC):
                tp = tt % 2
                if tt < NTC - 1:
                    if tt >= 1:
                        prog.wait("sync", outst[(tt + 1) % 2], 16 * ((tt + 1) // 2))
                    rdy[tt + 1] = load_tt(tt + 1)
                prog.wait("tensor", t5_ld[tp], rdy[tt])
                if tt >= 2:
                    prog.wait("vector", outst[tp], 16 * (tt // 2))
                for og in range(4):
                    gp = mmq % 2
                    if mmq >= 2:
                        prog.wait("tensor", findn, mmq - 1)
                    for hh in range(H):
                        prog.op("tensor", lambda e, tp=tp, og=og, hh=hh, gp=gp: e.matmul(
                            s5_ps[gp][:],
                            otc[tp][:, 128 * hh:128 * (hh + 1)],
                            wout_sb[:, E * hh + 512 * og: E * hh + 512 * (og + 1)],
                            start=(hh == 0), stop=(hh == H - 1)),
                            s5mm if hh == H - 1 else None)
                    mmq += 1
                    prog.wait("vector", s5mm, mmq)
                    t = prog.op("vector", lambda e, tp=tp, og=og, gp=gp: e.tensor_tensor(
                        sfin[gp][:], s5_ps[gp][:], gt[tp][:, 512 * og:512 * (og + 1)],
                        ALU.mult), v5_self)
                    prog.wait("vector", v5_self, t)
                    prog.op("vector", lambda e, tp=tp, og=og, gp=gp: e.tensor_tensor(
                        obf[tp][:, 512 * og:512 * (og + 1)], sfin[gp][:],
                        xt[tp][:, 512 * og:512 * (og + 1)], ALU.add), findn)
                prog.wait("sync", findn, findn.n)
                prog.dma("sync", out[128 * tt:128 * (tt + 1), :], obf[tp][:], outst[tp])
            prog.wait("sync", outst[0], outst[0].n)
            prog.wait("sync", outst[1], outst[1].n)

        finish(prog, nc)
    return nc


def finish(prog, nc):
    with nc.Block() as block:
        for name in ENGS:
            deco = getattr(block, name)

            def runner(eng, name=name):
                for fn in prog.streams[name]:
                    fn(eng)

            deco(runner)



def _consts():
    ident = np.eye(128, dtype=ml_dtypes.bfloat16)
    ones = np.ones((128, 512), dtype=ml_dtypes.bfloat16)
    p = np.arange(128)[:, None]
    f = np.arange(128)[None, :]
    triP = np.where(p >= f, 1.0, 0.0).astype(ml_dtypes.bfloat16)
    triO = np.where(p <= f, 1.0, 0.0).astype(ml_dtypes.bfloat16)
    return ident, ones, triP, triO


def make_in_maps(x, wqkv, wout, wffn, bias):
    ident, ones, triP, triO = _consts()
    wqkvT = np.ascontiguousarray(np.asarray(wqkv, dtype=np.float32).T)
    woutT = np.ascontiguousarray(np.asarray(wout, dtype=np.float32).T)
    wffnT = np.ascontiguousarray(np.asarray(wffn, dtype=np.float32).T)
    bias1 = np.ascontiguousarray(np.asarray(bias, dtype=np.float32).reshape(1, E))
    in_maps = []
    for core in range(8):
        b, j = core // 4, core % 4
        lo = 2048 * j
        if j == 0:
            halo = np.zeros((C, E), dtype=np.float32)
            m0 = np.full((1, 128), NEG, dtype=ml_dtypes.bfloat16)
        else:
            halo = np.asarray(x[b, lo - C:lo], dtype=np.float32)
            m0 = np.zeros((1, 128), dtype=ml_dtypes.bfloat16)
        xhc = np.ascontiguousarray(
            np.concatenate([halo, np.asarray(x[b, lo:lo + TC], dtype=np.float32)], 0))
        in_maps.append(dict(
            xh=xhc, wqkvT=wqkvT, woutT=woutT, wffnT=wffnT, bias1=bias1,
            identN=ident, ones_in=ones, triP_in=triP, triO_in=triO, m0row=m0))
    return in_maps


def kernel(L0_attn_x, L0_attn_w_qkv, L0_attn_w_out, L0_ffn_weight, L0_ffn_bias):
    x = np.asarray(L0_attn_x)
    in_maps = make_in_maps(x, L0_attn_w_qkv, L0_attn_w_out, L0_ffn_weight, L0_ffn_bias)
    nc = build()
    res = run_bass_kernel_spmd(nc, in_maps, core_ids=list(range(8)))
    outa = np.empty((2, 8192, 2048), dtype=np.float32)
    for core in range(8):
        b, j = core // 4, core % 4
        outa[b, 2048 * j:2048 * (j + 1)] = res.results[core]["out"]
    return outa


# version 04
# speedup vs baseline: 1.2572x; 1.0284x over previous
import math
import numpy as np
import ml_dtypes
from contextlib import ExitStack

import concourse.bass as bass
import concourse.mybir as mybir
from concourse.bass_utils import run_bass_kernel_spmd

F32 = mybir.dt.float32
BF16 = mybir.dt.bfloat16
AF = mybir.ActivationFunctionType
ALU = mybir.AluOpType

NEG = -1.0e30
E = 2048
NI = 16
H = 16
C = 512
TC = 2048
TA = 2560
NTC = 16
NTA = 20
SCALE = 1.0 / math.sqrt(128.0)
ENGS = ("sync", "tensor", "vector", "scalar", "gpsimd")


class Sem:
    def __init__(self, nc, stack, name):
        self.h = stack.enter_context(nc.semaphore(name))
        self.n = 0


class Prog:

    def __init__(self):
        self.streams = {e: [] for e in ENGS}

    def emit(self, eng, fn):
        self.streams[eng].append(fn)

    def wait(self, eng, sem, tgt):
        if tgt > 0:
            self.emit(eng, lambda e, s=sem.h, t=tgt: e.wait_ge(s, t))

    def op(self, eng, builder, sem=None, k=1):
        if sem is None:
            self.emit(eng, builder)
            return None
        sem.n += k
        self.emit(eng, lambda e, b=builder, s=sem.h, kk=k: b(e).then_inc(s, kk))
        return sem.n

    def dma(self, eng, out_ap, in_ap, sem):
        sem.n += 16
        self.emit(eng, lambda e, o=out_ap, i=in_ap, s=sem.h: e.dma_start(o, i).then_inc(s, 16))
        return sem.n

    def barrier(self, bar):
        for e in ENGS:
            self.emit(e, lambda en, s=bar.h: en.nop().then_inc(s, 1))
        bar.n += len(ENGS)
        tgt = bar.n
        for e in ENGS:
            self.emit(e, lambda en, s=bar.h, t=tgt: en.wait_ge(s, t))


def build(dbg=frozenset(), phases=5):
    nc = bass.Bass("TRN2", target_bir_lowering=False)

    def dram(name, shape, dtype):
        kind = "ExternalOutput" if name in dbg else "Internal"
        return nc.dram_tensor(name, shape, dtype, kind=kind)

    xh = nc.dram_tensor("xh", [TA, E], F32, kind="ExternalInput").ap()
    wqkvT = nc.dram_tensor("wqkvT", [E, 3 * E], F32, kind="ExternalInput").ap()
    woutT = nc.dram_tensor("woutT", [E, E], F32, kind="ExternalInput").ap()
    wffnT = nc.dram_tensor("wffnT", [E, E], F32, kind="ExternalInput").ap()
    bias1 = nc.dram_tensor("bias1", [1, E], F32, kind="ExternalInput").ap()
    identN = nc.dram_tensor("identN", [128, 128], BF16, kind="ExternalInput").ap()
    ones_in = nc.dram_tensor("ones_in", [128, 512], BF16, kind="ExternalInput").ap()
    triP_in = nc.dram_tensor("triP_in", [128, 128], BF16, kind="ExternalInput").ap()
    triO_in = nc.dram_tensor("triO_in", [128, 128], BF16, kind="ExternalInput").ap()
    m0row = nc.dram_tensor("m0row", [1, 128], BF16, kind="ExternalInput").ap()

    out = nc.dram_tensor("out", [TC, E], F32, kind="ExternalOutput").ap()

    QT = dram("QT", [H, 128, TC], BF16).ap()
    KT = dram("KT", [H, 128, TA], BF16).ap()
    VN = dram("VN", [NTA, 128, E], BF16).ap()
    OT = dram("OT", [H, 128, TC], BF16).ap()
    G = dram("G", [NTC, 128, E], BF16).ap()
    if "XNT" in dbg:
        XNTd = nc.dram_tensor("XNT", [NI, 128, TA], BF16, kind="ExternalOutput").ap()

    prog = Prog()

    with ExitStack() as outer:
        EO = outer.enter_context
        ident = EO(nc.sbuf_tensor("ident", [128, 128], BF16))
        ones_sb = EO(nc.sbuf_tensor("ones_sb", [128, 512], BF16))
        triP01 = EO(nc.sbuf_tensor("triP01", [128, 128], BF16))
        triO01 = EO(nc.sbuf_tensor("triO01", [128, 128], BF16))
        m0_sb = EO(nc.sbuf_tensor("m0_sb", [1, 128], BF16))
        bias_sb = EO(nc.sbuf_tensor("bias_sb", [1, E], BF16))

        SM = lambda name: Sem(nc, outer, name)
        bar = SM("bar")
        const_ld = SM("const_ld")
        CONST_TOT = 6 * 16

        prog.dma("sync", ident[:], identN, const_ld)
        prog.dma("sync", ones_sb[:], ones_in, const_ld)
        prog.dma("sync", triP01[:], triP_in, const_ld)
        prog.dma("sync", triO01[:], triO_in, const_ld)
        prog.dma("sync", m0_sb[:], m0row, const_ld)
        prog.dma("gpsimd", bias_sb[:], bias1, const_ld)

        abc = EO(nc.sbuf_tensor("abc", [128, E], BF16))
        s123 = ExitStack()
        xnt = s123.enter_context(nc.sbuf_tensor("xnt", [128, NI * TA], BF16))
        xnt3 = xnt[:].rearrange("p (i t) -> p i t", i=NI)

        with ExitStack() as p1:
            E1 = p1.enter_context
            xbuf = [E1(nc.sbuf_tensor(f"xbuf{p}", [128, 2 * E], F32)) for p in range(2)]
            xnat = [E1(nc.sbuf_tensor(f"xnat{p}", [128, 2 * E], BF16)) for p in range(2)]
            stats = E1(nc.sbuf_tensor("stats", [128, 24], F32))
            mv = E1(nc.sbuf_tensor("mv", [128, 4], F32))
            veps = E1(nc.sbuf_tensor("veps", [128, 2], F32))
            sqv = E1(nc.sbuf_tensor("sqv", [128, 2], F32))
            rstd = E1(nc.sbuf_tensor("rstd", [128, 2], F32))
            nmr = E1(nc.sbuf_tensor("nmr", [128, 2], F32))
            wfs = [E1(nc.sbuf_tensor(f"wfs{p}", [128, E], BF16)) for p in range(2)]
            abst = [E1(nc.sbuf_tensor(f"abst{p}", [128, E], F32)) for p in range(2)]
            accf = E1(nc.sbuf_tensor("accf", [128, E], F32))
            accbf = E1(nc.sbuf_tensor("accbf", [128, E], BF16))
            arow = E1(nc.sbuf_tensor("arow", [1, E], BF16))
            tp_ps = [E1(nc.psum_tensor(f"tp_ps{p}", [128, 512], F32)) for p in range(2)]
            al_ps = E1(nc.psum_tensor("al_ps", [128, 512], F32))

            x_ld = [SM("x_ld0"), SM("x_ld1")]
            v_self = SM("v_self")
            s_self = SM("s_self")
            norm_done = SM("norm_done")
            tp_done = SM("tp_done")
            tp_ev = SM("tp_ev")
            wf_ld = [SM(f"wf_ld{p}") for p in range(2)]
            abs_dn = SM("abs_dn")
            acA = SM("acA")
            alev = SM("alev")
            abev = SM("abev")

            def emit_alpha_tile(i):
                ip = i % 2
                if i >= 2:
                    prog.wait("gpsimd", abs_dn, i - 1)
                prog.dma("gpsimd", wfs[ip][:], wffnT[128 * i:128 * (i + 1), :], wf_ld[ip])
                prog.wait("scalar", wf_ld[ip], 16 * (i // 2 + 1))
                if i >= 2:
                    prog.wait("scalar", acA, i - 1)
                prog.op("scalar", lambda e, ip=ip: e.activation(
                    abst[ip][:], wfs[ip][:], AF.Abs), abs_dn)
                prog.wait("gpsimd", abs_dn, abs_dn.n)
                if i == 0:
                    prog.op("gpsimd", lambda e: e.tensor_copy(accf[:], abst[0][:]), acA)
                else:
                    prog.op("gpsimd", lambda e, ip=ip: e.tensor_tensor(
                        accf[:], accf[:], abst[ip][:], ALU.add), acA)

            for c in range(10):
                p = c % 2
                if c >= 2:
                    prog.wait("sync", norm_done, c - 1)
                ld_tgt = prog.dma(
                    "sync", xbuf[p][:].rearrange("p (a e) -> p a e", a=2),
                    xh[256 * c:256 * (c + 1), :].rearrange("(a p) e -> p a e", p=128),
                    x_ld[p])
                if c < 8:
                    emit_alpha_tile(2 * c)
                    emit_alpha_tile(2 * c + 1)
                prog.wait("vector", x_ld[p], ld_tgt)
                for a in range(2):
                    for g4 in range(4):
                        t = prog.op("vector", lambda e, a=a, g=g4, p=p: e.bn_stats(
                            stats[:, 6 * g:6 * (g + 1)],
                            xbuf[p][:, 2048 * a + 512 * g: 2048 * a + 512 * (g + 1)]), v_self)
                    prog.wait("vector", v_self, t)
                    t = prog.op("vector", lambda e, a=a: e.bn_aggr(
                        mv[:, 2 * a:2 * a + 2], stats[:]), v_self)
                    prog.wait("vector", v_self, t)
                t = prog.op("vector", lambda e: e.tensor_scalar(
                    veps[:], mv[:, 1:4:2], 1e-5, None, ALU.add), v_self)
                prog.wait("scalar", v_self, t)
                t = prog.op("scalar", lambda e: e.activation(sqv[:], veps[:], AF.Sqrt), s_self)
                prog.wait("vector", s_self, t)
                t = prog.op("vector", lambda e: e.reciprocal(rstd[:], sqv[:]), v_self)
                prog.wait("vector", v_self, t)
                t = prog.op("vector", lambda e: e.scalar_tensor_tensor(
                    nmr[:], mv[:, 0:4:2], -1.0, rstd[:], ALU.mult, ALU.mult), v_self)
                prog.wait("vector", v_self, t)
                if c >= 2:
                    prog.wait("vector", tp_done, 8 * (c - 1))
                for a in range(2):
                    sem = norm_done if a == 1 else v_self
                    prog.op("vector", lambda e, a=a, p=p: e.tensor_scalar(
                        xnat[p][:, 2048 * a:2048 * (a + 1)],
                        xbuf[p][:, 2048 * a:2048 * (a + 1)],
                        rstd[:, a:a + 1], nmr[:, a:a + 1], ALU.mult, ALU.add), sem)
                prog.wait("tensor", const_ld, CONST_TOT)
                prog.wait("tensor", norm_done, c + 1)
                for a in range(2):
                    for ig in range(4):
                        gpar = tp_done.n % 2
                        if tp_done.n >= 2:
                            prog.wait("tensor", tp_ev, tp_done.n - 1)
                        for kk in range(4):
                            i = 4 * ig + kk
                            prog.op("tensor", lambda e, a=a, i=i, kk=kk, p=p, gp=gpar: e.matmul(
                                tp_ps[gp][:, 128 * kk:128 * (kk + 1)],
                                xnat[p][:, 2048 * a + 128 * i: 2048 * a + 128 * (i + 1)],
                                ident[:], start=True, stop=True),
                                tp_done if kk == 3 else None)
                        tgt = tp_done.n
                        prog.wait("vector", tp_done, tgt)
                        dst = xnt3[:, 4 * ig:4 * ig + 4,
                                   256 * c + 128 * a: 256 * c + 128 * (a + 1)]
                        prog.op("vector", lambda e, d=dst, gp=gpar: e.tensor_copy(
                            d, tp_ps[gp][:].rearrange("p (i t) -> p i t", i=4)), tp_ev)
            almm = SM("almm")
            abmm = SM("abmm")
            prog.wait("vector", acA, 16)
            tcast = prog.op("vector", lambda e: e.tensor_copy(accbf[:], accf[:]), v_self)
            prog.wait("tensor", v_self, tcast)
            for g in range(4):
                if g >= 1:
                    prog.wait("tensor", alev, g)
                    prog.wait("tensor", abev, g)
                t = prog.op("tensor", lambda e, g=g: e.matmul(
                    al_ps[0:1, :], ones_sb[:, 0:1], accbf[:, 512 * g:512 * (g + 1)],
                    start=True, stop=True), almm)
                prog.wait("scalar", almm, t)
                t = prog.op("scalar", lambda e, g=g: e.activation(
                    arow[0:1, 512 * g:512 * (g + 1)], al_ps[0:1, :], AF.Copy,
                    scale=1.0 / E), alev)
                prog.wait("tensor", alev, t)
                t = prog.op("tensor", lambda e, g=g: e.matmul(
                    tp_ps[0][:], ones_sb[0:1, 0:128], arow[0:1, 512 * g:512 * (g + 1)],
                    start=True, stop=True), abmm)
                prog.wait("vector", abmm, t)
                prog.op("vector", lambda e, g=g: e.tensor_copy(
                    abc[:, 512 * g:512 * (g + 1)], tp_ps[0][:]), abev)
            if "XNT" in dbg:
                prog.wait("sync", tp_ev, tp_ev.n)
                dmp = SM("dmp")
                for hf in range(2):
                    prog.dma("sync",
                             XNTd[8 * hf:8 * (hf + 1)].rearrange("i p t -> p i t"),
                             xnt3[:, 8 * hf:8 * (hf + 1), :], dmp)
                prog.wait("sync", dmp, dmp.n)

        prog.barrier(bar)
        if phases < 2:
            s123.close()
            finish(prog, nc)
            return nc

        s23 = ExitStack()
        wsl = [s23.enter_context(nc.sbuf_tensor(f"wsl{p}", [128, NI * 512], BF16))
               for p in range(2)]
        slab_ld = [SM("slab_ld0"), SM("slab_ld1")]
        slab_free = {}
        slab_tgt = {}

        def load_slab(s):
            if s >= 2:
                sem, cnt = slab_free[s - 2]
                prog.wait("gpsimd", sem, cnt)
            slab_tgt[s] = prog.dma(
                "gpsimd", wsl[s % 2][:],
                wqkvT[:, 512 * s:512 * (s + 1)].rearrange("(i p) e -> p i e", p=128),
                slab_ld[s % 2])

        with ExitStack() as p2:
            E2 = p2.enter_context
            wffn = E2(nc.sbuf_tensor("wffn", [128, NI * E], BF16))
            gst = [E2(nc.sbuf_tensor(f"gst{p}", [128, 2 * E], BF16)) for p in range(2)]
            g_ps = [E2(nc.psum_tensor(f"g_ps{p}", [128, 512], F32)) for p in range(2)]

            wff_ld = [SM(f"wff_ld{r}") for r in range(4)]
            bin_dn = SM("bin_dn")
            gmm = SM("gmm")
            silu_dn = SM("silu_dn")
            g_st = [SM("g_st0"), SM("g_st1")]
            v2_self = SM("v2_self")

            wffn3 = wffn[:].rearrange("p (i e) -> p i e", i=NI)
            for r in range(4):
                prog.dma("gpsimd", wffn3[:, 4 * r:4 * (r + 1), :],
                         wffnT[512 * r:512 * (r + 1), :].rearrange("(i p) e -> p i e", p=128),
                         wff_ld[r])
            load_slab(0)
            load_slab(1)
            prog.wait("vector", abev, 4)
            for i in range(NI):
                if i % 4 == 0:
                    prog.wait("vector", wff_ld[i // 4], 16)
                t = prog.op("vector", lambda e, i=i: e.tensor_scalar(
                    wffn[:, E * i:E * (i + 1)], wffn[:, E * i:E * (i + 1)],
                    0.0, 2.0, ALU.is_ge, ALU.mult), v2_self)
                prog.wait("vector", v2_self, t)
                prog.op("vector", lambda e, i=i: e.scalar_tensor_tensor(
                    wffn[:, E * i:E * (i + 1)], wffn[:, E * i:E * (i + 1)],
                    -1.0, abc[:], ALU.add, ALU.mult), bin_dn)
            prog.wait("tensor", bin_dn, NI)
            prog.wait("tensor", const_ld, CONST_TOT)
            mmq = 0
            for tt in range(NTC):
                pr = tt // 2
                bufp = pr % 2
                for og in range(4):
                    gp = mmq % 2
                    if mmq >= 2:
                        prog.wait("tensor", silu_dn, mmq - 1)
                    for i in range(NI):
                        prog.op("tensor", lambda e, tt=tt, og=og, i=i, gp=gp: e.matmul(
                            g_ps[gp][:],
                            xnt[:, TA * i + 512 + 128 * tt: TA * i + 512 + 128 * (tt + 1)],
                            wffn[:, E * i + 512 * og: E * i + 512 * (og + 1)],
                            start=(i == 0), stop=False))
                    t = prog.op("tensor", lambda e, og=og, gp=gp: e.matmul(
                        g_ps[gp][:], ones_sb[0:1, 0:128], bias_sb[0:1, 512 * og:512 * (og + 1)],
                        start=False, stop=True), gmm)
                    prog.wait("scalar", gmm, t)
                    if tt % 2 == 0 and og == 0 and pr >= 2:
                        prog.wait("scalar", g_st[bufp], 16 * (pr // 2))
                    prog.op("scalar", lambda e, tt=tt, og=og, bp=bufp, gp=gp: e.activation(
                        gst[bp][:, E * (tt % 2) + 512 * og: E * (tt % 2) + 512 * (og + 1)],
                        g_ps[gp][:], AF.Silu), silu_dn)
                    mmq += 1
                if tt % 2 == 1:
                    prog.wait("scalar", silu_dn, mmq)
                    prog.dma("scalar",
                             G[tt - 1:tt + 1].rearrange("a p e -> p a e"),
                             gst[bufp][:].rearrange("p (a e) -> p a e", a=2), g_st[bufp])
            prog.wait("scalar", g_st[0], g_st[0].n)
            prog.wait("scalar", g_st[1], g_st[1].n)

        prog.barrier(bar)
        if phases < 3:
            s23.close()
            s123.close()
            finish(prog, nc)
            return nc

        with ExitStack() as p3:
            E3 = p3.enter_context
            qkst = [E3(nc.sbuf_tensor(f"qkst{p}", [128, TA], BF16)) for p in range(2)]
            vstb = [E3(nc.sbuf_tensor(f"vstb{p}", [128, 4 * 512], BF16)) for p in range(2)]
            qk_ps = [E3(nc.psum_tensor(f"qk_ps{p}", [128, 512], F32)) for p in range(2)]

            qkmm = SM("qkmm")
            qkev = SM("qkev")
            qkst_dn = [SM("qkst_dn0"), SM("qkst_dn1")]
            vmm = SM("vmm")
            vev = SM("vev")
            vst_dn = [SM("vst_dn0"), SM("vst_dn1")]

            for s in range(8):
                sp = s % 2
                is_q = s < 4
                ntg = 4 if is_q else 5
                toff = 512 if is_q else 0
                ev_eng = "scalar" if is_q else "vector"
                prog.wait("tensor", slab_ld[sp], slab_tgt[s])
                for el in range(4):
                    h = (4 * s + el) % 16
                    eg = 4 * s + el
                    stp = eg % 2
                    if eg >= 2:
                        prog.wait(ev_eng, qkst_dn[stp], 16 * (eg // 2))
                    for tg in range(ntg):
                        gp = qkmm.n % 2
                        if qkmm.n >= 2:
                            prog.wait("tensor", qkev, qkmm.n - 1)
                        for i in range(NI):
                            prog.op("tensor", lambda e, sp=sp, el=el, tg=tg, i=i, gp=gp, to=toff: e.matmul(
                                qk_ps[gp][:],
                                wsl[sp][:, 512 * i + 128 * el: 512 * i + 128 * (el + 1)],
                                xnt[:, TA * i + to + 512 * tg: TA * i + to + 512 * (tg + 1)],
                                start=(i == 0), stop=(i == NI - 1)),
                                qkmm if i == NI - 1 else None)
                        prog.wait(ev_eng, qkmm, qkmm.n)
                        if is_q:
                            prog.op("scalar", lambda e, tg=tg, stp=stp, gp=gp: e.activation(
                                qkst[stp][:, 512 * tg:512 * (tg + 1)], qk_ps[gp][:],
                                AF.Copy, scale=SCALE), qkev)
                        else:
                            prog.op("vector", lambda e, tg=tg, stp=stp, gp=gp: e.tensor_copy(
                                qkst[stp][:, 512 * tg:512 * (tg + 1)], qk_ps[gp][:]), qkev)
                    prog.wait("sync", qkev, qkev.n)
                    if is_q:
                        prog.dma("sync", QT[h], qkst[stp][:, 0:TC], qkst_dn[stp])
                    else:
                        prog.dma("sync", KT[h], qkst[stp][:, 0:TA], qkst_dn[stp])
                slab_free[s] = (qkmm, qkmm.n)
                if s + 2 < 12:
                    load_slab(s + 2)
            for s in range(8, 12):
                sp = s % 2
                g = s - 8
                prog.wait("tensor", slab_ld[sp], slab_tgt[s])
                for tq in range(5):
                    qg = 5 * g + tq
                    vp = qg % 2
                    if qg >= 2:
                        prog.wait("vector", vst_dn[vp], 16 * (qg // 2))
                    for a in range(4):
                        tau = 4 * tq + a
                        gp = vmm.n % 2
                        if vmm.n >= 2:
                            prog.wait("tensor", vev, vmm.n - 1)
                        for i in range(NI):
                            prog.op("tensor", lambda e, sp=sp, tau=tau, i=i, gp=gp: e.matmul(
                                qk_ps[gp][:],
                                xnt[:, TA * i + 128 * tau: TA * i + 128 * (tau + 1)],
                                wsl[sp][:, 512 * i:512 * (i + 1)],
                                start=(i == 0), stop=(i == NI - 1)),
                                vmm if i == NI - 1 else None)
                        prog.wait("vector", vmm, vmm.n)
                        prog.op("vector", lambda e, a=a, vp=vp, gp=gp: e.tensor_copy(
                            vstb[vp][:, 512 * a:512 * (a + 1)], qk_ps[gp][:]), vev)
                    prog.wait("sync", vev, vev.n)
                    prog.dma("sync",
                             VN[4 * tq:4 * tq + 4, :, 512 * g:512 * (g + 1)].rearrange(
                                 "a p e -> p a e"),
                             vstb[vp][:].rearrange("p (a e) -> p a e", a=4), vst_dn[vp])
                slab_free[s] = (vmm, vmm.n)
                if s + 2 < 12:
                    load_slab(s + 2)
            for sd in qkst_dn + vst_dn:
                prog.wait("sync", sd, sd.n)

        s23.close()
        s123.close()
        prog.barrier(bar)
        if phases < 4:
            finish(prog, nc)
            return nc

        with ExitStack() as p4:
            E4 = p4.enter_context
            qtb = [E4(nc.sbuf_tensor(f"qtb{p}", [128, H * 512], BF16)) for p in range(2)]
            ktw = E4(nc.sbuf_tensor("ktw", [128, H * 3 * 512], BF16))
            vnw = E4(nc.sbuf_tensor("vnw", [128, 3 * 4 * E], BF16))
            pts = [E4(nc.sbuf_tensor(f"pts{p}", [128, 8 * 1024], BF16)) for p in range(2)]
            rlb = [E4(nc.sbuf_tensor(f"rlb{p}", [128, 1024], F32)) for p in range(2)]
            rltmp = E4(nc.sbuf_tensor("rltmp", [128, 1024], F32))
            otst = E4(nc.sbuf_tensor("otst", [128, H * 512], BF16))
            otraw = [E4(nc.sbuf_tensor(f"otraw{p}", [128, 1024], BF16)) for p in range(2)]
            s4 = [E4(nc.psum_tensor(f"s4_{p}", [128, 1024], F32)) for p in range(2)]
            ob2 = E4(nc.psum_tensor("ob2", [128, 1024], F32))
            lb2 = E4(nc.psum_tensor("lb2", [128, 1024], F32))

            kv_ld = [SM("kv_ld0"), SM("kv_ld1"), SM("kv_ld2")]
            qt_ld = [SM("qt_ld0"), SM("qt_ld1")]
            smm = SM("smm")
            expd = SM("expd")
            mskd = SM("mskd")
            avl = SM("avl")
            lnd = SM("lnd")
            rld = SM("rld")
            oraw = SM("oraw")
            oev = SM("oev")
            ot_st = SM("ot_st")

            for sd in qkst_dn + vst_dn:
                prog.wait("sync", sd, sd.n)

            def load_kv(gc):
                sl = gc % 3
                prog.dma("sync",
                         ktw[:].rearrange("p (h r t) -> p h r t", h=H, r=3)[:, :, sl, :],
                         KT[:, :, 512 * gc:512 * (gc + 1)].rearrange("h p t -> p h t"),
                         kv_ld[sl])
                prog.dma("sync",
                         vnw[:].rearrange("p (r a e) -> p r a e", r=3, a=4)[:, sl],
                         VN[4 * gc:4 * (gc + 1)].rearrange("a p e -> p a e"),
                         kv_ld[sl])
                return kv_ld[sl].n

            def load_qt(c):
                prog.dma("sync", qtb[c % 2][:].rearrange("p (h t) -> p h t", h=H),
                         QT[:, :, 512 * c:512 * (c + 1)].rearrange("h p t -> p h t"),
                         qt_ld[c % 2])
                return qt_ld[c % 2].n

            kv_rdy = {0: load_kv(0), 1: load_kv(1)}
            qt_rdy = {0: load_qt(0)}

            for cc in range(4):
                gc = cc + 1
                if cc < 3:
                    if cc >= 1:
                        prog.wait("sync", avl, cc * 8)
                    kv_rdy[gc + 1] = load_kv(gc + 1)
                    qt_rdy[cc + 1] = load_qt(cc + 1)
                prog.wait("tensor", kv_ld[(gc - 1) % 3], kv_rdy[gc - 1])
                prog.wait("tensor", kv_ld[gc % 3], kv_rdy[gc])
                prog.wait("tensor", qt_ld[cc % 2], qt_rdy[cc])
                for h2 in range(H // 2):
                    gp2 = cc * 8 + h2
                    pp = h2 % 2
                    if gp2 >= 2:
                        prog.wait("scalar", avl, gp2 - 1)
                    jobs = [("own", jj, 128 * jj, 512, gc % 3, jj) for jj in range(4)] + \
                           [("prev", jj, 0, 128 * (jj + 1), (gc - 1) % 3, jj) for jj in range(4)]

                    def emit_sjob(idx, h2=h2, gp2=gp2, cc=cc, jobs=jobs, pp=pp):
                        gq = gp2 * 8 + idx
                        kind, jj, qlo, qhi, sl, jt = jobs[idx]
                        mend = (kind == "prev" and cc == 0)
                        if gq >= 2:
                            prog.wait("tensor", expd, gq - 1)
                        for a in range(2):
                            h = 2 * h2 + a
                            last = (a == 1) and not mend
                            prog.op("tensor", lambda e, h=h, jj=jj, qlo=qlo, qhi=qhi, sl=sl,
                                    gq=gq, a=a, cp=cc % 2, mend=mend: e.matmul(
                                s4[gq % 2][:, 512 * a + qlo:512 * a + qhi],
                                ktw[:, 1536 * h + 512 * sl + 128 * jj: 1536 * h + 512 * sl + 128 * (jj + 1)],
                                qtb[cp][:, 512 * h + qlo: 512 * h + qhi],
                                start=True, stop=not mend),
                                smm if last else None)
                            if mend:
                                prog.op("tensor", lambda e, qlo=qlo, qhi=qhi, gq=gq, a=a: e.matmul(
                                    s4[gq % 2][:, 512 * a + qlo:512 * a + qhi], m0_sb[0:1, :],
                                    ones_sb[0:1, 0:qhi - qlo],
                                    start=False, stop=True),
                                    smm if a == 1 else None)
                        prog.wait("scalar", smm, smm.n)
                        prog.op("scalar", lambda e, idx=idx, qlo=qlo, qhi=qhi, gq=gq, pp=pp: e.activation(
                            pts[pp][:].rearrange("p (s a w) -> p s a w", s=8, a=2)[:, idx, :, qlo:qhi],
                            s4[gq % 2][:].rearrange("p (a w) -> p a w", a=2)[:, :, qlo:qhi],
                            AF.Exp), expd)
                        prog.wait("vector", expd, expd.n)
                        tri = triO01 if kind == "own" else triP01
                        for a in range(2):
                            sl_ap = pts[pp][:, 1024 * idx + 512 * a + 128 * jj:
                                            1024 * idx + 512 * a + 128 * (jj + 1)]
                            prog.op("vector", lambda e, d=sl_ap, m=tri: e.tensor_tensor(
                                d, d, m[:], ALU.mult), mskd)

                    def emit_avl(idx, h2=h2, gp2=gp2, jobs=jobs, pp=pp):
                        kind, jj, qlo, qhi, sl, jt = jobs[idx]
                        prog.wait("tensor", mskd, 2 * (gp2 * 8 + idx + 1))
                        for a in range(2):
                            h = 2 * h2 + a
                            prog.op("tensor", lambda e, h=h, sl=sl, jt=jt, idx=idx, a=a,
                                    qlo=qlo, qhi=qhi, pp=pp: e.matmul(
                                ob2[:, 512 * a + qlo:512 * a + qhi],
                                vnw[:, 8192 * sl + 2048 * jt + 128 * h: 8192 * sl + 2048 * jt + 128 * (h + 1)],
                                pts[pp][:, 1024 * idx + 512 * a + qlo: 1024 * idx + 512 * a + qhi],
                                start=(idx == 0), stop=(idx == 7)))
                        for a in range(2):
                            prog.op("tensor", lambda e, idx=idx, a=a, qlo=qlo, qhi=qhi, pp=pp: e.matmul(
                                lb2[:, 512 * a + qlo:512 * a + qhi], ones_sb[:, 0:128],
                                pts[pp][:, 1024 * idx + 512 * a + qlo: 1024 * idx + 512 * a + qhi],
                                start=(idx == 0), stop=(idx == 7)),
                                avl if idx == 7 and a == 1 else None)

                    emit_sjob(0)
                    emit_sjob(1)
                    for idx in range(8):
                        if idx + 2 < 8:
                            emit_sjob(idx + 2)
                        if idx == 0 and gp2 >= 1:
                            prog.wait("tensor", oraw, gp2)
                            prog.wait("tensor", lnd, gp2)
                        emit_avl(idx)
                    prog.wait("vector", avl, avl.n)
                    prog.op("vector", lambda e, pp=pp: e.tensor_copy(
                        otraw[pp][:], ob2[:]), oraw)
                    prog.wait("scalar", avl, avl.n)
                    prog.op("scalar", lambda e: e.activation(
                        rltmp[:], lb2[:], AF.Ln), lnd)
                    if gp2 >= 2:
                        prog.wait("scalar", oev, gp2 - 1)
                    t = prog.op("scalar", lambda e, pp=pp: e.activation(
                        rlb[pp][:], rltmp[:], AF.Exp, scale=-1.0), rld)
                    prog.wait("vector", rld, t)
                    if cc >= 1 and h2 == 0:
                        prog.wait("vector", ot_st, 16 * cc)
                    prog.op("vector", lambda e, h2=h2, pp=pp: e.tensor_tensor(
                        otst[:, 1024 * h2:1024 * (h2 + 1)], otraw[pp][:], rlb[pp][:],
                        ALU.mult), oev)
                prog.wait("scalar", oev, oev.n)
                prog.dma("scalar", OT[:, :, 512 * cc:512 * (cc + 1)].rearrange("h p t -> p h t"),
                         otst[:].rearrange("p (h t) -> p h t", h=H), ot_st)
            prog.wait("scalar", ot_st, ot_st.n)

        prog.barrier(bar)
        if phases < 5:
            finish(prog, nc)
            return nc

        with ExitStack() as p5:
            E5 = p5.enter_context
            wout_sb = E5(nc.sbuf_tensor("wout_sb", [128, NI * E], BF16))
            otc = [E5(nc.sbuf_tensor(f"otc{p}", [128, H * 128], BF16)) for p in range(2)]
            gt = [E5(nc.sbuf_tensor(f"gt{p}", [128, E], BF16)) for p in range(2)]
            xt = [E5(nc.sbuf_tensor(f"xt{p}", [128, E], F32)) for p in range(2)]
            sfin = [E5(nc.sbuf_tensor(f"sfin{p}", [128, 512], F32)) for p in range(2)]
            obf = [E5(nc.sbuf_tensor(f"obf{p}", [128, E], F32)) for p in range(2)]
            s5_ps = [E5(nc.psum_tensor(f"s5_ps{p}", [128, 512], F32)) for p in range(2)]

            w5_ld = [SM(f"w5_ld{s}") for s in range(4)]
            t5_ld = [SM("t5_ld0"), SM("t5_ld1")]
            s5mm = SM("s5mm")
            findn = SM("findn")
            outst = [SM("outst0"), SM("outst1")]
            v5_self = SM("v5_self")

            for sd in g_st:
                prog.wait("sync", sd, sd.n)

            for s in range(4):
                prog.dma("gpsimd",
                         wout_sb[:].rearrange("p (i e) -> p i e", i=NI)[:, :, 512 * s:512 * (s + 1)],
                         woutT[:, 512 * s:512 * (s + 1)].rearrange("(i p) e -> p i e", p=128),
                         w5_ld[s])

            def load_tt(tt):
                tp = tt % 2
                prog.wait("sync", ot_st, 16 * (tt // 4 + 1))
                prog.dma("sync", otc[tp][:].rearrange("p (h t) -> p h t", h=H),
                         OT[:, :, 128 * tt:128 * (tt + 1)].rearrange("h p t -> p h t"),
                         t5_ld[tp])
                prog.dma("sync", gt[tp][:], G[tt], t5_ld[tp])
                prog.dma("sync", xt[tp][:],
                         xh[512 + 128 * tt: 512 + 128 * (tt + 1), :], t5_ld[tp])
                return t5_ld[tp].n

            rdy = {0: load_tt(0)}
            mmq = 0
            for tt in range(NTC):
                tp = tt % 2
                if tt < NTC - 1:
                    if tt >= 1:
                        prog.wait("sync", outst[(tt + 1) % 2], 16 * ((tt + 1) // 2))
                    rdy[tt + 1] = load_tt(tt + 1)
                prog.wait("tensor", t5_ld[tp], rdy[tt])
                if tt >= 2:
                    prog.wait("vector", outst[tp], 16 * (tt // 2))
                for og in range(4):
                    gp = mmq % 2
                    if tt == 0:
                        prog.wait("tensor", w5_ld[og], 16)
                    if mmq >= 2:
                        prog.wait("tensor", findn, mmq - 1)
                    for hh in range(H):
                        prog.op("tensor", lambda e, tp=tp, og=og, hh=hh, gp=gp: e.matmul(
                            s5_ps[gp][:],
                            otc[tp][:, 128 * hh:128 * (hh + 1)],
                            wout_sb[:, E * hh + 512 * og: E * hh + 512 * (og + 1)],
                            start=(hh == 0), stop=(hh == H - 1)),
                            s5mm if hh == H - 1 else None)
                    mmq += 1
                    prog.wait("vector", s5mm, mmq)
                    t = prog.op("vector", lambda e, tp=tp, og=og, gp=gp: e.tensor_tensor(
                        sfin[gp][:], s5_ps[gp][:], gt[tp][:, 512 * og:512 * (og + 1)],
                        ALU.mult), v5_self)
                    prog.wait("vector", v5_self, t)
                    prog.op("vector", lambda e, tp=tp, og=og, gp=gp: e.tensor_tensor(
                        obf[tp][:, 512 * og:512 * (og + 1)], sfin[gp][:],
                        xt[tp][:, 512 * og:512 * (og + 1)], ALU.add), findn)
                prog.wait("sync", findn, findn.n)
                prog.dma("sync", out[128 * tt:128 * (tt + 1), :], obf[tp][:], outst[tp])
            prog.wait("sync", outst[0], outst[0].n)
            prog.wait("sync", outst[1], outst[1].n)

        finish(prog, nc)
    return nc


def finish(prog, nc):
    with nc.Block() as block:
        for name in ENGS:
            deco = getattr(block, name)

            def runner(eng, name=name):
                for fn in prog.streams[name]:
                    fn(eng)

            deco(runner)



def _consts():
    ident = np.eye(128, dtype=ml_dtypes.bfloat16)
    ones = np.ones((128, 512), dtype=ml_dtypes.bfloat16)
    p = np.arange(128)[:, None]
    f = np.arange(128)[None, :]
    triP = np.where(p >= f, 1.0, 0.0).astype(ml_dtypes.bfloat16)
    triO = np.where(p <= f, 1.0, 0.0).astype(ml_dtypes.bfloat16)
    return ident, ones, triP, triO


def make_in_maps(x, wqkv, wout, wffn, bias):
    ident, ones, triP, triO = _consts()
    wqkvT = np.ascontiguousarray(np.asarray(wqkv, dtype=np.float32).T)
    woutT = np.ascontiguousarray(np.asarray(wout, dtype=np.float32).T)
    wffnT = np.ascontiguousarray(np.asarray(wffn, dtype=np.float32).T)
    bias1 = np.ascontiguousarray(np.asarray(bias, dtype=np.float32).reshape(1, E))
    in_maps = []
    for core in range(8):
        b, j = core // 4, core % 4
        lo = 2048 * j
        if j == 0:
            halo = np.zeros((C, E), dtype=np.float32)
            m0 = np.full((1, 128), NEG, dtype=ml_dtypes.bfloat16)
        else:
            halo = np.asarray(x[b, lo - C:lo], dtype=np.float32)
            m0 = np.zeros((1, 128), dtype=ml_dtypes.bfloat16)
        xhc = np.ascontiguousarray(
            np.concatenate([halo, np.asarray(x[b, lo:lo + TC], dtype=np.float32)], 0))
        in_maps.append(dict(
            xh=xhc, wqkvT=wqkvT, woutT=woutT, wffnT=wffnT, bias1=bias1,
            identN=ident, ones_in=ones, triP_in=triP, triO_in=triO, m0row=m0))
    return in_maps


def kernel(L0_attn_x, L0_attn_w_qkv, L0_attn_w_out, L0_ffn_weight, L0_ffn_bias):
    x = np.asarray(L0_attn_x)
    in_maps = make_in_maps(x, L0_attn_w_qkv, L0_attn_w_out, L0_ffn_weight, L0_ffn_bias)
    nc = build()
    res = run_bass_kernel_spmd(nc, in_maps, core_ids=list(range(8)))
    outa = np.empty((2, 8192, 2048), dtype=np.float32)
    for core in range(8):
        b, j = core // 4, core % 4
        outa[b, 2048 * j:2048 * (j + 1)] = res.results[core]["out"]
    return outa


# version 05
# speedup vs baseline: 1.2596x; 1.0019x over previous
import math
import numpy as np
import ml_dtypes
from contextlib import ExitStack

import concourse.bass as bass
import concourse.mybir as mybir
from concourse.bass_utils import run_bass_kernel_spmd

F32 = mybir.dt.float32
BF16 = mybir.dt.bfloat16
AF = mybir.ActivationFunctionType
ALU = mybir.AluOpType

NEG = -1.0e30
E = 2048
NI = 16
H = 16
C = 512
TC = 2048
TA = 2560
NTC = 16
NTA = 20
SCALE = 1.0 / math.sqrt(128.0)
ENGS = ("sync", "tensor", "vector", "scalar", "gpsimd")


class Sem:
    def __init__(self, nc, stack, name):
        self.h = stack.enter_context(nc.semaphore(name))
        self.n = 0


class Prog:

    def __init__(self):
        self.streams = {e: [] for e in ENGS}

    def emit(self, eng, fn):
        self.streams[eng].append(fn)

    def wait(self, eng, sem, tgt):
        if tgt > 0:
            self.emit(eng, lambda e, s=sem.h, t=tgt: e.wait_ge(s, t))

    def op(self, eng, builder, sem=None, k=1):
        if sem is None:
            self.emit(eng, builder)
            return None
        sem.n += k
        self.emit(eng, lambda e, b=builder, s=sem.h, kk=k: b(e).then_inc(s, kk))
        return sem.n

    def dma(self, eng, out_ap, in_ap, sem):
        sem.n += 16
        self.emit(eng, lambda e, o=out_ap, i=in_ap, s=sem.h: e.dma_start(o, i).then_inc(s, 16))
        return sem.n

    def barrier(self, bar):
        for e in ENGS:
            self.emit(e, lambda en, s=bar.h: en.nop().then_inc(s, 1))
        bar.n += len(ENGS)
        tgt = bar.n
        for e in ENGS:
            self.emit(e, lambda en, s=bar.h, t=tgt: en.wait_ge(s, t))


def build(dbg=frozenset(), phases=5):
    nc = bass.Bass("TRN2", target_bir_lowering=False)

    def dram(name, shape, dtype):
        kind = "ExternalOutput" if name in dbg else "Internal"
        return nc.dram_tensor(name, shape, dtype, kind=kind)

    xh = nc.dram_tensor("xh", [TA, E], F32, kind="ExternalInput").ap()
    wqkvT = nc.dram_tensor("wqkvT", [E, 3 * E], F32, kind="ExternalInput").ap()
    woutT = nc.dram_tensor("woutT", [E, E], F32, kind="ExternalInput").ap()
    wffnT = nc.dram_tensor("wffnT", [E, E], F32, kind="ExternalInput").ap()
    bias1 = nc.dram_tensor("bias1", [1, E], F32, kind="ExternalInput").ap()
    identN = nc.dram_tensor("identN", [128, 128], BF16, kind="ExternalInput").ap()
    ones_in = nc.dram_tensor("ones_in", [128, 512], BF16, kind="ExternalInput").ap()
    triP_in = nc.dram_tensor("triP_in", [128, 128], BF16, kind="ExternalInput").ap()
    triO_in = nc.dram_tensor("triO_in", [128, 128], BF16, kind="ExternalInput").ap()
    m0row = nc.dram_tensor("m0row", [1, 128], BF16, kind="ExternalInput").ap()

    out = nc.dram_tensor("out", [TC, E], F32, kind="ExternalOutput").ap()

    QT = dram("QT", [H, 128, TC], BF16).ap()
    KT = dram("KT", [H, 128, TA], BF16).ap()
    VN = dram("VN", [NTA, 128, E], BF16).ap()
    OT = dram("OT", [H, 128, TC], BF16).ap()
    G = dram("G", [NTC, 128, E], BF16).ap()
    if "XNT" in dbg:
        XNTd = nc.dram_tensor("XNT", [NI, 128, TA], BF16, kind="ExternalOutput").ap()

    prog = Prog()

    with ExitStack() as outer:
        EO = outer.enter_context
        ident = EO(nc.sbuf_tensor("ident", [128, 128], BF16))
        ones_sb = EO(nc.sbuf_tensor("ones_sb", [128, 512], BF16))
        triP01 = EO(nc.sbuf_tensor("triP01", [128, 128], BF16))
        triO01 = EO(nc.sbuf_tensor("triO01", [128, 128], BF16))
        m0_sb = EO(nc.sbuf_tensor("m0_sb", [1, 128], BF16))
        bias_sb = EO(nc.sbuf_tensor("bias_sb", [1, E], BF16))

        SM = lambda name: Sem(nc, outer, name)
        bar = SM("bar")
        const_ld = SM("const_ld")
        CONST_TOT = 6 * 16

        prog.dma("sync", ident[:], identN, const_ld)
        prog.dma("sync", ones_sb[:], ones_in, const_ld)
        prog.dma("sync", triP01[:], triP_in, const_ld)
        prog.dma("sync", triO01[:], triO_in, const_ld)
        prog.dma("sync", m0_sb[:], m0row, const_ld)
        prog.dma("gpsimd", bias_sb[:], bias1, const_ld)

        abc = EO(nc.sbuf_tensor("abc", [128, E], BF16))
        s123 = ExitStack()
        xnt = s123.enter_context(nc.sbuf_tensor("xnt", [128, NI * TA], BF16))
        xnt3 = xnt[:].rearrange("p (i t) -> p i t", i=NI)

        with ExitStack() as p1:
            E1 = p1.enter_context
            xbuf = [E1(nc.sbuf_tensor(f"xbuf{p}", [128, 2 * E], F32)) for p in range(2)]
            xnat = [E1(nc.sbuf_tensor(f"xnat{p}", [128, 2 * E], BF16)) for p in range(2)]
            stats = E1(nc.sbuf_tensor("stats", [128, 24], F32))
            mv = [E1(nc.sbuf_tensor(f"mv{p}", [128, 4], F32)) for p in range(2)]
            veps = [E1(nc.sbuf_tensor(f"veps{p}", [128, 2], F32)) for p in range(2)]
            sqv = [E1(nc.sbuf_tensor(f"sqv{p}", [128, 2], F32)) for p in range(2)]
            rstd = [E1(nc.sbuf_tensor(f"rstd{p}", [128, 2], F32)) for p in range(2)]
            nmr = [E1(nc.sbuf_tensor(f"nmr{p}", [128, 2], F32)) for p in range(2)]
            wfs = [E1(nc.sbuf_tensor(f"wfs{p}", [128, E], BF16)) for p in range(2)]
            abst = [E1(nc.sbuf_tensor(f"abst{p}", [128, E], F32)) for p in range(2)]
            accf = E1(nc.sbuf_tensor("accf", [128, E], F32))
            accbf = E1(nc.sbuf_tensor("accbf", [128, E], BF16))
            arow = E1(nc.sbuf_tensor("arow", [1, E], BF16))
            tp_ps = [E1(nc.psum_tensor(f"tp_ps{p}", [128, 512], F32)) for p in range(2)]
            al_ps = E1(nc.psum_tensor("al_ps", [128, 512], F32))

            x_ld = [SM("x_ld0"), SM("x_ld1")]
            v_self = SM("v_self")
            s_self = SM("s_self")
            norm_done = SM("norm_done")
            tp_done = SM("tp_done")
            tp_ev = SM("tp_ev")
            wf_ld = [SM(f"wf_ld{p}") for p in range(2)]
            abs_dn = SM("abs_dn")
            acA = SM("acA")
            alev = SM("alev")
            abev = SM("abev")

            def emit_alpha_tile(i):
                ip = i % 2
                if i >= 2:
                    prog.wait("gpsimd", abs_dn, i - 1)
                prog.dma("gpsimd", wfs[ip][:], wffnT[128 * i:128 * (i + 1), :], wf_ld[ip])
                prog.wait("scalar", wf_ld[ip], 16 * (i // 2 + 1))
                if i >= 2:
                    prog.wait("scalar", acA, i - 1)
                prog.op("scalar", lambda e, ip=ip: e.activation(
                    abst[ip][:], wfs[ip][:], AF.Abs), abs_dn)
                prog.wait("vector", abs_dn, i + 1)
                if i == 0:
                    prog.op("vector", lambda e: e.tensor_copy(accf[:], abst[0][:]), acA)
                else:
                    prog.op("vector", lambda e, ip=ip: e.tensor_tensor(
                        accf[:], accf[:], abst[ip][:], ALU.add), acA)

            for c in range(10):
                p = c % 2
                if c >= 2:
                    prog.wait("sync", norm_done, c - 1)
                ld_tgt = prog.dma(
                    "sync", xbuf[p][:].rearrange("p (a e) -> p a e", a=2),
                    xh[256 * c:256 * (c + 1), :].rearrange("(a p) e -> p a e", p=128),
                    x_ld[p])
                if c < 8:
                    emit_alpha_tile(2 * c)
                    emit_alpha_tile(2 * c + 1)
                prog.wait("vector", x_ld[p], ld_tgt)
                for a in range(2):
                    for g4 in range(4):
                        t = prog.op("vector", lambda e, a=a, g=g4, p=p: e.bn_stats(
                            stats[:, 6 * g:6 * (g + 1)],
                            xbuf[p][:, 2048 * a + 512 * g: 2048 * a + 512 * (g + 1)]), v_self)
                    prog.wait("vector", v_self, t)
                    t = prog.op("vector", lambda e, a=a, p=p: e.bn_aggr(
                        mv[p][:, 2 * a:2 * a + 2], stats[:]), v_self)
                    prog.wait("vector", v_self, t)
                t = prog.op("vector", lambda e, p=p: e.tensor_scalar(
                    veps[p][:], mv[p][:, 1:4:2], 1e-5, None, ALU.add), v_self)
                prog.wait("scalar", v_self, t)
                t = prog.op("scalar", lambda e, p=p: e.activation(
                    sqv[p][:], veps[p][:], AF.Sqrt), s_self)
                prog.wait("vector", s_self, t)
                t = prog.op("vector", lambda e, p=p: e.reciprocal(rstd[p][:], sqv[p][:]), v_self)
                prog.wait("vector", v_self, t)
                tn = prog.op("vector", lambda e, p=p: e.scalar_tensor_tensor(
                    nmr[p][:], mv[p][:, 0:4:2], -1.0, rstd[p][:], ALU.mult, ALU.mult), v_self)
                prog.wait("scalar", v_self, tn)
                if c >= 2:
                    prog.wait("scalar", tp_done, 8 * (c - 1))
                for a in range(2):
                    sem = norm_done if a == 1 else s_self
                    prog.op("scalar", lambda e, a=a, p=p: e.activation(
                        xnat[p][:, 2048 * a:2048 * (a + 1)],
                        xbuf[p][:, 2048 * a:2048 * (a + 1)], AF.Identity,
                        bias=nmr[p][:, a:a + 1], scale=rstd[p][:, a:a + 1]), sem)
                prog.wait("tensor", const_ld, CONST_TOT)
                prog.wait("tensor", norm_done, c + 1)
                for a in range(2):
                    for ig in range(4):
                        gpar = tp_done.n % 2
                        if tp_done.n >= 2:
                            prog.wait("tensor", tp_ev, tp_done.n - 1)
                        for kk in range(4):
                            i = 4 * ig + kk
                            prog.op("tensor", lambda e, a=a, i=i, kk=kk, p=p, gp=gpar: e.matmul(
                                tp_ps[gp][:, 128 * kk:128 * (kk + 1)],
                                xnat[p][:, 2048 * a + 128 * i: 2048 * a + 128 * (i + 1)],
                                ident[:], start=True, stop=True),
                                tp_done if kk == 3 else None)
                        tgt = tp_done.n
                        ev_eng = "vector" if ig % 4 == 0 else "scalar"
                        prog.wait(ev_eng, tp_done, tgt)
                        dst = xnt3[:, 4 * ig:4 * ig + 4,
                                   256 * c + 128 * a: 256 * c + 128 * (a + 1)]
                        if ig % 4 == 0:
                            prog.op("vector", lambda e, d=dst, gp=gpar: e.tensor_copy(
                                d, tp_ps[gp][:].rearrange("p (i t) -> p i t", i=4)), tp_ev)
                        else:
                            prog.op("scalar", lambda e, d=dst, gp=gpar: e.copy(
                                d, tp_ps[gp][:].rearrange("p (i t) -> p i t", i=4)), tp_ev)
            almm = SM("almm")
            abmm = SM("abmm")
            prog.wait("vector", acA, 16)
            tcast = prog.op("vector", lambda e: e.tensor_copy(accbf[:], accf[:]), v_self)
            prog.wait("tensor", v_self, tcast)
            for g in range(4):
                if g >= 1:
                    prog.wait("tensor", alev, g)
                    prog.wait("tensor", abev, g)
                t = prog.op("tensor", lambda e, g=g: e.matmul(
                    al_ps[0:1, :], ones_sb[:, 0:1], accbf[:, 512 * g:512 * (g + 1)],
                    start=True, stop=True), almm)
                prog.wait("scalar", almm, t)
                t = prog.op("scalar", lambda e, g=g: e.activation(
                    arow[0:1, 512 * g:512 * (g + 1)], al_ps[0:1, :], AF.Copy,
                    scale=1.0 / E), alev)
                prog.wait("tensor", alev, t)
                t = prog.op("tensor", lambda e, g=g: e.matmul(
                    tp_ps[0][:], ones_sb[0:1, 0:128], arow[0:1, 512 * g:512 * (g + 1)],
                    start=True, stop=True), abmm)
                prog.wait("vector", abmm, t)
                prog.op("vector", lambda e, g=g: e.tensor_copy(
                    abc[:, 512 * g:512 * (g + 1)], tp_ps[0][:]), abev)
            if "XNT" in dbg:
                prog.wait("sync", tp_ev, tp_ev.n)
                dmp = SM("dmp")
                for hf in range(2):
                    prog.dma("sync",
                             XNTd[8 * hf:8 * (hf + 1)].rearrange("i p t -> p i t"),
                             xnt3[:, 8 * hf:8 * (hf + 1), :], dmp)
                prog.wait("sync", dmp, dmp.n)

        prog.barrier(bar)
        if phases < 2:
            s123.close()
            finish(prog, nc)
            return nc

        s23 = ExitStack()
        wsl = [s23.enter_context(nc.sbuf_tensor(f"wsl{p}", [128, NI * 512], BF16))
               for p in range(2)]
        slab_ld = [SM("slab_ld0"), SM("slab_ld1")]
        slab_free = {}
        slab_tgt = {}

        def load_slab(s):
            if s >= 2:
                sem, cnt = slab_free[s - 2]
                prog.wait("gpsimd", sem, cnt)
            slab_tgt[s] = prog.dma(
                "gpsimd", wsl[s % 2][:],
                wqkvT[:, 512 * s:512 * (s + 1)].rearrange("(i p) e -> p i e", p=128),
                slab_ld[s % 2])

        with ExitStack() as p2:
            E2 = p2.enter_context
            wffn = E2(nc.sbuf_tensor("wffn", [128, NI * E], BF16))
            gst = [E2(nc.sbuf_tensor(f"gst{p}", [128, 2 * E], BF16)) for p in range(2)]
            g_ps = [E2(nc.psum_tensor(f"g_ps{p}", [128, 512], F32)) for p in range(2)]

            wff_ld = [SM(f"wff_ld{r}") for r in range(4)]
            bin_dn = SM("bin_dn")
            gmm = SM("gmm")
            silu_dn = SM("silu_dn")
            g_st = [SM("g_st0"), SM("g_st1")]
            v2_self = SM("v2_self")

            wffn3 = wffn[:].rearrange("p (i e) -> p i e", i=NI)
            for r in range(4):
                prog.dma("gpsimd", wffn3[:, 4 * r:4 * (r + 1), :],
                         wffnT[512 * r:512 * (r + 1), :].rearrange("(i p) e -> p i e", p=128),
                         wff_ld[r])
            load_slab(0)
            load_slab(1)
            prog.wait("vector", abev, 4)
            for i in range(NI):
                if i % 4 == 0:
                    prog.wait("vector", wff_ld[i // 4], 16)
                t = prog.op("vector", lambda e, i=i: e.tensor_scalar(
                    wffn[:, E * i:E * (i + 1)], wffn[:, E * i:E * (i + 1)],
                    0.0, 2.0, ALU.is_ge, ALU.mult), v2_self)
                prog.wait("vector", v2_self, t)
                prog.op("vector", lambda e, i=i: e.scalar_tensor_tensor(
                    wffn[:, E * i:E * (i + 1)], wffn[:, E * i:E * (i + 1)],
                    -1.0, abc[:], ALU.add, ALU.mult), bin_dn)
            prog.wait("tensor", bin_dn, NI)
            prog.wait("tensor", const_ld, CONST_TOT)
            mmq = 0
            for tt in range(NTC):
                pr = tt // 2
                bufp = pr % 2
                for og in range(4):
                    gp = mmq % 2
                    if mmq >= 2:
                        prog.wait("tensor", silu_dn, mmq - 1)
                    for i in range(NI):
                        prog.op("tensor", lambda e, tt=tt, og=og, i=i, gp=gp: e.matmul(
                            g_ps[gp][:],
                            xnt[:, TA * i + 512 + 128 * tt: TA * i + 512 + 128 * (tt + 1)],
                            wffn[:, E * i + 512 * og: E * i + 512 * (og + 1)],
                            start=(i == 0), stop=False))
                    t = prog.op("tensor", lambda e, og=og, gp=gp: e.matmul(
                        g_ps[gp][:], ones_sb[0:1, 0:128], bias_sb[0:1, 512 * og:512 * (og + 1)],
                        start=False, stop=True), gmm)
                    prog.wait("scalar", gmm, t)
                    if tt % 2 == 0 and og == 0 and pr >= 2:
                        prog.wait("scalar", g_st[bufp], 16 * (pr // 2))
                    prog.op("scalar", lambda e, tt=tt, og=og, bp=bufp, gp=gp: e.activation(
                        gst[bp][:, E * (tt % 2) + 512 * og: E * (tt % 2) + 512 * (og + 1)],
                        g_ps[gp][:], AF.Silu), silu_dn)
                    mmq += 1
                if tt % 2 == 1:
                    prog.wait("scalar", silu_dn, mmq)
                    prog.dma("scalar",
                             G[tt - 1:tt + 1].rearrange("a p e -> p a e"),
                             gst[bufp][:].rearrange("p (a e) -> p a e", a=2), g_st[bufp])
            prog.wait("scalar", g_st[0], g_st[0].n)
            prog.wait("scalar", g_st[1], g_st[1].n)

        prog.barrier(bar)
        if phases < 3:
            s23.close()
            s123.close()
            finish(prog, nc)
            return nc

        with ExitStack() as p3:
            E3 = p3.enter_context
            qkst = [E3(nc.sbuf_tensor(f"qkst{p}", [128, TA], BF16)) for p in range(2)]
            vstb = [E3(nc.sbuf_tensor(f"vstb{p}", [128, 4 * 512], BF16)) for p in range(2)]
            qk_ps = [E3(nc.psum_tensor(f"qk_ps{p}", [128, 512], F32)) for p in range(2)]

            qkmm = SM("qkmm")
            qkev = SM("qkev")
            qkst_dn = [SM("qkst_dn0"), SM("qkst_dn1")]
            vmm = SM("vmm")
            vev = SM("vev")
            vst_dn = [SM("vst_dn0"), SM("vst_dn1")]

            for s in range(8):
                sp = s % 2
                is_q = s < 4
                ntg = 4 if is_q else 5
                toff = 512 if is_q else 0
                ev_eng = "scalar" if is_q else "vector"
                prog.wait("tensor", slab_ld[sp], slab_tgt[s])
                for el in range(4):
                    h = (4 * s + el) % 16
                    eg = 4 * s + el
                    stp = eg % 2
                    if eg >= 2:
                        prog.wait(ev_eng, qkst_dn[stp], 16 * (eg // 2))
                    for tg in range(ntg):
                        gp = qkmm.n % 2
                        if qkmm.n >= 2:
                            prog.wait("tensor", qkev, qkmm.n - 1)
                        for i in range(NI):
                            prog.op("tensor", lambda e, sp=sp, el=el, tg=tg, i=i, gp=gp, to=toff: e.matmul(
                                qk_ps[gp][:],
                                wsl[sp][:, 512 * i + 128 * el: 512 * i + 128 * (el + 1)],
                                xnt[:, TA * i + to + 512 * tg: TA * i + to + 512 * (tg + 1)],
                                start=(i == 0), stop=(i == NI - 1)),
                                qkmm if i == NI - 1 else None)
                        prog.wait(ev_eng, qkmm, qkmm.n)
                        if is_q:
                            prog.op("scalar", lambda e, tg=tg, stp=stp, gp=gp: e.activation(
                                qkst[stp][:, 512 * tg:512 * (tg + 1)], qk_ps[gp][:],
                                AF.Copy, scale=SCALE), qkev)
                        else:
                            prog.op("vector", lambda e, tg=tg, stp=stp, gp=gp: e.tensor_copy(
                                qkst[stp][:, 512 * tg:512 * (tg + 1)], qk_ps[gp][:]), qkev)
                    prog.wait("sync", qkev, qkev.n)
                    if is_q:
                        prog.dma("sync", QT[h], qkst[stp][:, 0:TC], qkst_dn[stp])
                    else:
                        prog.dma("sync", KT[h], qkst[stp][:, 0:TA], qkst_dn[stp])
                slab_free[s] = (qkmm, qkmm.n)
                if s + 2 < 12:
                    load_slab(s + 2)
            for s in range(8, 12):
                sp = s % 2
                g = s - 8
                prog.wait("tensor", slab_ld[sp], slab_tgt[s])
                for tq in range(5):
                    qg = 5 * g + tq
                    vp = qg % 2
                    if qg >= 2:
                        prog.wait("vector", vst_dn[vp], 16 * (qg // 2))
                    for a in range(4):
                        tau = 4 * tq + a
                        gp = vmm.n % 2
                        if vmm.n >= 2:
                            prog.wait("tensor", vev, vmm.n - 1)
                        for i in range(NI):
                            prog.op("tensor", lambda e, sp=sp, tau=tau, i=i, gp=gp: e.matmul(
                                qk_ps[gp][:],
                                xnt[:, TA * i + 128 * tau: TA * i + 128 * (tau + 1)],
                                wsl[sp][:, 512 * i:512 * (i + 1)],
                                start=(i == 0), stop=(i == NI - 1)),
                                vmm if i == NI - 1 else None)
                        prog.wait("vector", vmm, vmm.n)
                        prog.op("vector", lambda e, a=a, vp=vp, gp=gp: e.tensor_copy(
                            vstb[vp][:, 512 * a:512 * (a + 1)], qk_ps[gp][:]), vev)
                    prog.wait("sync", vev, vev.n)
                    prog.dma("sync",
                             VN[4 * tq:4 * tq + 4, :, 512 * g:512 * (g + 1)].rearrange(
                                 "a p e -> p a e"),
                             vstb[vp][:].rearrange("p (a e) -> p a e", a=4), vst_dn[vp])
                slab_free[s] = (vmm, vmm.n)
                if s + 2 < 12:
                    load_slab(s + 2)
            for sd in qkst_dn + vst_dn:
                prog.wait("sync", sd, sd.n)

        s23.close()
        s123.close()
        prog.barrier(bar)
        if phases < 4:
            finish(prog, nc)
            return nc

        with ExitStack() as p4:
            E4 = p4.enter_context
            qtb = [E4(nc.sbuf_tensor(f"qtb{p}", [128, H * 512], BF16)) for p in range(2)]
            ktw = E4(nc.sbuf_tensor("ktw", [128, H * 3 * 512], BF16))
            vnw = E4(nc.sbuf_tensor("vnw", [128, 3 * 4 * E], BF16))
            pts = [E4(nc.sbuf_tensor(f"pts{p}", [128, 8 * 1024], BF16)) for p in range(2)]
            rlb = [E4(nc.sbuf_tensor(f"rlb{p}", [128, 1024], F32)) for p in range(2)]
            rltmp = E4(nc.sbuf_tensor("rltmp", [128, 1024], F32))
            otst = E4(nc.sbuf_tensor("otst", [128, H * 512], BF16))
            otraw = [E4(nc.sbuf_tensor(f"otraw{p}", [128, 1024], BF16)) for p in range(2)]
            s4 = [E4(nc.psum_tensor(f"s4_{p}", [128, 1024], F32)) for p in range(2)]
            ob2 = E4(nc.psum_tensor("ob2", [128, 1024], F32))
            lb2 = E4(nc.psum_tensor("lb2", [128, 1024], F32))

            kv_ld = [SM("kv_ld0"), SM("kv_ld1"), SM("kv_ld2")]
            qt_ld = [SM("qt_ld0"), SM("qt_ld1")]
            smm = SM("smm")
            expd = SM("expd")
            mskd = SM("mskd")
            avl = SM("avl")
            lnd = SM("lnd")
            rld = SM("rld")
            oraw = SM("oraw")
            oev = SM("oev")
            ot_st = SM("ot_st")

            for sd in qkst_dn + vst_dn:
                prog.wait("sync", sd, sd.n)

            def load_kv(gc):
                sl = gc % 3
                prog.dma("sync",
                         ktw[:].rearrange("p (h r t) -> p h r t", h=H, r=3)[:, :, sl, :],
                         KT[:, :, 512 * gc:512 * (gc + 1)].rearrange("h p t -> p h t"),
                         kv_ld[sl])
                prog.dma("sync",
                         vnw[:].rearrange("p (r a e) -> p r a e", r=3, a=4)[:, sl],
                         VN[4 * gc:4 * (gc + 1)].rearrange("a p e -> p a e"),
                         kv_ld[sl])
                return kv_ld[sl].n

            def load_qt(c):
                prog.dma("sync", qtb[c % 2][:].rearrange("p (h t) -> p h t", h=H),
                         QT[:, :, 512 * c:512 * (c + 1)].rearrange("h p t -> p h t"),
                         qt_ld[c % 2])
                return qt_ld[c % 2].n

            kv_rdy = {0: load_kv(0), 1: load_kv(1)}
            qt_rdy = {0: load_qt(0)}

            for cc in range(4):
                gc = cc + 1
                if cc < 3:
                    if cc >= 1:
                        prog.wait("sync", avl, cc * 8)
                    kv_rdy[gc + 1] = load_kv(gc + 1)
                    qt_rdy[cc + 1] = load_qt(cc + 1)
                prog.wait("tensor", kv_ld[(gc - 1) % 3], kv_rdy[gc - 1])
                prog.wait("tensor", kv_ld[gc % 3], kv_rdy[gc])
                prog.wait("tensor", qt_ld[cc % 2], qt_rdy[cc])
                for h2 in range(H // 2):
                    gp2 = cc * 8 + h2
                    pp = h2 % 2
                    if gp2 >= 2:
                        prog.wait("scalar", avl, gp2 - 1)
                    jobs = [("own", jj, 128 * jj, 512, gc % 3, jj) for jj in range(4)] + \
                           [("prev", jj, 0, 128 * (jj + 1), (gc - 1) % 3, jj) for jj in range(4)]

                    def emit_sjob(idx, h2=h2, gp2=gp2, cc=cc, jobs=jobs, pp=pp):
                        gq = gp2 * 8 + idx
                        kind, jj, qlo, qhi, sl, jt = jobs[idx]
                        mend = (kind == "prev" and cc == 0)
                        if gq >= 2:
                            prog.wait("tensor", expd, gq - 1)
                        for a in range(2):
                            h = 2 * h2 + a
                            last = (a == 1) and not mend
                            prog.op("tensor", lambda e, h=h, jj=jj, qlo=qlo, qhi=qhi, sl=sl,
                                    gq=gq, a=a, cp=cc % 2, mend=mend: e.matmul(
                                s4[gq % 2][:, 512 * a + qlo:512 * a + qhi],
                                ktw[:, 1536 * h + 512 * sl + 128 * jj: 1536 * h + 512 * sl + 128 * (jj + 1)],
                                qtb[cp][:, 512 * h + qlo: 512 * h + qhi],
                                start=True, stop=not mend),
                                smm if last else None)
                            if mend:
                                prog.op("tensor", lambda e, qlo=qlo, qhi=qhi, gq=gq, a=a: e.matmul(
                                    s4[gq % 2][:, 512 * a + qlo:512 * a + qhi], m0_sb[0:1, :],
                                    ones_sb[0:1, 0:qhi - qlo],
                                    start=False, stop=True),
                                    smm if a == 1 else None)
                        prog.wait("scalar", smm, smm.n)
                        prog.op("scalar", lambda e, idx=idx, qlo=qlo, qhi=qhi, gq=gq, pp=pp: e.activation(
                            pts[pp][:].rearrange("p (s a w) -> p s a w", s=8, a=2)[:, idx, :, qlo:qhi],
                            s4[gq % 2][:].rearrange("p (a w) -> p a w", a=2)[:, :, qlo:qhi],
                            AF.Exp), expd)
                        prog.wait("vector", expd, expd.n)
                        tri = triO01 if kind == "own" else triP01
                        for a in range(2):
                            sl_ap = pts[pp][:, 1024 * idx + 512 * a + 128 * jj:
                                            1024 * idx + 512 * a + 128 * (jj + 1)]
                            prog.op("vector", lambda e, d=sl_ap, m=tri: e.tensor_tensor(
                                d, d, m[:], ALU.mult), mskd)

                    def emit_avl(idx, h2=h2, gp2=gp2, jobs=jobs, pp=pp):
                        kind, jj, qlo, qhi, sl, jt = jobs[idx]
                        prog.wait("tensor", mskd, 2 * (gp2 * 8 + idx + 1))
                        for a in range(2):
                            h = 2 * h2 + a
                            prog.op("tensor", lambda e, h=h, sl=sl, jt=jt, idx=idx, a=a,
                                    qlo=qlo, qhi=qhi, pp=pp: e.matmul(
                                ob2[:, 512 * a + qlo:512 * a + qhi],
                                vnw[:, 8192 * sl + 2048 * jt + 128 * h: 8192 * sl + 2048 * jt + 128 * (h + 1)],
                                pts[pp][:, 1024 * idx + 512 * a + qlo: 1024 * idx + 512 * a + qhi],
                                start=(idx == 0), stop=(idx == 7)))
                        for a in range(2):
                            prog.op("tensor", lambda e, idx=idx, a=a, qlo=qlo, qhi=qhi, pp=pp: e.matmul(
                                lb2[:, 512 * a + qlo:512 * a + qhi], ones_sb[:, 0:128],
                                pts[pp][:, 1024 * idx + 512 * a + qlo: 1024 * idx + 512 * a + qhi],
                                start=(idx == 0), stop=(idx == 7)),
                                avl if idx == 7 and a == 1 else None)

                    emit_sjob(0)
                    emit_sjob(1)
                    for idx in range(8):
                        if idx + 2 < 8:
                            emit_sjob(idx + 2)
                        if idx == 0 and gp2 >= 1:
                            prog.wait("tensor", oraw, gp2)
                            prog.wait("tensor", lnd, gp2)
                        emit_avl(idx)
                    prog.wait("vector", avl, avl.n)
                    prog.op("vector", lambda e, pp=pp: e.tensor_copy(
                        otraw[pp][:], ob2[:]), oraw)
                    prog.wait("scalar", avl, avl.n)
                    prog.op("scalar", lambda e: e.activation(
                        rltmp[:], lb2[:], AF.Ln), lnd)
                    if gp2 >= 2:
                        prog.wait("scalar", oev, gp2 - 1)
                    t = prog.op("scalar", lambda e, pp=pp: e.activation(
                        rlb[pp][:], rltmp[:], AF.Exp, scale=-1.0), rld)
                    prog.wait("vector", rld, t)
                    if cc >= 1 and h2 == 0:
                        prog.wait("vector", ot_st, 16 * cc)
                    prog.op("vector", lambda e, h2=h2, pp=pp: e.tensor_tensor(
                        otst[:, 1024 * h2:1024 * (h2 + 1)], otraw[pp][:], rlb[pp][:],
                        ALU.mult), oev)
                prog.wait("scalar", oev, oev.n)
                prog.dma("scalar", OT[:, :, 512 * cc:512 * (cc + 1)].rearrange("h p t -> p h t"),
                         otst[:].rearrange("p (h t) -> p h t", h=H), ot_st)
            prog.wait("scalar", ot_st, ot_st.n)

        prog.barrier(bar)
        if phases < 5:
            finish(prog, nc)
            return nc

        with ExitStack() as p5:
            E5 = p5.enter_context
            wout_sb = E5(nc.sbuf_tensor("wout_sb", [128, NI * E], BF16))
            otc = [E5(nc.sbuf_tensor(f"otc{p}", [128, H * 128], BF16)) for p in range(2)]
            gt = [E5(nc.sbuf_tensor(f"gt{p}", [128, E], BF16)) for p in range(2)]
            xt = [E5(nc.sbuf_tensor(f"xt{p}", [128, E], F32)) for p in range(2)]
            sfin = [E5(nc.sbuf_tensor(f"sfin{p}", [128, 512], F32)) for p in range(2)]
            obf = [E5(nc.sbuf_tensor(f"obf{p}", [128, E], F32)) for p in range(2)]
            s5_ps = [E5(nc.psum_tensor(f"s5_ps{p}", [128, 512], F32)) for p in range(2)]

            w5_ld = [SM(f"w5_ld{s}") for s in range(4)]
            t5_ld = [SM("t5_ld0"), SM("t5_ld1")]
            s5mm = SM("s5mm")
            findn = SM("findn")
            outst = [SM("outst0"), SM("outst1")]
            v5_self = SM("v5_self")

            for sd in g_st:
                prog.wait("sync", sd, sd.n)

            for s in range(4):
                prog.dma("gpsimd",
                         wout_sb[:].rearrange("p (i e) -> p i e", i=NI)[:, :, 512 * s:512 * (s + 1)],
                         woutT[:, 512 * s:512 * (s + 1)].rearrange("(i p) e -> p i e", p=128),
                         w5_ld[s])

            def load_tt(tt):
                tp = tt % 2
                prog.wait("sync", ot_st, 16 * (tt // 4 + 1))
                prog.dma("sync", otc[tp][:].rearrange("p (h t) -> p h t", h=H),
                         OT[:, :, 128 * tt:128 * (tt + 1)].rearrange("h p t -> p h t"),
                         t5_ld[tp])
                prog.dma("sync", gt[tp][:], G[tt], t5_ld[tp])
                prog.dma("sync", xt[tp][:],
                         xh[512 + 128 * tt: 512 + 128 * (tt + 1), :], t5_ld[tp])
                return t5_ld[tp].n

            rdy = {0: load_tt(0)}
            mmq = 0
            for tt in range(NTC):
                tp = tt % 2
                if tt < NTC - 1:
                    if tt >= 1:
                        prog.wait("sync", outst[(tt + 1) % 2], 16 * ((tt + 1) // 2))
                    rdy[tt + 1] = load_tt(tt + 1)
                prog.wait("tensor", t5_ld[tp], rdy[tt])
                if tt >= 2:
                    prog.wait("vector", outst[tp], 16 * (tt // 2))
                for og in range(4):
                    gp = mmq % 2
                    if tt == 0:
                        prog.wait("tensor", w5_ld[og], 16)
                    if mmq >= 2:
                        prog.wait("tensor", findn, mmq - 1)
                    for hh in range(H):
                        prog.op("tensor", lambda e, tp=tp, og=og, hh=hh, gp=gp: e.matmul(
                            s5_ps[gp][:],
                            otc[tp][:, 128 * hh:128 * (hh + 1)],
                            wout_sb[:, E * hh + 512 * og: E * hh + 512 * (og + 1)],
                            start=(hh == 0), stop=(hh == H - 1)),
                            s5mm if hh == H - 1 else None)
                    mmq += 1
                    prog.wait("vector", s5mm, mmq)
                    t = prog.op("vector", lambda e, tp=tp, og=og, gp=gp: e.tensor_tensor(
                        sfin[gp][:], s5_ps[gp][:], gt[tp][:, 512 * og:512 * (og + 1)],
                        ALU.mult), v5_self)
                    prog.wait("vector", v5_self, t)
                    prog.op("vector", lambda e, tp=tp, og=og, gp=gp: e.tensor_tensor(
                        obf[tp][:, 512 * og:512 * (og + 1)], sfin[gp][:],
                        xt[tp][:, 512 * og:512 * (og + 1)], ALU.add), findn)
                prog.wait("sync", findn, findn.n)
                prog.dma("sync", out[128 * tt:128 * (tt + 1), :], obf[tp][:], outst[tp])
            prog.wait("sync", outst[0], outst[0].n)
            prog.wait("sync", outst[1], outst[1].n)

        finish(prog, nc)
    return nc


def finish(prog, nc):
    with nc.Block() as block:
        for name in ENGS:
            deco = getattr(block, name)

            def runner(eng, name=name):
                for fn in prog.streams[name]:
                    fn(eng)

            deco(runner)



def _consts():
    ident = np.eye(128, dtype=ml_dtypes.bfloat16)
    ones = np.ones((128, 512), dtype=ml_dtypes.bfloat16)
    p = np.arange(128)[:, None]
    f = np.arange(128)[None, :]
    triP = np.where(p >= f, 1.0, 0.0).astype(ml_dtypes.bfloat16)
    triO = np.where(p <= f, 1.0, 0.0).astype(ml_dtypes.bfloat16)
    return ident, ones, triP, triO


def make_in_maps(x, wqkv, wout, wffn, bias):
    ident, ones, triP, triO = _consts()
    wqkvT = np.ascontiguousarray(np.asarray(wqkv, dtype=np.float32).T)
    woutT = np.ascontiguousarray(np.asarray(wout, dtype=np.float32).T)
    wffnT = np.ascontiguousarray(np.asarray(wffn, dtype=np.float32).T)
    bias1 = np.ascontiguousarray(np.asarray(bias, dtype=np.float32).reshape(1, E))
    in_maps = []
    for core in range(8):
        b, j = core // 4, core % 4
        lo = 2048 * j
        if j == 0:
            halo = np.zeros((C, E), dtype=np.float32)
            m0 = np.full((1, 128), NEG, dtype=ml_dtypes.bfloat16)
        else:
            halo = np.asarray(x[b, lo - C:lo], dtype=np.float32)
            m0 = np.zeros((1, 128), dtype=ml_dtypes.bfloat16)
        xhc = np.ascontiguousarray(
            np.concatenate([halo, np.asarray(x[b, lo:lo + TC], dtype=np.float32)], 0))
        in_maps.append(dict(
            xh=xhc, wqkvT=wqkvT, woutT=woutT, wffnT=wffnT, bias1=bias1,
            identN=ident, ones_in=ones, triP_in=triP, triO_in=triO, m0row=m0))
    return in_maps


def kernel(L0_attn_x, L0_attn_w_qkv, L0_attn_w_out, L0_ffn_weight, L0_ffn_bias):
    x = np.asarray(L0_attn_x)
    in_maps = make_in_maps(x, L0_attn_w_qkv, L0_attn_w_out, L0_ffn_weight, L0_ffn_bias)
    nc = build()
    res = run_bass_kernel_spmd(nc, in_maps, core_ids=list(range(8)))
    outa = np.empty((2, 8192, 2048), dtype=np.float32)
    for core in range(8):
        b, j = core // 4, core % 4
        outa[b, 2048 * j:2048 * (j + 1)] = res.results[core]["out"]
    return outa


# version 06
# speedup vs baseline: 1.2690x; 1.0075x over previous
import math
import numpy as np
import ml_dtypes
from contextlib import ExitStack

import concourse.bass as bass
import concourse.mybir as mybir
from concourse.bass_utils import run_bass_kernel_spmd

F32 = mybir.dt.float32
BF16 = mybir.dt.bfloat16
AF = mybir.ActivationFunctionType
ALU = mybir.AluOpType

NEG = -1.0e30
E = 2048
NI = 16
H = 16
C = 512
TC = 2048
TA = 2560
NTC = 16
NTA = 20
SCALE = 1.0 / math.sqrt(128.0)
ENGS = ("sync", "tensor", "vector", "scalar", "gpsimd")


class Sem:
    def __init__(self, nc, stack, name):
        self.h = stack.enter_context(nc.semaphore(name))
        self.n = 0


class Prog:

    def __init__(self):
        self.streams = {e: [] for e in ENGS}

    def emit(self, eng, fn):
        self.streams[eng].append(fn)

    def wait(self, eng, sem, tgt):
        if tgt > 0:
            self.emit(eng, lambda e, s=sem.h, t=tgt: e.wait_ge(s, t))

    def op(self, eng, builder, sem=None, k=1):
        if sem is None:
            self.emit(eng, builder)
            return None
        sem.n += k
        self.emit(eng, lambda e, b=builder, s=sem.h, kk=k: b(e).then_inc(s, kk))
        return sem.n

    def dma(self, eng, out_ap, in_ap, sem):
        sem.n += 16
        self.emit(eng, lambda e, o=out_ap, i=in_ap, s=sem.h: e.dma_start(o, i).then_inc(s, 16))
        return sem.n

    def barrier(self, bar):
        for e in ENGS:
            self.emit(e, lambda en, s=bar.h: en.nop().then_inc(s, 1))
        bar.n += len(ENGS)
        tgt = bar.n
        for e in ENGS:
            self.emit(e, lambda en, s=bar.h, t=tgt: en.wait_ge(s, t))


def build(dbg=frozenset(), phases=5):
    nc = bass.Bass("TRN2", target_bir_lowering=False)

    def dram(name, shape, dtype):
        kind = "ExternalOutput" if name in dbg else "Internal"
        return nc.dram_tensor(name, shape, dtype, kind=kind)

    xh = nc.dram_tensor("xh", [TA, E], F32, kind="ExternalInput").ap()
    wqkvT = nc.dram_tensor("wqkvT", [E, 3 * E], F32, kind="ExternalInput").ap()
    woutT = nc.dram_tensor("woutT", [E, E], F32, kind="ExternalInput").ap()
    wffnT = nc.dram_tensor("wffnT", [E, E], F32, kind="ExternalInput").ap()
    bias1 = nc.dram_tensor("bias1", [1, E], F32, kind="ExternalInput").ap()
    identN = nc.dram_tensor("identN", [128, 128], BF16, kind="ExternalInput").ap()
    ones_in = nc.dram_tensor("ones_in", [128, 512], BF16, kind="ExternalInput").ap()
    triP_in = nc.dram_tensor("triP_in", [128, 128], BF16, kind="ExternalInput").ap()
    triO_in = nc.dram_tensor("triO_in", [128, 128], BF16, kind="ExternalInput").ap()
    m0row = nc.dram_tensor("m0row", [1, 128], BF16, kind="ExternalInput").ap()

    out = nc.dram_tensor("out", [TC, E], F32, kind="ExternalOutput").ap()

    QT = dram("QT", [H, 128, TC], BF16).ap()
    KT = dram("KT", [H, 128, TA], BF16).ap()
    VN = dram("VN", [NTA, 128, E], BF16).ap()
    OT = dram("OT", [H, 128, TC], BF16).ap()
    G = dram("G", [NTC, 128, E], BF16).ap()
    if "XNT" in dbg:
        XNTd = nc.dram_tensor("XNT", [NI, 128, TA], BF16, kind="ExternalOutput").ap()

    prog = Prog()

    with ExitStack() as outer:
        EO = outer.enter_context
        ident = EO(nc.sbuf_tensor("ident", [128, 128], BF16))
        ones_sb = EO(nc.sbuf_tensor("ones_sb", [128, 512], BF16))
        triP01 = EO(nc.sbuf_tensor("triP01", [128, 128], BF16))
        triO01 = EO(nc.sbuf_tensor("triO01", [128, 128], BF16))
        m0_sb = EO(nc.sbuf_tensor("m0_sb", [1, 128], BF16))
        bias_sb = EO(nc.sbuf_tensor("bias_sb", [1, E], BF16))

        SM = lambda name: Sem(nc, outer, name)
        bar = SM("bar")
        const_ld = SM("const_ld")
        CONST_TOT = 6 * 16

        prog.dma("sync", ident[:], identN, const_ld)
        prog.dma("sync", ones_sb[:], ones_in, const_ld)
        prog.dma("sync", triP01[:], triP_in, const_ld)
        prog.dma("sync", triO01[:], triO_in, const_ld)
        prog.dma("sync", m0_sb[:], m0row, const_ld)
        prog.dma("gpsimd", bias_sb[:], bias1, const_ld)

        abc = EO(nc.sbuf_tensor("abc", [128, E], BF16))
        s123 = ExitStack()
        xnt = s123.enter_context(nc.sbuf_tensor("xnt", [128, NI * TA], BF16))
        xnt3 = xnt[:].rearrange("p (i t) -> p i t", i=NI)

        with ExitStack() as p1:
            E1 = p1.enter_context
            xbuf = [E1(nc.sbuf_tensor(f"xbuf{p}", [128, 2 * E], F32)) for p in range(4)]
            xnat = [E1(nc.sbuf_tensor(f"xnat{p}", [128, 2 * E], BF16)) for p in range(2)]
            stats = E1(nc.sbuf_tensor("stats", [128, 24], F32))
            mv = [E1(nc.sbuf_tensor(f"mv{p}", [128, 4], F32)) for p in range(2)]
            veps = [E1(nc.sbuf_tensor(f"veps{p}", [128, 2], F32)) for p in range(2)]
            sqv = [E1(nc.sbuf_tensor(f"sqv{p}", [128, 2], F32)) for p in range(2)]
            rstd = [E1(nc.sbuf_tensor(f"rstd{p}", [128, 2], F32)) for p in range(2)]
            nmr = [E1(nc.sbuf_tensor(f"nmr{p}", [128, 2], F32)) for p in range(2)]
            wfs = [E1(nc.sbuf_tensor(f"wfs{p}", [128, E], BF16)) for p in range(2)]
            abst = [E1(nc.sbuf_tensor(f"abst{p}", [128, E], BF16)) for p in range(2)]
            accf = E1(nc.sbuf_tensor("accf", [128, E], F32))
            accbf = E1(nc.sbuf_tensor("accbf", [128, E], BF16))
            arow = E1(nc.sbuf_tensor("arow", [1, E], BF16))
            tp_ps = [E1(nc.psum_tensor(f"tp_ps{p}", [128, 512], F32)) for p in range(2)]
            al_ps = E1(nc.psum_tensor("al_ps", [128, 512], F32))

            x_ld = [SM(f"x_ld{p}") for p in range(4)]
            v_self = SM("v_self")
            s_self = SM("s_self")
            norm_done = SM("norm_done")
            tp_done = SM("tp_done")
            tp_ev = SM("tp_ev")
            wf_ld = [SM(f"wf_ld{p}") for p in range(2)]
            abs_dn = SM("abs_dn")
            acA = SM("acA")
            alev = SM("alev")
            abev = SM("abev")

            def emit_alpha_tile(i):
                ip = i % 2
                if i >= 2:
                    prog.wait("gpsimd", abs_dn, i - 1)
                prog.dma("gpsimd", wfs[ip][:], wffnT[128 * i:128 * (i + 1), :], wf_ld[ip])
                prog.wait("scalar", wf_ld[ip], 16 * (i // 2 + 1))
                if i >= 2:
                    prog.wait("scalar", acA, i - 1)
                prog.op("scalar", lambda e, ip=ip: e.activation(
                    abst[ip][:], wfs[ip][:], AF.Abs), abs_dn)
                prog.wait("vector", abs_dn, i + 1)
                if i == 0:
                    prog.op("vector", lambda e: e.tensor_copy(accf[:], abst[0][:]), acA)
                else:
                    prog.op("vector", lambda e, ip=ip: e.tensor_tensor(
                        accf[:], accf[:], abst[ip][:], ALU.add), acA)

            for c in range(10):
                p = c % 2
                p4 = c % 4
                if c >= 4:
                    prog.wait("sync", norm_done, c - 3)
                ld_tgt = prog.dma(
                    "sync", xbuf[p4][:].rearrange("p (a e) -> p a e", a=2),
                    xh[256 * c:256 * (c + 1), :].rearrange("(a p) e -> p a e", p=128),
                    x_ld[p4])
                if c < 8:
                    emit_alpha_tile(2 * c)
                    emit_alpha_tile(2 * c + 1)
                prog.wait("vector", x_ld[p4], ld_tgt)
                for a in range(2):
                    for g4 in range(4):
                        t = prog.op("vector", lambda e, a=a, g=g4, p=p4: e.bn_stats(
                            stats[:, 6 * g:6 * (g + 1)],
                            xbuf[p][:, 2048 * a + 512 * g: 2048 * a + 512 * (g + 1)]), v_self)
                    prog.wait("vector", v_self, t)
                    t = prog.op("vector", lambda e, a=a, p=p: e.bn_aggr(
                        mv[p][:, 2 * a:2 * a + 2], stats[:]), v_self)
                    prog.wait("vector", v_self, t)
                t = prog.op("vector", lambda e, p=p: e.tensor_scalar(
                    veps[p][:], mv[p][:, 1:4:2], 1e-5, None, ALU.add), v_self)
                prog.wait("scalar", v_self, t)
                t = prog.op("scalar", lambda e, p=p: e.activation(
                    sqv[p][:], veps[p][:], AF.Sqrt), s_self)
                prog.wait("vector", s_self, t)
                t = prog.op("vector", lambda e, p=p: e.reciprocal(rstd[p][:], sqv[p][:]), v_self)
                prog.wait("vector", v_self, t)
                tn = prog.op("vector", lambda e, p=p: e.scalar_tensor_tensor(
                    nmr[p][:], mv[p][:, 0:4:2], -1.0, rstd[p][:], ALU.mult, ALU.mult), v_self)
                prog.wait("scalar", v_self, tn)
                if c >= 2:
                    prog.wait("scalar", tp_done, 8 * (c - 1))
                for a in range(2):
                    sem = norm_done if a == 1 else s_self
                    prog.op("scalar", lambda e, a=a, p=p, p4=p4: e.activation(
                        xnat[p][:, 2048 * a:2048 * (a + 1)],
                        xbuf[p4][:, 2048 * a:2048 * (a + 1)], AF.Identity,
                        bias=nmr[p][:, a:a + 1], scale=rstd[p][:, a:a + 1]), sem)
                prog.wait("tensor", const_ld, CONST_TOT)
                prog.wait("tensor", norm_done, c + 1)
                for a in range(2):
                    for ig in range(4):
                        gpar = tp_done.n % 2
                        if tp_done.n >= 2:
                            prog.wait("tensor", tp_ev, tp_done.n - 1)
                        for kk in range(4):
                            i = 4 * ig + kk
                            prog.op("tensor", lambda e, a=a, i=i, kk=kk, p=p, gp=gpar: e.matmul(
                                tp_ps[gp][:, 128 * kk:128 * (kk + 1)],
                                xnat[p][:, 2048 * a + 128 * i: 2048 * a + 128 * (i + 1)],
                                ident[:], start=True, stop=True),
                                tp_done if kk == 3 else None)
                        tgt = tp_done.n
                        ev_eng = "vector" if ig % 4 == 0 else "scalar"
                        prog.wait(ev_eng, tp_done, tgt)
                        dst = xnt3[:, 4 * ig:4 * ig + 4,
                                   256 * c + 128 * a: 256 * c + 128 * (a + 1)]
                        if ig % 4 == 0:
                            prog.op("vector", lambda e, d=dst, gp=gpar: e.tensor_copy(
                                d, tp_ps[gp][:].rearrange("p (i t) -> p i t", i=4)), tp_ev)
                        else:
                            prog.op("scalar", lambda e, d=dst, gp=gpar: e.copy(
                                d, tp_ps[gp][:].rearrange("p (i t) -> p i t", i=4)), tp_ev)
            almm = SM("almm")
            abmm = SM("abmm")
            prog.wait("vector", acA, 16)
            tcast = prog.op("vector", lambda e: e.tensor_copy(accbf[:], accf[:]), v_self)
            prog.wait("tensor", v_self, tcast)
            for g in range(4):
                if g >= 1:
                    prog.wait("tensor", alev, g)
                    prog.wait("tensor", abev, g)
                t = prog.op("tensor", lambda e, g=g: e.matmul(
                    al_ps[0:1, :], ones_sb[:, 0:1], accbf[:, 512 * g:512 * (g + 1)],
                    start=True, stop=True), almm)
                prog.wait("scalar", almm, t)
                t = prog.op("scalar", lambda e, g=g: e.activation(
                    arow[0:1, 512 * g:512 * (g + 1)], al_ps[0:1, :], AF.Copy,
                    scale=1.0 / E), alev)
                prog.wait("tensor", alev, t)
                t = prog.op("tensor", lambda e, g=g: e.matmul(
                    tp_ps[0][:], ones_sb[0:1, 0:128], arow[0:1, 512 * g:512 * (g + 1)],
                    start=True, stop=True), abmm)
                prog.wait("vector", abmm, t)
                prog.op("vector", lambda e, g=g: e.tensor_copy(
                    abc[:, 512 * g:512 * (g + 1)], tp_ps[0][:]), abev)
            if "XNT" in dbg:
                prog.wait("sync", tp_ev, tp_ev.n)
                dmp = SM("dmp")
                for hf in range(2):
                    prog.dma("sync",
                             XNTd[8 * hf:8 * (hf + 1)].rearrange("i p t -> p i t"),
                             xnt3[:, 8 * hf:8 * (hf + 1), :], dmp)
                prog.wait("sync", dmp, dmp.n)

        prog.barrier(bar)
        if phases < 2:
            s123.close()
            finish(prog, nc)
            return nc

        s23 = ExitStack()
        wsl = [s23.enter_context(nc.sbuf_tensor(f"wsl{p}", [128, NI * 512], BF16))
               for p in range(2)]
        slab_ld = [SM("slab_ld0"), SM("slab_ld1")]
        slab_free = {}
        slab_tgt = {}

        def load_slab(s):
            if s >= 2:
                sem, cnt = slab_free[s - 2]
                prog.wait("gpsimd", sem, cnt)
            slab_tgt[s] = prog.dma(
                "gpsimd", wsl[s % 2][:],
                wqkvT[:, 512 * s:512 * (s + 1)].rearrange("(i p) e -> p i e", p=128),
                slab_ld[s % 2])

        with ExitStack() as p2:
            E2 = p2.enter_context
            wffn = E2(nc.sbuf_tensor("wffn", [128, NI * E], BF16))
            gst = [E2(nc.sbuf_tensor(f"gst{p}", [128, 2 * E], BF16)) for p in range(2)]
            g_ps = [E2(nc.psum_tensor(f"g_ps{p}", [128, 512], F32)) for p in range(2)]

            wff_ld = [SM(f"wff_ld{r}") for r in range(4)]
            bin_dn = SM("bin_dn")
            gmm = SM("gmm")
            silu_dn = SM("silu_dn")
            g_st = [SM("g_st0"), SM("g_st1")]
            v2_self = SM("v2_self")

            wffn3 = wffn[:].rearrange("p (i e) -> p i e", i=NI)
            for r in range(4):
                prog.dma("gpsimd", wffn3[:, 4 * r:4 * (r + 1), :],
                         wffnT[512 * r:512 * (r + 1), :].rearrange("(i p) e -> p i e", p=128),
                         wff_ld[r])
            load_slab(0)
            load_slab(1)
            prog.wait("vector", abev, 4)
            for i in range(NI):
                if i % 4 == 0:
                    prog.wait("vector", wff_ld[i // 4], 16)
                t = prog.op("vector", lambda e, i=i: e.tensor_scalar(
                    wffn[:, E * i:E * (i + 1)], wffn[:, E * i:E * (i + 1)],
                    0.0, 2.0, ALU.is_ge, ALU.mult), v2_self)
                prog.wait("vector", v2_self, t)
                prog.op("vector", lambda e, i=i: e.scalar_tensor_tensor(
                    wffn[:, E * i:E * (i + 1)], wffn[:, E * i:E * (i + 1)],
                    -1.0, abc[:], ALU.add, ALU.mult), bin_dn)
            prog.wait("tensor", bin_dn, NI)
            prog.wait("tensor", const_ld, CONST_TOT)
            mmq = 0
            for tt in range(NTC):
                pr = tt // 2
                bufp = pr % 2
                for og in range(4):
                    gp = mmq % 2
                    if mmq >= 2:
                        prog.wait("tensor", silu_dn, mmq - 1)
                    for i in range(NI):
                        prog.op("tensor", lambda e, tt=tt, og=og, i=i, gp=gp: e.matmul(
                            g_ps[gp][:],
                            xnt[:, TA * i + 512 + 128 * tt: TA * i + 512 + 128 * (tt + 1)],
                            wffn[:, E * i + 512 * og: E * i + 512 * (og + 1)],
                            start=(i == 0), stop=False))
                    t = prog.op("tensor", lambda e, og=og, gp=gp: e.matmul(
                        g_ps[gp][:], ones_sb[0:1, 0:128], bias_sb[0:1, 512 * og:512 * (og + 1)],
                        start=False, stop=True), gmm)
                    prog.wait("scalar", gmm, t)
                    if tt % 2 == 0 and og == 0 and pr >= 2:
                        prog.wait("scalar", g_st[bufp], 16 * (pr // 2))
                    prog.op("scalar", lambda e, tt=tt, og=og, bp=bufp, gp=gp: e.activation(
                        gst[bp][:, E * (tt % 2) + 512 * og: E * (tt % 2) + 512 * (og + 1)],
                        g_ps[gp][:], AF.Silu), silu_dn)
                    mmq += 1
                if tt % 2 == 1:
                    prog.wait("scalar", silu_dn, mmq)
                    prog.dma("scalar",
                             G[tt - 1:tt + 1].rearrange("a p e -> p a e"),
                             gst[bufp][:].rearrange("p (a e) -> p a e", a=2), g_st[bufp])
            prog.wait("scalar", g_st[0], g_st[0].n)
            prog.wait("scalar", g_st[1], g_st[1].n)

        prog.barrier(bar)
        if phases < 3:
            s23.close()
            s123.close()
            finish(prog, nc)
            return nc

        with ExitStack() as p3:
            E3 = p3.enter_context
            qkst = [E3(nc.sbuf_tensor(f"qkst{p}", [128, TA], BF16)) for p in range(2)]
            vstb = [E3(nc.sbuf_tensor(f"vstb{p}", [128, 4 * 512], BF16)) for p in range(2)]
            qk_ps = [E3(nc.psum_tensor(f"qk_ps{p}", [128, 512], F32)) for p in range(2)]

            qkmm = SM("qkmm")
            qkev = SM("qkev")
            qkst_dn = [SM("qkst_dn0"), SM("qkst_dn1")]
            vmm = SM("vmm")
            vev = SM("vev")
            vst_dn = [SM("vst_dn0"), SM("vst_dn1")]

            for s in range(8):
                sp = s % 2
                is_q = s < 4
                ntg = 4 if is_q else 5
                toff = 512 if is_q else 0
                ev_eng = "scalar" if is_q else "vector"
                prog.wait("tensor", slab_ld[sp], slab_tgt[s])
                for el in range(4):
                    h = (4 * s + el) % 16
                    eg = 4 * s + el
                    stp = eg % 2
                    if eg >= 2:
                        prog.wait(ev_eng, qkst_dn[stp], 16 * (eg // 2))
                    for tg in range(ntg):
                        gp = qkmm.n % 2
                        if qkmm.n >= 2:
                            prog.wait("tensor", qkev, qkmm.n - 1)
                        for i in range(NI):
                            prog.op("tensor", lambda e, sp=sp, el=el, tg=tg, i=i, gp=gp, to=toff: e.matmul(
                                qk_ps[gp][:],
                                wsl[sp][:, 512 * i + 128 * el: 512 * i + 128 * (el + 1)],
                                xnt[:, TA * i + to + 512 * tg: TA * i + to + 512 * (tg + 1)],
                                start=(i == 0), stop=(i == NI - 1)),
                                qkmm if i == NI - 1 else None)
                        prog.wait(ev_eng, qkmm, qkmm.n)
                        if is_q:
                            prog.op("scalar", lambda e, tg=tg, stp=stp, gp=gp: e.activation(
                                qkst[stp][:, 512 * tg:512 * (tg + 1)], qk_ps[gp][:],
                                AF.Copy, scale=SCALE), qkev)
                        else:
                            prog.op("vector", lambda e, tg=tg, stp=stp, gp=gp: e.tensor_copy(
                                qkst[stp][:, 512 * tg:512 * (tg + 1)], qk_ps[gp][:]), qkev)
                    prog.wait("sync", qkev, qkev.n)
                    if is_q:
                        prog.dma("sync", QT[h], qkst[stp][:, 0:TC], qkst_dn[stp])
                    else:
                        prog.dma("sync", KT[h], qkst[stp][:, 0:TA], qkst_dn[stp])
                slab_free[s] = (qkmm, qkmm.n)
                if s + 2 < 12:
                    load_slab(s + 2)
            for s in range(8, 12):
                sp = s % 2
                g = s - 8
                prog.wait("tensor", slab_ld[sp], slab_tgt[s])
                for tq in range(5):
                    qg = 5 * g + tq
                    vp = qg % 2
                    if qg >= 2:
                        prog.wait("vector", vst_dn[vp], 16 * (qg // 2))
                    for a in range(4):
                        tau = 4 * tq + a
                        gp = vmm.n % 2
                        if vmm.n >= 2:
                            prog.wait("tensor", vev, vmm.n - 1)
                        for i in range(NI):
                            prog.op("tensor", lambda e, sp=sp, tau=tau, i=i, gp=gp: e.matmul(
                                qk_ps[gp][:],
                                xnt[:, TA * i + 128 * tau: TA * i + 128 * (tau + 1)],
                                wsl[sp][:, 512 * i:512 * (i + 1)],
                                start=(i == 0), stop=(i == NI - 1)),
                                vmm if i == NI - 1 else None)
                        prog.wait("vector", vmm, vmm.n)
                        prog.op("vector", lambda e, a=a, vp=vp, gp=gp: e.tensor_copy(
                            vstb[vp][:, 512 * a:512 * (a + 1)], qk_ps[gp][:]), vev)
                    prog.wait("sync", vev, vev.n)
                    prog.dma("sync",
                             VN[4 * tq:4 * tq + 4, :, 512 * g:512 * (g + 1)].rearrange(
                                 "a p e -> p a e"),
                             vstb[vp][:].rearrange("p (a e) -> p a e", a=4), vst_dn[vp])
                slab_free[s] = (vmm, vmm.n)
                if s + 2 < 12:
                    load_slab(s + 2)
            for sd in qkst_dn + vst_dn:
                prog.wait("sync", sd, sd.n)

        s23.close()
        s123.close()
        prog.barrier(bar)
        if phases < 4:
            finish(prog, nc)
            return nc

        with ExitStack() as p4:
            E4 = p4.enter_context
            qtb = [E4(nc.sbuf_tensor(f"qtb{p}", [128, H * 512], BF16)) for p in range(2)]
            ktw = E4(nc.sbuf_tensor("ktw", [128, H * 3 * 512], BF16))
            vnw = E4(nc.sbuf_tensor("vnw", [128, 3 * 4 * E], BF16))
            pts = [E4(nc.sbuf_tensor(f"pts{p}", [128, 8 * 1024], BF16)) for p in range(2)]
            rlb = [E4(nc.sbuf_tensor(f"rlb{p}", [128, 1024], F32)) for p in range(2)]
            rltmp = E4(nc.sbuf_tensor("rltmp", [128, 1024], F32))
            otst = E4(nc.sbuf_tensor("otst", [128, H * 512], BF16))
            otraw = [E4(nc.sbuf_tensor(f"otraw{p}", [128, 1024], BF16)) for p in range(2)]
            s4 = [E4(nc.psum_tensor(f"s4_{p}", [128, 1024], F32)) for p in range(2)]
            ob2 = E4(nc.psum_tensor("ob2", [128, 1024], F32))
            lb2 = E4(nc.psum_tensor("lb2", [128, 1024], F32))

            kv_ld = [SM("kv_ld0"), SM("kv_ld1"), SM("kv_ld2")]
            qt_ld = [SM("qt_ld0"), SM("qt_ld1")]
            smm = SM("smm")
            expd = SM("expd")
            mskd = SM("mskd")
            avl = SM("avl")
            lnd = SM("lnd")
            rld = SM("rld")
            oraw = SM("oraw")
            oev = SM("oev")
            ot_st = SM("ot_st")

            for sd in qkst_dn + vst_dn:
                prog.wait("sync", sd, sd.n)

            def load_kv(gc):
                sl = gc % 3
                prog.dma("sync",
                         ktw[:].rearrange("p (h r t) -> p h r t", h=H, r=3)[:, :, sl, :],
                         KT[:, :, 512 * gc:512 * (gc + 1)].rearrange("h p t -> p h t"),
                         kv_ld[sl])
                prog.dma("sync",
                         vnw[:].rearrange("p (r a e) -> p r a e", r=3, a=4)[:, sl],
                         VN[4 * gc:4 * (gc + 1)].rearrange("a p e -> p a e"),
                         kv_ld[sl])
                return kv_ld[sl].n

            def load_qt(c):
                prog.dma("sync", qtb[c % 2][:].rearrange("p (h t) -> p h t", h=H),
                         QT[:, :, 512 * c:512 * (c + 1)].rearrange("h p t -> p h t"),
                         qt_ld[c % 2])
                return qt_ld[c % 2].n

            kv_rdy = {0: load_kv(0), 1: load_kv(1)}
            qt_rdy = {0: load_qt(0)}

            for cc in range(4):
                gc = cc + 1
                if cc < 3:
                    if cc >= 1:
                        prog.wait("sync", avl, cc * 8)
                    kv_rdy[gc + 1] = load_kv(gc + 1)
                    qt_rdy[cc + 1] = load_qt(cc + 1)
                prog.wait("tensor", kv_ld[(gc - 1) % 3], kv_rdy[gc - 1])
                prog.wait("tensor", kv_ld[gc % 3], kv_rdy[gc])
                prog.wait("tensor", qt_ld[cc % 2], qt_rdy[cc])
                for h2 in range(H // 2):
                    gp2 = cc * 8 + h2
                    pp = h2 % 2
                    if gp2 >= 2:
                        prog.wait("scalar", avl, gp2 - 1)
                    jobs = [("own", jj, 128 * jj, 512, gc % 3, jj) for jj in range(4)] + \
                           [("prev", jj, 0, 128 * (jj + 1), (gc - 1) % 3, jj) for jj in range(4)]

                    def emit_sjob(idx, h2=h2, gp2=gp2, cc=cc, jobs=jobs, pp=pp):
                        gq = gp2 * 8 + idx
                        kind, jj, qlo, qhi, sl, jt = jobs[idx]
                        mend = (kind == "prev" and cc == 0)
                        if gq >= 2:
                            prog.wait("tensor", expd, gq - 1)
                        for a in range(2):
                            h = 2 * h2 + a
                            last = (a == 1) and not mend
                            prog.op("tensor", lambda e, h=h, jj=jj, qlo=qlo, qhi=qhi, sl=sl,
                                    gq=gq, a=a, cp=cc % 2, mend=mend: e.matmul(
                                s4[gq % 2][:, 512 * a + qlo:512 * a + qhi],
                                ktw[:, 1536 * h + 512 * sl + 128 * jj: 1536 * h + 512 * sl + 128 * (jj + 1)],
                                qtb[cp][:, 512 * h + qlo: 512 * h + qhi],
                                start=True, stop=not mend),
                                smm if last else None)
                            if mend:
                                prog.op("tensor", lambda e, qlo=qlo, qhi=qhi, gq=gq, a=a: e.matmul(
                                    s4[gq % 2][:, 512 * a + qlo:512 * a + qhi], m0_sb[0:1, :],
                                    ones_sb[0:1, 0:qhi - qlo],
                                    start=False, stop=True),
                                    smm if a == 1 else None)
                        prog.wait("scalar", smm, smm.n)
                        prog.op("scalar", lambda e, idx=idx, qlo=qlo, qhi=qhi, gq=gq, pp=pp: e.activation(
                            pts[pp][:].rearrange("p (s a w) -> p s a w", s=8, a=2)[:, idx, :, qlo:qhi],
                            s4[gq % 2][:].rearrange("p (a w) -> p a w", a=2)[:, :, qlo:qhi],
                            AF.Exp), expd)
                        prog.wait("vector", expd, expd.n)
                        tri = triO01 if kind == "own" else triP01
                        for a in range(2):
                            sl_ap = pts[pp][:, 1024 * idx + 512 * a + 128 * jj:
                                            1024 * idx + 512 * a + 128 * (jj + 1)]
                            prog.op("vector", lambda e, d=sl_ap, m=tri: e.tensor_tensor(
                                d, d, m[:], ALU.mult), mskd)

                    def emit_avl(idx, h2=h2, gp2=gp2, jobs=jobs, pp=pp):
                        kind, jj, qlo, qhi, sl, jt = jobs[idx]
                        prog.wait("tensor", mskd, 2 * (gp2 * 8 + idx + 1))
                        for a in range(2):
                            h = 2 * h2 + a
                            prog.op("tensor", lambda e, h=h, sl=sl, jt=jt, idx=idx, a=a,
                                    qlo=qlo, qhi=qhi, pp=pp: e.matmul(
                                ob2[:, 512 * a + qlo:512 * a + qhi],
                                vnw[:, 8192 * sl + 2048 * jt + 128 * h: 8192 * sl + 2048 * jt + 128 * (h + 1)],
                                pts[pp][:, 1024 * idx + 512 * a + qlo: 1024 * idx + 512 * a + qhi],
                                start=(idx == 0), stop=(idx == 7)))
                        for a in range(2):
                            prog.op("tensor", lambda e, idx=idx, a=a, qlo=qlo, qhi=qhi, pp=pp: e.matmul(
                                lb2[:, 512 * a + qlo:512 * a + qhi], ones_sb[:, 0:128],
                                pts[pp][:, 1024 * idx + 512 * a + qlo: 1024 * idx + 512 * a + qhi],
                                start=(idx == 0), stop=(idx == 7)),
                                avl if idx == 7 and a == 1 else None)

                    emit_sjob(0)
                    emit_sjob(1)
                    for idx in range(8):
                        if idx + 2 < 8:
                            emit_sjob(idx + 2)
                        if idx == 0 and gp2 >= 1:
                            prog.wait("tensor", oraw, gp2)
                            prog.wait("tensor", lnd, gp2)
                        emit_avl(idx)
                    prog.wait("vector", avl, avl.n)
                    prog.op("vector", lambda e, pp=pp: e.tensor_copy(
                        otraw[pp][:], ob2[:]), oraw)
                    prog.wait("scalar", avl, avl.n)
                    prog.op("scalar", lambda e: e.activation(
                        rltmp[:], lb2[:], AF.Ln), lnd)
                    if gp2 >= 2:
                        prog.wait("scalar", oev, gp2 - 1)
                    t = prog.op("scalar", lambda e, pp=pp: e.activation(
                        rlb[pp][:], rltmp[:], AF.Exp, scale=-1.0), rld)
                    prog.wait("vector", rld, t)
                    if cc >= 1 and h2 == 0:
                        prog.wait("vector", ot_st, 16 * cc)
                    prog.op("vector", lambda e, h2=h2, pp=pp: e.tensor_tensor(
                        otst[:, 1024 * h2:1024 * (h2 + 1)], otraw[pp][:], rlb[pp][:],
                        ALU.mult), oev)
                prog.wait("scalar", oev, oev.n)
                prog.dma("scalar", OT[:, :, 512 * cc:512 * (cc + 1)].rearrange("h p t -> p h t"),
                         otst[:].rearrange("p (h t) -> p h t", h=H), ot_st)
            prog.wait("scalar", ot_st, ot_st.n)

        prog.barrier(bar)
        if phases < 5:
            finish(prog, nc)
            return nc

        with ExitStack() as p5:
            E5 = p5.enter_context
            wout_sb = E5(nc.sbuf_tensor("wout_sb", [128, NI * E], BF16))
            otc = [E5(nc.sbuf_tensor(f"otc{p}", [128, H * 128], BF16)) for p in range(2)]
            gt = [E5(nc.sbuf_tensor(f"gt{p}", [128, E], BF16)) for p in range(2)]
            xt = [E5(nc.sbuf_tensor(f"xt{p}", [128, E], F32)) for p in range(2)]
            sfin = [E5(nc.sbuf_tensor(f"sfin{p}", [128, 512], F32)) for p in range(2)]
            obf = [E5(nc.sbuf_tensor(f"obf{p}", [128, E], F32)) for p in range(2)]
            s5_ps = [E5(nc.psum_tensor(f"s5_ps{p}", [128, 512], F32)) for p in range(2)]

            w5_ld = [SM(f"w5_ld{s}") for s in range(4)]
            t5_ld = [SM("t5_ld0"), SM("t5_ld1")]
            s5mm = SM("s5mm")
            findn = SM("findn")
            outst = [SM("outst0"), SM("outst1")]
            v5_self = SM("v5_self")

            for sd in g_st:
                prog.wait("sync", sd, sd.n)

            for s in range(4):
                prog.dma("gpsimd",
                         wout_sb[:].rearrange("p (i e) -> p i e", i=NI)[:, :, 512 * s:512 * (s + 1)],
                         woutT[:, 512 * s:512 * (s + 1)].rearrange("(i p) e -> p i e", p=128),
                         w5_ld[s])

            def load_tt(tt):
                tp = tt % 2
                prog.wait("sync", ot_st, 16 * (tt // 4 + 1))
                prog.dma("sync", otc[tp][:].rearrange("p (h t) -> p h t", h=H),
                         OT[:, :, 128 * tt:128 * (tt + 1)].rearrange("h p t -> p h t"),
                         t5_ld[tp])
                prog.dma("sync", gt[tp][:], G[tt], t5_ld[tp])
                prog.dma("sync", xt[tp][:],
                         xh[512 + 128 * tt: 512 + 128 * (tt + 1), :], t5_ld[tp])
                return t5_ld[tp].n

            rdy = {0: load_tt(0)}
            mmq = 0
            for tt in range(NTC):
                tp = tt % 2
                if tt < NTC - 1:
                    if tt >= 1:
                        prog.wait("sync", outst[(tt + 1) % 2], 16 * ((tt + 1) // 2))
                    rdy[tt + 1] = load_tt(tt + 1)
                prog.wait("tensor", t5_ld[tp], rdy[tt])
                if tt >= 2:
                    prog.wait("vector", outst[tp], 16 * (tt // 2))
                for og in range(4):
                    gp = mmq % 2
                    if tt == 0:
                        prog.wait("tensor", w5_ld[og], 16)
                    if mmq >= 2:
                        prog.wait("tensor", findn, mmq - 1)
                    for hh in range(H):
                        prog.op("tensor", lambda e, tp=tp, og=og, hh=hh, gp=gp: e.matmul(
                            s5_ps[gp][:],
                            otc[tp][:, 128 * hh:128 * (hh + 1)],
                            wout_sb[:, E * hh + 512 * og: E * hh + 512 * (og + 1)],
                            start=(hh == 0), stop=(hh == H - 1)),
                            s5mm if hh == H - 1 else None)
                    mmq += 1
                    prog.wait("vector", s5mm, mmq)
                    t = prog.op("vector", lambda e, tp=tp, og=og, gp=gp: e.tensor_tensor(
                        sfin[gp][:], s5_ps[gp][:], gt[tp][:, 512 * og:512 * (og + 1)],
                        ALU.mult), v5_self)
                    prog.wait("vector", v5_self, t)
                    prog.op("vector", lambda e, tp=tp, og=og, gp=gp: e.tensor_tensor(
                        obf[tp][:, 512 * og:512 * (og + 1)], sfin[gp][:],
                        xt[tp][:, 512 * og:512 * (og + 1)], ALU.add), findn)
                prog.wait("sync", findn, findn.n)
                prog.dma("sync", out[128 * tt:128 * (tt + 1), :], obf[tp][:], outst[tp])
            prog.wait("sync", outst[0], outst[0].n)
            prog.wait("sync", outst[1], outst[1].n)

        finish(prog, nc)
    return nc


def finish(prog, nc):
    with nc.Block() as block:
        for name in ENGS:
            deco = getattr(block, name)

            def runner(eng, name=name):
                for fn in prog.streams[name]:
                    fn(eng)

            deco(runner)



def _consts():
    ident = np.eye(128, dtype=ml_dtypes.bfloat16)
    ones = np.ones((128, 512), dtype=ml_dtypes.bfloat16)
    p = np.arange(128)[:, None]
    f = np.arange(128)[None, :]
    triP = np.where(p >= f, 1.0, 0.0).astype(ml_dtypes.bfloat16)
    triO = np.where(p <= f, 1.0, 0.0).astype(ml_dtypes.bfloat16)
    return ident, ones, triP, triO


def make_in_maps(x, wqkv, wout, wffn, bias):
    ident, ones, triP, triO = _consts()
    wqkvT = np.ascontiguousarray(np.asarray(wqkv, dtype=np.float32).T)
    woutT = np.ascontiguousarray(np.asarray(wout, dtype=np.float32).T)
    wffnT = np.ascontiguousarray(np.asarray(wffn, dtype=np.float32).T)
    bias1 = np.ascontiguousarray(np.asarray(bias, dtype=np.float32).reshape(1, E))
    in_maps = []
    for core in range(8):
        b, j = core // 4, core % 4
        lo = 2048 * j
        if j == 0:
            halo = np.zeros((C, E), dtype=np.float32)
            m0 = np.full((1, 128), NEG, dtype=ml_dtypes.bfloat16)
        else:
            halo = np.asarray(x[b, lo - C:lo], dtype=np.float32)
            m0 = np.zeros((1, 128), dtype=ml_dtypes.bfloat16)
        xhc = np.ascontiguousarray(
            np.concatenate([halo, np.asarray(x[b, lo:lo + TC], dtype=np.float32)], 0))
        in_maps.append(dict(
            xh=xhc, wqkvT=wqkvT, woutT=woutT, wffnT=wffnT, bias1=bias1,
            identN=ident, ones_in=ones, triP_in=triP, triO_in=triO, m0row=m0))
    return in_maps


def kernel(L0_attn_x, L0_attn_w_qkv, L0_attn_w_out, L0_ffn_weight, L0_ffn_bias):
    x = np.asarray(L0_attn_x)
    in_maps = make_in_maps(x, L0_attn_w_qkv, L0_attn_w_out, L0_ffn_weight, L0_ffn_bias)
    nc = build()
    res = run_bass_kernel_spmd(nc, in_maps, core_ids=list(range(8)))
    outa = np.empty((2, 8192, 2048), dtype=np.float32)
    for core in range(8):
        b, j = core // 4, core % 4
        outa[b, 2048 * j:2048 * (j + 1)] = res.results[core]["out"]
    return outa
